# version 03
# baseline (speedup 1.0000x reference)
import os
import numpy as np
import ml_dtypes
from contextlib import ExitStack

import concourse.bass as bass
import concourse.mybir as mybir
from concourse.bass_utils import run_bass_kernel_spmd

F32 = mybir.dt.float32
BF16 = mybir.dt.bfloat16
AF = mybir.ActivationFunctionType
OP = mybir.AluOpType

import concourse.dve_ops as dve_ops
from concourse.dve_op import DveOp
from concourse.dve_spec import Spec, Src0, Src1, C0, C1, One, lower as dve_lower


def _register_op(name, spec_body, reference):
    ver = "v3"
    spec = Spec(body=spec_body)
    compiled = dve_lower(spec, ver=ver, perf_en=False, name=name)
    op = DveOp(name, spec, subdim=False, uops_sha={ver: compiled.sha(ver)}, reference=reference)
    if name not in dve_ops._SUB_OPCODE_FOR_NAME:
        dve_ops.OPS.append(op)
        row = dve_ops._CUSTOM_DVE_ROW_BASE + len(dve_ops.OPS) - 1
        assert row < 0x20
        dve_ops._SUB_OPCODE_FOR_NAME[name] = row
        dve_ops.CUSTOM_DVE_SPECS[name] = spec
        dve_ops.CUSTOM_DVE_REFS[name] = reference
    return next(o for o in dve_ops.OPS if o.name == name)


_m1 = Src0 * Src1
QLAD1 = _register_op(
    "AQL1",
    (((_m1 > One) + (_m1 > (C1 - C0))) + (_m1 > C0)) + (_m1 > (C1 - One)),
    lambda in0, in1, s0, s1, imm2: (
        ((in0 * in1 > 1.0) * 1.0 + (in0 * in1 > (s1 - s0)) * 1.0)
        + ((in0 * in1 > s0) * 1.0 + (in0 * in1 > (s1 - 1.0)) * 1.0)
    ),
)
_m2 = Src0 * Src1
_q7 = _m2 > (C0 + C0)
_t2 = ((_m2 > C0) + (_m2 > C1)) + (_q7 + _q7)
QLAD2 = _register_op(
    "AQL2",
    _t2 + _t2,
    lambda in0, in1, s0, s1, imm2: 2.0 * (
        ((in0 * in1 > s0) * 1.0 + (in0 * in1 > s1) * 1.0)
        + 2.0 * (in0 * in1 > (s0 + s0)) * 1.0
    ),
)

C = 4096
D = 1024
I = 2816
NCC = C // 128
DK = D // 128
IK = I // 128
TILE = 512
NTILES = C // TILE
CSUB = TILE // 128
STRIPS = [(0, 512), (512, 512), (1024, 512), (1536, 512), (2048, 512), (2560, 256)]
NST = len(STRIPS)
EPS_ATTN = 1e-6
EPS_LN = 1e-5


class VFence:
    def __init__(self):
        self.sem = None
        self.n = 0

    def bind(self, sem):
        self.sem = sem

    def __call__(self, eng, inst):
        inst.then_inc(self.sem, 1)
        self.n += 1
        eng.wait_ge(self.sem, self.n)
        return inst


VF = VFence()


def emit_quant(vector, src_abs, w, sc, rs4, rs2, sh, o1, o2):
    nb = w // 16
    nbt = sc.shape[1]
    a3 = src_abs[:, 0:w].rearrange("p (b k) -> p b k", k=16)
    VF(vector, vector.tensor_reduce(out=sc[:, 0:nb], in_=a3, axis=mybir.AxisListType.X,
                                    op=OP.max, apply_absolute_value=False))
    VF(vector, vector.tensor_scalar(out=sc[:, 0:nb], in0=sc[:, 0:nb],
                                    scalar1=1.0 / 6.0, scalar2=1e-12, op0=OP.mult, op1=OP.max))
    VF(vector, vector.reciprocal(rs4[:, 0:nb], sc[:, 0:nb]))
    vector.tensor_scalar(out=rs2[:, 0:nb], in0=rs4[:, 0:nb], scalar1=2.0, scalar2=None, op0=OP.mult)
    vector.tensor_scalar(out=sh[:, 0:nb], in0=sc[:, 0:nb], scalar1=0.5, scalar2=None, op0=OP.mult)
    VF(vector, vector.tensor_scalar(out=rs4[:, 0:nb], in0=rs4[:, 0:nb], scalar1=4.0, scalar2=None, op0=OP.mult))
    rs4b = bass.AP(rs4, 0, [[nbt, 128], [1, nb], [0, 16]])
    rs2b = bass.AP(rs2, 0, [[nbt, 128], [1, nb], [0, 16]])
    vector._custom_dve(QLAD1, out=o1[:, 0:w], in0=src_abs[:, 0:w], in1=rs4b, s0=5.0, s1=8.0)
    VF(vector, vector._custom_dve(QLAD2, out=o2[:, 0:w], in0=src_abs[:, 0:w], in1=rs2b, s0=5.0, s1=7.0))
    return vector.tensor_tensor(out=o1[:, 0:w], in0=o1[:, 0:w], in1=o2[:, 0:w], op=OP.add)


def emit_bn_combine(vector, bnst, bnt4, bnmu, bnva):
    mu_v = bass.AP(bnst, 1, [[12, 128], [3, 4]])
    q_v = bass.AP(bnst, 2, [[12, 128], [3, 4]])
    VF(vector, vector.tensor_reduce(out=bnmu[:], in_=mu_v, axis=mybir.AxisListType.X, op=OP.add))
    VF(vector, vector.tensor_reduce(out=bnva[:], in_=q_v, axis=mybir.AxisListType.X, op=OP.add))
    VF(vector, vector.tensor_tensor(out=bnt4[:], in0=mu_v, in1=mu_v, op=OP.mult))
    VF(vector, vector.tensor_scalar(out=bnmu[:], in0=bnmu[:], scalar1=0.25, scalar2=None, op0=OP.mult))
    VF(vector, vector.tensor_scalar(out=bnva[:], in0=bnva[:], scalar1=1.0 / 1024.0, scalar2=None, op0=OP.mult))
    VF(vector, vector.tensor_reduce(out=bnt4[:, 0:1], in_=bnt4[:], axis=mybir.AxisListType.X, op=OP.add))
    VF(vector, vector.tensor_scalar(out=bnt4[:, 0:1], in0=bnt4[:, 0:1], scalar1=0.25, scalar2=None, op0=OP.mult))
    VF(vector, vector.tensor_tensor(out=bnva[:], in0=bnva[:], in1=bnt4[:, 0:1], op=OP.add))
    VF(vector, vector.tensor_tensor(out=bnt4[:, 1:2], in0=bnmu[:], in1=bnmu[:], op=OP.mult))
    return vector.tensor_tensor(out=bnva[:], in0=bnva[:], in1=bnt4[:, 1:2], op=OP.subtract)


def build_core_program(stage="ALL", debug=()):
    nc = bass.Bass("TRN2", target_bir_lowering=False, detect_race_conditions=False)

    x_tok = nc.dram_tensor("x_tok", [C, D], F32, kind="ExternalInput")
    x_feat = nc.dram_tensor("x_feat", [D, C], F32, kind="ExternalInput")
    wqT_d = nc.dram_tensor("wqT_hbm", [D, D], F32, kind="ExternalInput")
    wkT_d = nc.dram_tensor("wkT_hbm", [D, D], F32, kind="ExternalInput")
    wvT_d = nc.dram_tensor("wvT_hbm", [D, D], F32, kind="ExternalInput")
    bq_d = nc.dram_tensor("bq", [1, D], F32, kind="ExternalInput")
    bk_d = nc.dram_tensor("bk", [1, D], F32, kind="ExternalInput")
    bvc_d = nc.dram_tensor("bvc_hbm", [128, DK], F32, kind="ExternalInput")
    gamma_d = nc.dram_tensor("gamma", [1, 1], F32, kind="ExternalInput")
    wg_d = nc.dram_tensor("wg", [I, D], F32, kind="ExternalInput")
    wu_d = nc.dram_tensor("wu", [I, D], F32, kind="ExternalInput")
    wd_d = nc.dram_tensor("wd", [D, I], F32, kind="ExternalInput")
    idb_d = nc.dram_tensor("idb_hbm", [128, 128], BF16, kind="ExternalInput")
    idf_d = nc.dram_tensor("idf_hbm", [1, 1], F32, kind="ExternalInput")
    onec_d = nc.dram_tensor("onec_hbm", [128, 1], BF16, kind="ExternalInput")

    y_d = nc.dram_tensor("y", [C, D], F32, kind="ExternalOutput")

    wg_scr = nc.dram_tensor("wg_scr", [IK, DK, 128, 128], BF16, kind="Internal")
    row_scr = nc.dram_tensor("row_scr", [2, D], F32, kind="Internal")
    wu_scr = nc.dram_tensor("wu_scr", [IK, DK, 128, 128], BF16, kind="Internal")
    wd_scr = nc.dram_tensor("wd_scr", [IK, 128, D], BF16, kind="Internal")

    dbg = {}
    if "A" in debug:
        dbg["qt"] = nc.dram_tensor("dbg_qt", [C, D], F32, kind="ExternalOutput")
        dbg["kt"] = nc.dram_tensor("dbg_kt", [C, D], F32, kind="ExternalOutput")
        dbg["g"] = nc.dram_tensor("dbg_g", [D, D], F32, kind="ExternalOutput")
        dbg["rows"] = nc.dram_tensor("dbg_rows", [8, D], F32, kind="ExternalOutput")
    if "W" in debug:
        dbg["wgq"] = nc.dram_tensor("dbg_wgq", [IK * DK * 128, 128], F32, kind="ExternalOutput")
        dbg["wdq"] = nc.dram_tensor("dbg_wdq", [I, D], F32, kind="ExternalOutput")
    if "B" in debug:
        dbg["y1"] = nc.dram_tensor("dbg_y1", [C, D], F32, kind="ExternalOutput")
        dbg["xqt"] = nc.dram_tensor("dbg_xqt", [NTILES * 128, DK * TILE], F32, kind="ExternalOutput")
        dbg["hqt"] = nc.dram_tensor("dbg_hqt", [NTILES * 128, IK * TILE], F32, kind="ExternalOutput")
        dbg["zf"] = nc.dram_tensor("dbg_zf", [C, D], F32, kind="ExternalOutput")

    es = ExitStack()
    sb = lambda name, shape, dt: es.enter_context(nc.sbuf_tensor(name, shape, dt))
    sem = lambda name: es.enter_context(nc.semaphore(name))

    idb = sb("idb", [128, 128], BF16)
    idf = sb("idf", [1, 1], F32)
    onec = sb("onec", [128, 1], BF16)
    gam = sb("gam", [1, 1], F32)
    bvc = sb("bvc", [128, DK], F32)
    G = sb("G", [128, DK, D], BF16)

    s_init = sem("s_init")
    VF.bind(sem("ve_fence"))
    VF.n = 0
    with nc.Block() as block:
        @block.sync
        def _(sync):
            for dst, src in ((idb, idb_d), (idf, idf_d), (onec, onec_d),
                             (gam, gamma_d), (bvc, bvc_d)):
                sync.dma_start(out=dst[:], in_=src.ap()).then_inc(s_init, 16)

    esA = ExitStack()
    sbA = lambda name, shape, dt: esA.enter_context(nc.sbuf_tensor(name, shape, dt))

    Q_t = sbA("Q_t", [128, NCC, D], BF16)
    K_t = sbA("K_t", [128, NCC, D], BF16)
    esA1 = ExitStack()
    sbA1 = lambda name, shape, dt: esA1.enter_context(nc.sbuf_tensor(name, shape, dt))
    wqT = sbA1("wqT", [128, DK, D], BF16)
    wkT = sbA1("wkT", [128, DK, D], BF16)
    bq_bc = sbA1("bq_bc", [128, D], F32)
    bk_bc = sbA1("bk_bc", [128, D], F32)
    xTc = [sbA1(f"xTc{i}", [128, DK, 128], BF16) for i in range(2)]

    s_aw = sem("s_aw")
    s_awg = sem("s_awg")
    s_ax = [sem("s_ax0"), sem("s_ax1")]
    s_aq = sem("s_aq")
    s_ae = sem("s_ae")

    with nc.Block() as block, ExitStack() as esP:
        qk_ps = [esP.enter_context(nc.psum_tensor(f"qkps{i}", [128, 512], F32)) for i in range(8)]

        @block.gpsimd
        def _(gpsimd):
            gpsimd.wait_ge(s_init, 80)
            gpsimd.dma_start(out=wqT[:], in_=wqT_d.ap()).then_inc(s_awg, 16)
            gpsimd.dma_start(out=wkT[:], in_=wkT_d.ap()).then_inc(s_awg, 16)
            for cc in range(NCC):
                if cc >= 2:
                    gpsimd.wait_ge(s_aq, 4 * (cc - 1))
                src = bass.AP(x_feat, cc * 128, [[C, 128], [128 * C, DK], [1, 128]])
                gpsimd.dma_start(out=xTc[cc % 2][:], in_=src).then_inc(s_ax[cc % 2], 16)

        @block.sync
        def _(sync):
            sync.dma_start(out=bq_bc[:], in_=bass.AP(bq_d, 0, [[0, 128], [1, D]])).then_inc(s_aw, 16)
            sync.dma_start(out=bk_bc[:], in_=bass.AP(bk_d, 0, [[0, 128], [1, D]])).then_inc(s_aw, 16)

        @block.tensor
        def _(tensor):
            tensor.wait_ge(s_aw, 32)
            tensor.wait_ge(s_awg, 32)
            for cc in range(NCC):
                tensor.wait_ge(s_ax[cc % 2], 16 * (cc // 2 + 1))
                if cc >= 2:
                    tensor.wait_ge(s_ae, cc - 1)
                bs = 4 * (cc % 2)
                xc = xTc[cc % 2]
                for dk in range(DK):
                    lhs = xc[:, dk, :]
                    for j, (wT, nh) in enumerate(((wqT, 0), (wqT, 1), (wkT, 0), (wkT, 1))):
                        mm = tensor.matmul(qk_ps[bs + j][:], lhs,
                                           wT[:, dk, nh * 512:(nh + 1) * 512],
                                           start=(dk == 0), stop=(dk == DK - 1))
                        if dk == DK - 1:
                            mm.then_inc(s_aq, 1)

        @block.vector
        def _(vector):
            for cc in range(NCC):
                vector.wait_ge(s_aq, 4 * (cc + 1))
                bs = 4 * (cc % 2)
                for j, (dst, bias, nh) in enumerate(((Q_t, bq_bc, 0), (Q_t, bq_bc, 1),
                                                     (K_t, bk_bc, 0), (K_t, bk_bc, 1))):
                    ins = vector.tensor_tensor(
                        out=dst[:, cc, nh * 512:(nh + 1) * 512],
                        in0=qk_ps[bs + j][:], in1=bias[:, nh * 512:(nh + 1) * 512], op=OP.add)
                    if j == 3:
                        ins.then_inc(s_ae, 1)

    esA1.close()
    sqb = [sbA(f"sqb{i}", [128, D], BF16) for i in range(4)]
    gt_bc = sbA("gt_bc", [128, D], F32)
    gq_bc = sbA("gq_bc", [128, D], F32)
    qsq_r = sbA("qsq_r", [1, D], F32)
    qcs_r = sbA("qcs_r", [1, D], F32)
    ksq_r = sbA("ksq_r", [1, D], F32)
    p_r = sbA("p_r", [1, D], F32)
    qn_r = sbA("qn_r", [1, D], F32)
    kn_r = sbA("kn_r", [1, D], F32)
    rqn_r = sbA("rqn_r", [1, D], F32)
    t_r = sbA("t_r", [1, D], F32)
    tmp_r = sbA("tmp_r", [1, D], F32)
    gt_r = sbA("gt_r", [1, D], F32)
    rkn_c = sbA("rkn_c", [128, DK], F32)
    s_sq = sem("s_sq")
    s_st = sem("s_st")
    with nc.Block() as block, ExitStack() as esP:
        st_ps = [esP.enter_context(nc.psum_tensor(f"stps{i}", [1, 512], F32)) for i in range(6)]

        @block.scalar
        def _(scalar):
            for cc in range(NCC):
                if cc >= 2:
                    scalar.wait_ge(s_st, cc - 1)
                scalar.activation(sqb[2 * (cc % 2)][:], Q_t[:, cc, :], AF.Square)
                scalar.activation(sqb[2 * (cc % 2) + 1][:], K_t[:, cc, :], AF.Square).then_inc(s_sq, 1)

        @block.tensor
        def _(tensor):
            for cc in range(NCC):
                tensor.wait_ge(s_sq, cc + 1)
                qs, ks = sqb[2 * (cc % 2)], sqb[2 * (cc % 2) + 1]
                st, sp = (cc == 0), (cc == NCC - 1)
                last = None
                for nh in range(2):
                    sl = slice(nh * 512, (nh + 1) * 512)
                    tensor.matmul(st_ps[nh][:], onec[:], qs[:, sl], start=st, stop=sp)
                    tensor.matmul(st_ps[2 + nh][:], onec[:], Q_t[:, cc, sl], start=st, stop=sp)
                    last = tensor.matmul(st_ps[4 + nh][:], onec[:], ks[:, sl], start=st, stop=sp)
                last.then_inc(s_st, 1)

        @block.vector
        def _(vector):
            vector.wait_ge(s_st, NCC)
            for nh in range(2):
                sl = slice(nh * 512, (nh + 1) * 512)
                vector.tensor_copy(qsq_r[:, sl], st_ps[nh][:])
                vector.tensor_copy(qcs_r[:, sl], st_ps[2 + nh][:])
                vector.tensor_copy(ksq_r[:, sl], st_ps[4 + nh][:])

    s_r = sem("s_r")
    s_rc = sem("s_rc")
    with nc.Block() as block, ExitStack() as esP:
        kc_ps = esP.enter_context(nc.psum_tensor("kcps", [128, DK], F32))

        @block.scalar
        def _(scalar):
            scalar.activation(tmp_r[:], qsq_r[:], AF.Ln).then_inc(s_rc, 1)
            scalar.wait_ge(s_rc, 1)
            scalar.activation(qn_r[:], tmp_r[:], AF.Exp, scale=0.5).then_inc(s_rc, 1)
            scalar.wait_ge(s_rc, 2)
            scalar.activation(tmp_r[:], ksq_r[:], AF.Ln).then_inc(s_rc, 1)
            scalar.wait_ge(s_rc, 3)
            scalar.activation(kn_r[:], tmp_r[:], AF.Exp, scale=0.5).then_inc(s_r, 1)

        @block.vector
        def _(vector):
            vector.wait_ge(s_r, 1)
            VF(vector, vector.reciprocal(rqn_r[:], qn_r[:]))
            VF(vector, vector.reciprocal(tmp_r[:], kn_r[:]))
            vector.engine_nop().then_inc(s_r, 1)

        @block.tensor
        def _(tensor):
            tensor.wait_ge(s_r, 2)
            last = None
            for k in range(DK):
                last = tensor.transpose(kc_ps[:, k:k + 1], tmp_r[:, k * 128:(k + 1) * 128], idf[:])
            last.then_inc(s_r, 1)

        @block.vector
        def _(vector):
            vector.wait_ge(s_r, 3)
            vector.tensor_copy(rkn_c[:], kc_ps[:])

    s_g = sem("s_g")
    s_ge = sem("s_ge")
    s_p = sem("s_p")
    with nc.Block() as block, ExitStack() as esP:
        g_ps = [esP.enter_context(nc.psum_tensor(f"gps{i}", [128, 512], F32)) for i in range(8)]

        @block.tensor
        def _(tensor):
            for mh in range(2):
                if mh == 1:
                    tensor.wait_ge(s_ge, 8)
                for cc in range(NCC):
                    for mk in range(4):
                        m0 = (mh * 4 + mk) * 128
                        lhs = K_t[:, cc, m0:m0 + 128]
                        for nh in range(2):
                            mm = tensor.matmul(g_ps[2 * mk + nh][:], lhs,
                                               Q_t[:, cc, nh * 512:(nh + 1) * 512],
                                               start=(cc == 0), stop=(cc == NCC - 1))
                            if cc == NCC - 1:
                                mm.then_inc(s_g, 1)
        @block.vector
        def _(vector):
            for mh in range(2):
                vector.wait_ge(s_g, 8 * (mh + 1))
                for mk in range(4):
                    gmk = mh * 4 + mk
                    for nh in range(2):
                        vector.tensor_scalar(
                            out=G[:, gmk, nh * 512:(nh + 1) * 512],
                            in0=g_ps[2 * mk + nh][:],
                            scalar1=rkn_c[:, gmk:gmk + 1], scalar2=None,
                            op0=OP.mult).then_inc(s_ge, 1)

    with nc.Block() as block, ExitStack() as esP:
        p_ps = [esP.enter_context(nc.psum_tensor(f"pps{i}", [1, 512], F32)) for i in range(2)]

        @block.tensor
        def _(tensor):
            last = None
            for gmk in range(DK):
                for nh in range(2):
                    last = tensor.matmul(p_ps[nh][:], onec[:], G[:, gmk, nh * 512:(nh + 1) * 512],
                                         start=(gmk == 0), stop=(gmk == DK - 1))
            last.then_inc(s_p, 1)

        @block.vector
        def _(vector):
            vector.wait_ge(s_p, 1)
            for nh in range(2):
                vector.tensor_copy(p_r[:, nh * 512:(nh + 1) * 512], p_ps[nh][:])

    s_t = sem("s_t")
    with nc.Block() as block:
        @block.vector
        def _(vector):
            VF(vector, vector.tensor_scalar(out=tmp_r[:], in0=qcs_r[:], scalar1=EPS_ATTN, scalar2=None, op0=OP.mult))
            VF(vector, vector.tensor_tensor(out=tmp_r[:], in0=tmp_r[:], in1=p_r[:], op=OP.add))
            VF(vector, vector.tensor_tensor(out=tmp_r[:], in0=tmp_r[:], in1=rqn_r[:], op=OP.mult))
            VF(vector, vector.tensor_scalar(out=tmp_r[:], in0=tmp_r[:], scalar1=float(D), scalar2=None, op0=OP.add))
            VF(vector, vector.reciprocal(t_r[:], tmp_r[:]))
            VF(vector, vector.tensor_tensor(out=tmp_r[:], in0=t_r[:], in1=rqn_r[:], op=OP.mult))
            VF(vector, vector.tensor_scalar(out=tmp_r[:], in0=tmp_r[:], scalar1=gam[0:1, 0:1], scalar2=None, op0=OP.mult))
            vector.tensor_scalar(out=gt_r[:], in0=t_r[:], scalar1=gam[0:1, 0:1],
                                 scalar2=None, op0=OP.mult).then_inc(s_t, 1)

        @block.sync
        def _(sync):
            sync.wait_ge(s_t, 1)
            sync.dma_start(out=bass.AP(row_scr, 0, [[D, 1], [1, D]]), in_=tmp_r[:]).then_inc(s_t, 16)
            sync.dma_start(out=bass.AP(row_scr, D, [[D, 1], [1, D]]), in_=gt_r[:]).then_inc(s_t, 16)
            sync.wait_ge(s_t, 33)
            sync.dma_start(out=gq_bc[:], in_=bass.AP(row_scr, 0, [[0, 128], [1, D]])).then_inc(s_t, 16)
            sync.dma_start(out=gt_bc[:], in_=bass.AP(row_scr, D, [[0, 128], [1, D]])).then_inc(s_t, 16)

        @block.vector
        def _(vector):
            vector.wait_ge(s_t, 65)
            for mk in range(DK):
                vector.tensor_tensor(out=G[:, mk, :], in0=G[:, mk, :], in1=gq_bc[:], op=OP.mult)
                vector.tensor_tensor(out=G[:, mk, :], in0=G[:, mk, :], in1=gt_bc[:], op=OP.add)

    if "A" in debug:
        s_dA = sem("s_dA")
        s_dA2 = sem("s_dA2")
        with nc.Block() as block:
            @block.gpsimd
            def _(gpsimd):
                n = 0
                for cc in range(NCC):
                    gpsimd.dma_start(out=bass.AP(dbg["qt"], cc * 128 * D, [[D, 128], [1, D]]),
                                     in_=Q_t[:, cc, :]).then_inc(s_dA, 16)
                    gpsimd.dma_start(out=bass.AP(dbg["kt"], cc * 128 * D, [[D, 128], [1, D]]),
                                     in_=K_t[:, cc, :]).then_inc(s_dA, 16)
                    n += 2
                for mk in range(DK):
                    gpsimd.dma_start(out=bass.AP(dbg["g"], mk * 128 * D, [[D, 128], [1, D]]),
                                     in_=G[:, mk, :]).then_inc(s_dA, 16)
                    n += 1
                gpsimd.wait_ge(s_dA, 16 * n)

            @block.sync
            def _(sync):
                for i, r in enumerate((qsq_r, qcs_r, ksq_r, p_r, qn_r, kn_r, t_r, tmp_r)):
                    sync.dma_start(out=bass.AP(dbg["rows"], i * D, [[D, 1], [1, D]]),
                                   in_=r[:]).then_inc(s_dA2, 16)
                sync.wait_ge(s_dA2, 16 * 8)

    esA.close()
    if stage == "A":
        es.close()
        return nc, dbg

    esW = ExitStack()
    sbW = lambda name, shape, dt: esW.enter_context(nc.sbuf_tensor(name, shape, dt))

    wbuf = [sbW(f"wbuf{i}", [128, I], F32) for i in range(2)]
    wabs = sbW("wabs", [128, I], F32)
    wsgb = sbW("wsgb", [128, I], BF16)
    wsc = sbW("wsc", [128, I // 16], F32)
    wrs4 = sbW("wrs4", [128, I // 16], F32)
    wrs2 = sbW("wrs2", [128, I // 16], F32)
    wsh = sbW("wsh", [128, I // 16], F32)
    wo1 = sbW("wo1", [128, I], BF16)
    wo2 = sbW("wo2", [128, I], BF16)
    wqv = [sbW(f"wqv{i}", [128, I], BF16) for i in range(2)]
    wtr = [sbW(f"wtr{i}", [128, IK * 128], BF16) for i in range(2)]

    s_wl = [sem("s_wl0"), sem("s_wl1")]
    s_wa = sem("s_wa")
    s_wv = sem("s_wv")
    s_wg2 = sem("s_wg2")
    s_wq = sem("s_wq")
    s_wt = sem("s_wt")
    s_we = sem("s_we")
    s_ws = [sem("s_ws0"), sem("s_ws1")]

    jobs = [("wg", i) for i in range(IK)] + [("wu", i) for i in range(IK)] + [("wd", i) for i in range(DK)]
    NJ = len(jobs)
    jw = lambda j: D if j[0] in ("wg", "wu") else I

    def stores_through(jn):
        tot = 0
        for j in jobs[:jn]:
            tot += DK if j[0] in ("wg", "wu") else 0
        return tot

    with nc.Block() as block, ExitStack() as esP:
        wt_ps = [[esP.enter_context(nc.psum_tensor(f"wtps{i}_{j}", [128, 1024], BF16)) for j in range(3)]
                 for i in range(2)]

        def emit_wstore(sync, pj):
            kind, idx = jobs[pj]
            sync.wait_ge(s_we, pj + 1)
            if kind in ("wg", "wu"):
                scr = wg_scr if kind == "wg" else wu_scr
                for hk in range(DK):
                    dst = bass.AP(scr, (idx * DK + hk) * 128 * 128, [[128, 128], [1, 128]])
                    sync.dma_start(out=dst, in_=wtr[pj % 2][:, hk * 128:(hk + 1) * 128]).then_inc(s_ws[pj % 2], 16)
                return DK
            for ik in range(IK):
                dst = bass.AP(wd_scr, ik * 128 * D + idx * 128, [[D, 128], [1, 128]])
                sync.dma_start(out=dst, in_=wtr[pj % 2][:, ik * 128:(ik + 1) * 128]).then_inc(s_ws[pj % 2], 16)
            return IK

        @block.sync
        def _(sync):
            tot = 0
            for jn, job in enumerate(jobs):
                kind, idx = job
                W = jw(job)
                if jn >= 2:
                    sync.wait_ge(s_wa, 2 * (jn - 1))
                src = {"wg": wg_d, "wu": wu_d, "wd": wd_d}[kind]
                sync.dma_start(out=wbuf[jn % 2][:, 0:W],
                               in_=bass.AP(src, idx * 128 * W, [[W, 128], [1, W]])).then_inc(s_wl[jn % 2], 16)
                if jn >= 1:
                    tot += emit_wstore(sync, jn - 1)
            tot += emit_wstore(sync, NJ - 1)
            tot0 = sum((DK if jobs[j][0] in ("wg", "wu") else IK) for j in range(0, NJ, 2))
            tot1 = sum((DK if jobs[j][0] in ("wg", "wu") else IK) for j in range(1, NJ, 2))
            sync.wait_ge(s_ws[0], 16 * tot0)
            sync.wait_ge(s_ws[1], 16 * tot1)
            sync.wait_ge(s_we, NJ)

        @block.scalar
        def _(scalar):
            for jn, job in enumerate(jobs):
                W = jw(job)
                scalar.wait_ge(s_wl[jn % 2], 16 * (jn // 2 + 1))
                if jn >= 1:
                    scalar.wait_ge(s_wq, jn)
                scalar.activation(wabs[:, 0:W], wbuf[jn % 2][:, 0:W], AF.Abs)
                scalar.activation(wsgb[:, 0:W], wbuf[jn % 2][:, 0:W], AF.Sign).then_inc(s_wa, 2)

        def n_stores_par(pj):
            tot = 0
            for j in range(pj % 2, pj - 1, 2):
                tot += DK if jobs[j][0] in ("wg", "wu") else IK
            return tot

        def emit_wevac(vector, pj):
            W = jw(jobs[pj])
            vector.wait_ge(s_wt, pj + 1)
            if pj >= 2:
                vector.wait_ge(s_ws[pj % 2], 16 * n_stores_par(pj))
            nblk = W // 128
            last = None
            for p in range((nblk + 7) // 8):
                w_p = min(1024, (nblk - 8 * p) * 128)
                last = vector.tensor_copy(wtr[pj % 2][:, p * 1024:p * 1024 + w_p],
                                          wt_ps[pj % 2][p][:, 0:w_p])
            last.then_inc(s_we, 1)

        @block.vector
        def _(vector):
            for jn, job in enumerate(jobs):
                W = jw(job)
                vector.wait_ge(s_wa, 2 * (jn + 1))
                emit_quant(vector, wabs, W, wsc, wrs4, wrs2, wsh, wo1, wo2).then_inc(s_wv, 1)
                vector.wait_ge(s_wg2, jn + 1)
                nb = W // 16
                shb = bass.AP(wsh, 0, [[I // 16, 128], [1, nb], [0, 16]])
                if jn >= 2:
                    vector.wait_ge(s_wt, jn - 1)
                vector.tensor_tensor(out=wqv[jn % 2][:, 0:W], in0=wo1[:, 0:W],
                                     in1=shb, op=OP.mult).then_inc(s_wq, 1)
                if jn >= 1:
                    emit_wevac(vector, jn - 1)
            emit_wevac(vector, NJ - 1)

        @block.gpsimd
        def _(gpsimd):
            for jn, job in enumerate(jobs):
                W = jw(job)
                gpsimd.wait_ge(s_wv, jn + 1)
                gpsimd.tensor_tensor(out=wo1[:, 0:W], in0=wo1[:, 0:W],
                                     in1=wsgb[:, 0:W], op=OP.mult).then_inc(s_wg2, 1)

        @block.tensor
        def _(tensor):
            for jn, job in enumerate(jobs):
                W = jw(job)
                tensor.wait_ge(s_wq, jn + 1)
                if jn >= 2:
                    tensor.wait_ge(s_we, jn - 1)
                nblk = W // 128
                last = None
                for b in range(nblk):
                    last = tensor.transpose(wt_ps[jn % 2][b // 8][:, (b % 8) * 128:(b % 8 + 1) * 128],
                                            wqv[jn % 2][:, b * 128:(b + 1) * 128], idb[:])
                last.then_inc(s_wt, 1)

    if "W" in debug:
        s_dW = sem("s_dW")
        with nc.Block() as block:
            @block.gpsimd
            def _(gpsimd):
                n = 0
                for ik in range(IK):
                    for hk in range(DK):
                        gpsimd.dma_start(
                            out=wbuf[0][:, 0:128],
                            in_=bass.AP(wg_scr, (ik * DK + hk) * 128 * 128, [[128, 128], [1, 128]])
                        ).then_inc(s_dW, 16)
                        n += 1
                        gpsimd.wait_ge(s_dW, 32 * n - 16)
                        gpsimd.dma_start(
                            out=bass.AP(dbg["wgq"], (ik * DK + hk) * 128 * 128, [[128, 128], [1, 128]]),
                            in_=wbuf[0][:, 0:128]).then_inc(s_dW, 16)
                        gpsimd.wait_ge(s_dW, 32 * n)
                for ik in range(IK):
                    gpsimd.dma_start(out=wbuf[0][:, 0:D],
                                     in_=bass.AP(wd_scr, ik * 128 * D, [[D, 128], [1, D]])).then_inc(s_dW, 16)
                    n += 1
                    gpsimd.wait_ge(s_dW, 32 * n - 16)
                    gpsimd.dma_start(out=bass.AP(dbg["wdq"], ik * 128 * D, [[D, 128], [1, D]]),
                                     in_=wbuf[0][:, 0:D]).then_inc(s_dW, 16)
                    gpsimd.wait_ge(s_dW, 32 * n)

    esW.close()
    if stage == "W":
        es.close()
        return nc, dbg

    esB = ExitStack()
    sbB = lambda name, shape, dt: esB.enter_context(nc.sbuf_tensor(name, shape, dt))

    wvT = sbB("wvT", [128, DK, D], BF16)
    xTt = [sbB(f"xTt{i}", [128, DK, TILE], BF16) for i in range(2)]
    xres = [sbB(f"xres{i}", [128, D], F32) for i in range(2)]
    Vf = sbB("Vf", [128, DK, TILE], BF16)
    y1t = sbB("y1t", [128, CSUB, D], F32)
    s1f = sbB("s1f", [128, D], F32)
    yab = sbB("yab", [128, D], F32)
    ysg = sbB("ysg", [128, D], BF16)
    yo1 = sbB("yo1", [128, D], BF16)
    yo2 = sbB("yo2", [128, D], BF16)
    yxq = sbB("yxq", [128, D], BF16)
    ysc = sbB("ysc", [128, D // 16], F32)
    yrs4 = sbB("yrs4", [128, D // 16], F32)
    yrs2 = sbB("yrs2", [128, D // 16], F32)
    ysh = sbB("ysh", [128, D // 16], F32)
    xqT = sbB("xqT", [128, DK, TILE], BF16)
    wgs = [sbB(f"wgs{i}", [128, 4 * DK * 128], BF16) for i in range(2)]
    wus = [sbB(f"wus{i}", [128, 4 * DK * 128], BF16) for i in range(2)]
    hst = sbB("hst", [128, 512], F32)
    hab = sbB("hab", [128, 512], F32)
    hsgb = sbB("hsgb", [128, 512], BF16)
    ho1 = sbB("ho1", [128, 512], BF16)
    ho2 = sbB("ho2", [128, 512], BF16)
    hqv = sbB("hqv", [128, 512], BF16)
    hsc = sbB("hsc", [128, 32], F32)
    hrs4 = sbB("hrs4", [128, 32], F32)
    hrs2 = sbB("hrs2", [128, 32], F32)
    hsh = sbB("hsh", [128, 32], F32)
    hqT = sbB("hqT", [128, IK, TILE], BF16)
    wds = [sbB(f"wds{i}", [128, D], BF16) for i in range(4)]
    sil = sbB("sil", [128, 512], F32)
    y2b = [sbB(f"y2b{i}", [128, D], F32) for i in range(2)]
    bnst = sbB("bnst", [128, 12], F32)
    bnt4 = sbB("bnt4", [128, 4], F32)
    bnmu = sbB("bnmu", [128, 1], F32)
    bnva = sbB("bnva", [128, 1], F32)
    bnr1 = sbB("bnr1", [128, 1], F32)
    bnr2 = sbB("bnr2", [128, 1], F32)
    epsc = sbB("epsc", [128, 1], F32)

    s_bw0 = sem("s_bw0")
    s_bx = [sem("s_bx0"), sem("s_bx1")]
    s_br = [sem("s_br0"), sem("s_br1")]
    s_bv = sem("s_bv")
    s_bve = sem("s_bve")
    s_ba = sem("s_ba")
    s_be = sem("s_be")
    s_bse = sem("s_bse")
    s_bcm = sem("s_bcm")
    s_bgp = sem("s_bgp")
    s_bq = sem("s_bq")
    s2_wg = [sem("s2_wg0"), sem("s2_wg1")]
    s2_g = sem("s2_g")
    s2_sil = sem("s2_sil")
    s2_h = sem("s2_h")
    s2_hse = sem("s2_hse")
    s2_hcm = sem("s2_hcm")
    s2_hgp = sem("s2_hgp")
    s2_t = sem("s2_t")
    s2_te = sem("s2_te")
    s2_d = sem("s2_d")
    s2_dk = sem("s2_dk")
    s2_wd = [sem(f"s2_wd{i}") for i in range(4)]
    s2_l2 = sem("s2_l2")
    s2_lse = sem("s2_lse")
    s2_o = sem("s2_o")

    with nc.Block() as block:
        @block.gpsimd
        def _(gpsimd):
            gpsimd.dma_start(out=wvT[:], in_=wvT_d.ap()).then_inc(s_bw0, 16)
            src = bass.AP(x_feat, 0, [[C, 128], [128 * C, DK], [1, TILE]])
            gpsimd.dma_start(out=xTt[0][:], in_=src).then_inc(s_bx[0], 16)

        @block.vector
        def _(vector):
            vector.memset(epsc[:], EPS_LN)

        @block.sync
        def _(sync):
            sync.wait_ge(s_bw0, 16)
            sync.wait_ge(s_bx[0], 16)

    for t in range(NTILES):
        b_bv, b_bve, b_ba = 8 * t, 8 * t, CSUB * t
        b_be, b_bse, b_bcm, b_bgp, b_bq = 12 * t, 12 * t, 4 * t, 4 * t, 8 * t
        with nc.Block() as block, ExitStack() as esP:
            pp = lambda name, shape, dt: esP.enter_context(nc.psum_tensor(name, shape, dt))
            vps = [pp(f"vps{t}_{i}", [128, TILE], F32) for i in range(2)]
            aps = [pp(f"aps{t}_{i}", [128, D], F32) for i in range(2)]
            tps = [pp(f"tps{t}_{i}", [128, 1024], BF16) for i in range(2)]

            @block.gpsimd
            def _(gpsimd):
                if t + 1 < NTILES:
                    src = bass.AP(x_feat, (t + 1) * TILE, [[C, 128], [128 * C, DK], [1, TILE]])
                    gpsimd.dma_start(out=xTt[(t + 1) % 2][:], in_=src).then_inc(s_bx[(t + 1) % 2], 16)
                for cs in range(CSUB):
                    gpsimd.wait_ge(s_bcm, b_bcm + cs + 1)
                    gpsimd.wait_ge(s_bse, b_bse + 3 * cs + 3)
                    gpsimd.tensor_tensor(out=yo1[:], in0=yo1[:], in1=ysg[:], op=OP.mult).then_inc(s_bgp, 1)

            @block.sync
            def _(sync):
                for cs in range(CSUB):
                    if cs >= 2:
                        sync.wait_ge(s_be, b_be + 3 * (cs - 2) + 1)
                    r0 = t * TILE + cs * 128
                    sync.dma_start(out=xres[cs % 2][:],
                                   in_=bass.AP(x_tok, r0 * D, [[D, 128], [1, D]])).then_inc(s_br[cs % 2], 16)
                i0, W = STRIPS[0]
                nik = W // 128
                sync.dma_start(out=wgs[0][:, 0:nik * DK * 128],
                               in_=bass.AP(wg_scr, 0, [[128, 128], [128 * 128, nik * DK], [1, 128]])
                               ).then_inc(s2_wg[0], 16)
                sync.dma_start(out=wus[0][:, 0:nik * DK * 128],
                               in_=bass.AP(wu_scr, 0, [[128, 128], [128 * 128, nik * DK], [1, 128]])
                               ).then_inc(s2_wg[0], 16)

            @block.tensor
            def _(tensor):
                tensor.wait_ge(s_bx[t % 2], 16 * (t // 2 + 1))
                for mk in range(DK):
                    if mk >= 2:
                        tensor.wait_ge(s_bve, b_bve + mk - 1)
                    for dk in range(DK):
                        mm = tensor.matmul(vps[mk % 2][:], wvT[:, dk, mk * 128:(mk + 1) * 128],
                                           xTt[t % 2][:, dk, :], start=(dk == 0), stop=(dk == DK - 1))
                        if dk == DK - 1:
                            mm.then_inc(s_bv, 1)
                tensor.wait_ge(s_bve, b_bve + 8)

                def emit_xq_transpose(j):
                    tensor.wait_ge(s_be, b_be + 3 * j + 3)
                    if j >= 2:
                        tensor.wait_ge(s_bq, b_bq + 2 * (j - 1))
                    last = None
                    for dk in range(DK):
                        last = tensor.transpose(tps[j % 2][:, dk * 128:(dk + 1) * 128],
                                                yxq[:, dk * 128:(dk + 1) * 128], idb[:])
                    last.then_inc(s_bq, 1)

                for cs in range(CSUB):
                    if cs >= 2:
                        tensor.wait_ge(s_be, b_be + 3 * (cs - 1))
                    for mk in range(DK):
                        for nh in range(2):
                            mm = tensor.matmul(aps[cs % 2][:, nh * 512:(nh + 1) * 512],
                                               Vf[:, mk, cs * 128:(cs + 1) * 128],
                                               G[:, mk, nh * 512:(nh + 1) * 512],
                                               start=(mk == 0), stop=(mk == DK - 1))
                            if mk == DK - 1 and nh == 1:
                                mm.then_inc(s_ba, 1)
                    if cs >= 1:
                        emit_xq_transpose(cs - 1)
                emit_xq_transpose(CSUB - 1)

            @block.scalar
            def _(scalar):
                for mk in range(DK):
                    scalar.wait_ge(s_bv, b_bv + mk + 1)
                    scalar.activation(Vf[:, mk, :], vps[mk % 2][:], AF.Identity,
                                      bias=bvc[:, mk:mk + 1]).then_inc(s_bve, 1)
                for cs in range(CSUB):
                    scalar.wait_ge(s_be, b_be + 3 * cs + 1)
                    scalar.activation(bnr1[:], bnva[:], AF.Ln, bias=epsc[:]).then_inc(s_bse, 1)
                    scalar.wait_ge(s_bse, b_bse + 3 * cs + 1)
                    scalar.activation(bnr2[:], bnr1[:], AF.Exp, scale=-0.5).then_inc(s_bse, 1)
                    scalar.wait_ge(s_be, b_be + 3 * cs + 2)
                    scalar.activation(yab[:], y1t[:, cs, :], AF.Abs)
                    scalar.activation(ysg[:], y1t[:, cs, :], AF.Sign).then_inc(s_bse, 1)

            @block.vector
            def _(vector):
                for cs in range(CSUB):
                    vector.wait_ge(s_ba, b_ba + cs + 1)
                    vector.wait_ge(s_br[cs % 2], 16 * (2 * t + cs // 2 + 1))
                    vector.tensor_tensor(out=s1f[:], in0=xres[cs % 2][:], in1=aps[cs % 2][:], op=OP.add)
                    vector.bn_stats(bnst[:, 0:6], s1f[:, 0:512])
                    VF(vector, vector.bn_stats(bnst[:, 6:12], s1f[:, 512:1024]))
                    emit_bn_combine(vector, bnst, bnt4, bnmu, bnva).then_inc(s_be, 1)
                    vector.wait_ge(s_bse, b_bse + 3 * cs + 2)
                    vector.tensor_scalar(out=y1t[:, cs, :], in0=s1f[:],
                                         scalar1=bnmu[0:128, 0:1], scalar2=bnr2[0:128, 0:1],
                                         op0=OP.subtract, op1=OP.mult).then_inc(s_be, 1)
                    vector.wait_ge(s_bse, b_bse + 3 * cs + 3)
                    emit_quant(vector, yab, D, ysc, yrs4, yrs2, ysh, yo1, yo2).then_inc(s_bcm, 1)
                    vector.wait_ge(s_bgp, b_bgp + cs + 1)
                    shb = bass.AP(ysh, 0, [[D // 16, 128], [1, D // 16], [0, 16]])
                    vector.tensor_tensor(out=yxq[:], in0=yo1[:], in1=shb, op=OP.mult).then_inc(s_be, 1)
                    vector.wait_ge(s_bq, b_bq + 2 * cs + 1)
                    dst = bass.AP(xqT, cs * 128, [[DK * TILE, 128], [TILE, DK], [1, 128]])
                    vector.tensor_copy(dst, tps[cs % 2][:]).then_inc(s_bq, 1)

        b2_wg = 32 * (NST * t + 1)
        b_g, b_sil, b_h = 2 * NST * CSUB * t, NST * CSUB * t, 2 * NST * CSUB * t
        b_hse, b_hcm, b_hgp = 2 * NST * CSUB * t, NST * CSUB * t, NST * CSUB * t
        b_t2, b_te = NST * CSUB * t, NST * CSUB * t
        b_d, b_l2, b_lse, b_o = CSUB * t, 2 * CSUB * t, 2 * CSUB * t, 64 * t

        with nc.Block() as block, ExitStack() as esP:
            pp = lambda name, shape, dt: esP.enter_context(nc.psum_tensor(name, shape, dt))
            gps = [pp(f"bgps{t}_{i}", [128, 512], F32) for i in range(4)]
            hps = [pp(f"bhps{t}_{i}", [128, 512], BF16) for i in range(2)]

            @block.sync
            def _(sync):
                for sn in range(1, NST):
                    i0, W = STRIPS[sn]
                    nik = W // 128
                    ik0 = i0 // 128
                    if sn >= 2:
                        sync.wait_ge(s2_g, b_g + 2 * CSUB * (sn - 1))
                    sync.dma_start(out=wgs[sn % 2][:, 0:nik * DK * 128],
                                   in_=bass.AP(wg_scr, ik0 * DK * 128 * 128,
                                               [[128, 128], [128 * 128, nik * DK], [1, 128]])
                                   ).then_inc(s2_wg[sn % 2], 16)
                    sync.dma_start(out=wus[sn % 2][:, 0:nik * DK * 128],
                                   in_=bass.AP(wu_scr, ik0 * DK * 128 * 128,
                                               [[128, 128], [128 * 128, nik * DK], [1, 128]])
                                   ).then_inc(s2_wg[sn % 2], 16)

            @block.tensor
            def _(tensor):
                for sn in range(NST):
                    i0, W = STRIPS[sn]
                    nik = W // 128
                    tensor.wait_ge(s2_wg[sn % 2], 32 * (3 * t + sn // 2 + 1))
                    for cs in range(CSUB):
                        gu = sn * CSUB + cs
                        if gu >= 2:
                            tensor.wait_ge(s2_h, b_h + 2 * (gu - 1))
                        for pi, wbufs in enumerate((wgs, wus)):
                            last = None
                            for hk in range(DK):
                                rhs = bass.AP(wbufs[sn % 2], hk * 128,
                                              [[4 * DK * 128, 128], [DK * 128, nik], [1, 128]])
                                last = tensor.matmul(gps[2 * (gu % 2) + pi][:, 0:W],
                                                     xqT[:, hk, cs * 128:(cs + 1) * 128], rhs,
                                                     start=(hk == 0), stop=(hk == DK - 1))
                            last.then_inc(s2_g, 1)
                        tensor.wait_ge(s2_h, b_h + 2 * gu + 2)
                        if gu >= 2:
                            tensor.wait_ge(s2_te, b_te + gu - 1)
                        lastt = None
                        for b_ in range(nik):
                            lastt = tensor.transpose(hps[gu % 2][:, b_ * 128:(b_ + 1) * 128],
                                                     hqv[:, b_ * 128:(b_ + 1) * 128], idb[:])
                        lastt.then_inc(s2_t, 1)

            @block.scalar
            def _(scalar):
                for sn in range(NST):
                    i0, W = STRIPS[sn]
                    for cs in range(CSUB):
                        gu = sn * CSUB + cs
                        scalar.wait_ge(s2_g, b_g + 2 * gu + 1)
                        if gu >= 1:
                            scalar.wait_ge(s2_h, b_h + 2 * gu - 1)
                        scalar.activation(sil[:, 0:W], gps[2 * (gu % 2)][:, 0:W], AF.Silu).then_inc(s2_sil, 1)
                        scalar.wait_ge(s2_h, b_h + 2 * gu + 1)
                        scalar.activation(hab[:, 0:W], hst[:, 0:W], AF.Abs)
                        scalar.activation(hsgb[:, 0:W], hst[:, 0:W], AF.Sign).then_inc(s2_hse, 2)

            @block.vector
            def _(vector):
                for sn in range(NST):
                    i0, W = STRIPS[sn]
                    for cs in range(CSUB):
                        gu = sn * CSUB + cs
                        vector.wait_ge(s2_sil, b_sil + gu + 1)
                        vector.wait_ge(s2_g, b_g + 2 * gu + 2)
                        if gu >= 1:
                            vector.wait_ge(s2_hse, b_hse + 2 * gu)
                        vector.tensor_tensor(out=hst[:, 0:W], in0=sil[:, 0:W],
                                             in1=gps[2 * (gu % 2) + 1][:, 0:W], op=OP.mult).then_inc(s2_h, 1)
                        vector.wait_ge(s2_hse, b_hse + 2 * gu + 2)
                        emit_quant(vector, hab, W, hsc, hrs4, hrs2, hsh, ho1, ho2).then_inc(s2_hcm, 1)
                        vector.wait_ge(s2_hgp, b_hgp + gu + 1)
                        nb = W // 16
                        shb = bass.AP(hsh, 0, [[32, 128], [1, nb], [0, 16]])
                        vector.tensor_tensor(out=hqv[:, 0:W], in0=ho1[:, 0:W],
                                             in1=shb, op=OP.mult).then_inc(s2_h, 1)
                        vector.wait_ge(s2_t, b_t2 + gu + 1)
                        nik = W // 128
                        ik0 = i0 // 128
                        dst = bass.AP(hqT, ik0 * TILE + cs * 128, [[IK * TILE, 128], [TILE, nik], [1, 128]])
                        vector.tensor_copy(dst, hps[gu % 2][:, 0:W]).then_inc(s2_te, 1)

            @block.gpsimd
            def _(gpsimd):
                for sn in range(NST):
                    i0, W = STRIPS[sn]
                    for cs in range(CSUB):
                        gu = sn * CSUB + cs
                        gpsimd.wait_ge(s2_hcm, b_hcm + gu + 1)
                        gpsimd.wait_ge(s2_hse, b_hse + 2 * gu + 2)
                        gpsimd.tensor_tensor(out=ho1[:, 0:W], in0=ho1[:, 0:W],
                                             in1=hsgb[:, 0:W], op=OP.mult).then_inc(s2_hgp, 1)

        wd_slot_cnt = [6, 6, 5, 5]
        with nc.Block() as block, ExitStack() as esP:
            pp = lambda name, shape, dt: esP.enter_context(nc.psum_tensor(name, shape, dt))
            dps = [pp(f"bdps{t}_{i}", [128, D], F32) for i in range(4)]

            @block.sync
            def _(sync):
                for ik in range(IK):
                    g = IK * t + ik
                    if g >= 4:
                        sync.wait_ge(s2_dk, g - 3)
                    sync.dma_start(out=wds[ik % 4][:],
                                   in_=bass.AP(wd_scr, ik * 128 * D, [[D, 128], [1, D]])
                                   ).then_inc(s2_wd[ik % 4], 16)
                for cs in range(CSUB):
                    sync.wait_ge(s2_l2, b_l2 + 2 * cs + 2)
                    r0 = t * TILE + cs * 128
                    if "zf" in dbg:
                        sync.dma_start(out=bass.AP(dbg["zf"], r0 * D, [[D, 128], [1, D]]),
                                       in_=s1f[:]).then_inc(s2_o, 16)
                    sync.dma_start(out=bass.AP(y_d, r0 * D, [[D, 128], [1, D]]),
                                   in_=y2b[cs % 2][:]).then_inc(s2_o, 16)
                if t == NTILES - 1:
                    sync.wait_ge(s2_o, 16 * 4 * NTILES)

            @block.tensor
            def _(tensor):
                for ik in range(IK):
                    slot = ik % 4
                    tensor.wait_ge(s2_wd[slot], 16 * (wd_slot_cnt[slot] * t + ik // 4 + 1))
                    last = None
                    for cs in range(CSUB):
                        for nh in range(2):
                            last = tensor.matmul(dps[cs][:, nh * 512:(nh + 1) * 512],
                                                 hqT[:, ik, cs * 128:(cs + 1) * 128],
                                                 wds[slot][:, nh * 512:(nh + 1) * 512],
                                                 start=(ik == 0), stop=(ik == IK - 1))
                    last.then_inc(s2_dk, 1)

            @block.scalar
            def _(scalar):
                for cs in range(CSUB):
                    scalar.wait_ge(s2_l2, b_l2 + 2 * cs + 1)
                    scalar.activation(bnr1[:], bnva[:], AF.Ln, bias=epsc[:]).then_inc(s2_lse, 1)
                    scalar.wait_ge(s2_lse, b_lse + 2 * cs + 1)
                    scalar.activation(bnr2[:], bnr1[:], AF.Exp, scale=-0.5).then_inc(s2_lse, 1)

            @block.vector
            def _(vector):
                for cs in range(CSUB):
                    vector.wait_ge(s2_dk, IK * (t + 1))
                    if cs >= 2:
                        vector.wait_ge(s2_o, b_o + 16 * (cs - 1))
                    for nh in range(2):
                        sl = slice(nh * 512, (nh + 1) * 512)
                        vector.tensor_tensor(out=s1f[:, sl], in0=y1t[:, cs, sl], in1=dps[cs][:, sl], op=OP.add)
                    vector.bn_stats(bnst[:, 0:6], s1f[:, 0:512])
                    VF(vector, vector.bn_stats(bnst[:, 6:12], s1f[:, 512:1024]))
                    emit_bn_combine(vector, bnst, bnt4, bnmu, bnva).then_inc(s2_l2, 1)
                    vector.wait_ge(s2_lse, b_lse + 2 * cs + 2)
                    vector.tensor_scalar(out=y2b[cs % 2][:], in0=s1f[:],
                                         scalar1=bnmu[0:128, 0:1], scalar2=bnr2[0:128, 0:1],
                                         op0=OP.subtract, op1=OP.mult).then_inc(s2_l2, 1)

        if "B" in debug:
            with nc.Block() as block:
                @block.sync
                def _(sync):
                    for cs in range(CSUB):
                        r0 = t * TILE + cs * 128
                        sync.dma_start(out=bass.AP(dbg["y1"], r0 * D, [[D, 128], [1, D]]),
                                       in_=y1t[:, cs, :]).then_inc(s2_o, 16)

                @block.gpsimd
                def _(gpsimd):
                    gpsimd.dma_start(out=bass.AP(dbg["xqt"], t * 128 * DK * TILE,
                                                 [[DK * TILE, 128], [1, DK * TILE]]),
                                     in_=xqT[:]).then_inc(s2_o, 16)
                    gpsimd.dma_start(out=bass.AP(dbg["hqt"], t * 128 * IK * TILE,
                                                 [[IK * TILE, 128], [1, IK * TILE]]),
                                     in_=hqT[:]).then_inc(s2_o, 16)

    esB.close()
    es.close()
    return nc, dbg


_PROG_CACHE = {}


def _build():
    if "nc" not in _PROG_CACHE:
        nc, _ = build_core_program(stage="ALL", debug=())
        mybir.codegen_inst_isa_subclasses(nc)
        _PROG_CACHE["nc"] = nc
    return _PROG_CACHE["nc"]


def _wT_layout(W):
    Dm = W.shape[0]
    return np.ascontiguousarray(W.T.reshape(Dm // 128, 128, Dm).transpose(1, 0, 2).reshape(Dm, Dm))


def kernel(L0_attn_x, L0_attn_Wq, L0_attn_bq, L0_attn_Wk, L0_attn_bk,
           L0_attn_Wv, L0_attn_bv, L0_attn_gamma,
           L0_ffn_w_gate, L0_ffn_w_up, L0_ffn_w_down):
    x = np.ascontiguousarray(np.asarray(L0_attn_x, dtype=np.float32))
    B = x.shape[0]
    assert x.shape == (B, C, D)
    wqT = _wT_layout(np.asarray(L0_attn_Wq, dtype=np.float32))
    wkT = _wT_layout(np.asarray(L0_attn_Wk, dtype=np.float32))
    wvT = _wT_layout(np.asarray(L0_attn_Wv, dtype=np.float32))
    bq = np.ascontiguousarray(np.asarray(L0_attn_bq, dtype=np.float32).reshape(1, D))
    bk = np.ascontiguousarray(np.asarray(L0_attn_bk, dtype=np.float32).reshape(1, D))
    bvc = np.ascontiguousarray(np.asarray(L0_attn_bv, dtype=np.float32).reshape(DK, 128).T)
    gam = np.ascontiguousarray(np.asarray(L0_attn_gamma, dtype=np.float32).reshape(1, 1))
    wg = np.ascontiguousarray(np.asarray(L0_ffn_w_gate, dtype=np.float32))
    wu = np.ascontiguousarray(np.asarray(L0_ffn_w_up, dtype=np.float32))
    wd = np.ascontiguousarray(np.asarray(L0_ffn_w_down, dtype=np.float32))
    idb = np.eye(128, dtype=ml_dtypes.bfloat16)
    idf = np.ones((1, 1), dtype=np.float32)
    onec = np.ones((128, 1), dtype=ml_dtypes.bfloat16)

    nc = _build()
    in_maps = []
    for b in range(B):
        in_maps.append({
            "x_tok": x[b],
            "x_feat": np.ascontiguousarray(x[b].T),
            "wqT_hbm": wqT, "wkT_hbm": wkT, "wvT_hbm": wvT,
            "bq": bq, "bk": bk, "bvc_hbm": bvc, "gamma": gam,
            "wg": wg, "wu": wu, "wd": wd,
            "idb_hbm": idb, "idf_hbm": idf, "onec_hbm": onec,
        })
    res = run_bass_kernel_spmd(nc, in_maps, core_ids=list(range(B)))
    out = np.stack([np.asarray(res.results[b]["y"], dtype=np.float32) for b in range(B)], axis=0)
    return out


# version 09
# speedup vs baseline: 1.0219x; 1.0219x over previous
import os
import numpy as np
import ml_dtypes
from contextlib import ExitStack

import concourse.bass as bass
import concourse.mybir as mybir
from concourse.bass_utils import run_bass_kernel_spmd

F32 = mybir.dt.float32
BF16 = mybir.dt.bfloat16
AF = mybir.ActivationFunctionType
OP = mybir.AluOpType

import concourse.dve_ops as dve_ops
from concourse.dve_op import DveOp
from concourse.dve_spec import Spec, Src0, Src1, C0, C1, One, lower as dve_lower


def _register_op(name, spec_body, reference):
    ver = "v3"
    spec = Spec(body=spec_body)
    compiled = dve_lower(spec, ver=ver, perf_en=False, name=name)
    op = DveOp(name, spec, subdim=False, uops_sha={ver: compiled.sha(ver)}, reference=reference)
    if name not in dve_ops._SUB_OPCODE_FOR_NAME:
        dve_ops.OPS.append(op)
        row = dve_ops._CUSTOM_DVE_ROW_BASE + len(dve_ops.OPS) - 1
        assert row < 0x20
        dve_ops._SUB_OPCODE_FOR_NAME[name] = row
        dve_ops.CUSTOM_DVE_SPECS[name] = spec
        dve_ops.CUSTOM_DVE_REFS[name] = reference
    return next(o for o in dve_ops.OPS if o.name == name)


_m1 = Src0 * Src1
QLAD1 = _register_op(
    "AQL1",
    (((_m1 > One) + (_m1 > (C1 - C0))) + (_m1 > C0)) + (_m1 > (C1 - One)),
    lambda in0, in1, s0, s1, imm2: (
        ((in0 * in1 > 1.0) * 1.0 + (in0 * in1 > (s1 - s0)) * 1.0)
        + ((in0 * in1 > s0) * 1.0 + (in0 * in1 > (s1 - 1.0)) * 1.0)
    ),
)
_m2 = Src0 * Src1
_q7 = _m2 > (C0 + C0)
_t2 = ((_m2 > C0) + (_m2 > C1)) + (_q7 + _q7)
QLAD2 = _register_op(
    "AQL2",
    _t2 + _t2,
    lambda in0, in1, s0, s1, imm2: 2.0 * (
        ((in0 * in1 > s0) * 1.0 + (in0 * in1 > s1) * 1.0)
        + 2.0 * (in0 * in1 > (s0 + s0)) * 1.0
    ),
)

C = 4096
D = 1024
I = 2816
NCC = C // 128
DK = D // 128
IK = I // 128
TILE = 512
NTILES = C // TILE
CSUB = TILE // 128
STRIPS = [(0, 512), (512, 512), (1024, 512), (1536, 512), (2048, 512), (2560, 256)]
NST = len(STRIPS)
EPS_ATTN = 1e-6
EPS_LN = 1e-5


class VFence:
    def __init__(self):
        self.sem = None
        self.n = 0

    def bind(self, sem):
        self.sem = sem

    def __call__(self, eng, inst):
        return inst


VF = VFence()


def emit_quant(vector, src_abs, w, sc, rs4, rs2, sh, o1, o2):
    nb = w // 16
    nbt = sc.shape[1]
    a3 = src_abs[:, 0:w].rearrange("p (b k) -> p b k", k=16)
    vector.tensor_reduce(out=sc[:, 0:nb], in_=a3, axis=mybir.AxisListType.X,
                         op=OP.max, apply_absolute_value=False)
    vector.drain()
    vector.reciprocal(rs2[:, 0:nb], sc[:, 0:nb])
    vector.drain()
    vector.tensor_scalar(out=rs4[:, 0:nb], in0=rs2[:, 0:nb],
                         scalar1=1.6666666666666666e11, scalar2=24.0, op0=OP.min, op1=OP.mult)
    vector.tensor_scalar(out=sh[:, 0:nb], in0=sc[:, 0:nb],
                         scalar1=6e-12, scalar2=1.0 / 12.0, op0=OP.max, op1=OP.mult)
    vector.tensor_scalar(out=rs2[:, 0:nb], in0=rs2[:, 0:nb],
                         scalar1=1.6666666666666666e11, scalar2=12.0, op0=OP.min, op1=OP.mult)
    rs4b = bass.AP(rs4, 0, [[nbt, 128], [1, nb], [0, 16]])
    rs2b = bass.AP(rs2, 0, [[nbt, 128], [1, nb], [0, 16]])
    vector._custom_dve(QLAD1, out=o1[:, 0:w], in0=src_abs[:, 0:w], in1=rs4b, s0=5.0, s1=8.0)
    vector._custom_dve(QLAD2, out=o2[:, 0:w], in0=src_abs[:, 0:w], in1=rs2b, s0=5.0, s1=7.0)
    return vector.tensor_tensor(out=o1[:, 0:w], in0=o1[:, 0:w], in1=o2[:, 0:w], op=OP.add)


def emit_bn(vector, src, bnsq, bnag, bns2, bred, bred2):
    a3 = src[:, 0:1024].rearrange("p (b k) -> p b k", k=16)
    q3 = bnsq[:, 0:1024].rearrange("p (b k) -> p b k", k=16)
    vector.tensor_tensor(out=bnsq[:], in0=src[:], in1=src[:], op=OP.mult)
    vector.drain()
    vector.tensor_reduce(out=bred2[:, 0:64], in_=q3, axis=mybir.AxisListType.X, op=OP.add)
    vector.drain()
    vector.tensor_reduce(out=bred[:, 0:64], in_=a3, axis=mybir.AxisListType.X, op=OP.add)
    vector.drain()
    w = 32
    while w >= 1:
        vector.tensor_tensor(out=bred[:, 0:w], in0=bred[:, 0:w], in1=bred[:, w:2 * w], op=OP.add)
        vector.tensor_tensor(out=bred2[:, 0:w], in0=bred2[:, 0:w],
                             in1=bred2[:, w:2 * w], op=OP.add)
        w //= 2
    vector.tensor_scalar(out=bnag[:, 0:1], in0=bred[:, 0:1], scalar1=1.0 / 1024.0, scalar2=None, op0=OP.mult)
    vector.tensor_scalar(out=bnag[:, 1:2], in0=bred2[:, 0:1], scalar1=1.0 / 1024.0, scalar2=None, op0=OP.mult)
    vector.tensor_tensor(out=bns2[:, 0:1], in0=bnag[:, 0:1], in1=bnag[:, 0:1], op=OP.mult)
    vector.drain()
    return vector.tensor_tensor(out=bnag[:, 1:2], in0=bnag[:, 1:2], in1=bns2[:, 0:1], op=OP.subtract)


def build_core_program(stage="ALL", debug=()):
    dbg_flags = debug
    nc = bass.Bass("TRN2", target_bir_lowering=False, detect_race_conditions=False)

    x_tok = nc.dram_tensor("x_tok", [C, D], F32, kind="ExternalInput")
    x_feat = nc.dram_tensor("x_feat", [D, C], F32, kind="ExternalInput")
    wqT_d = nc.dram_tensor("wqT_hbm", [D, D], F32, kind="ExternalInput")
    wkT_d = nc.dram_tensor("wkT_hbm", [D, D], F32, kind="ExternalInput")
    wvT_d = nc.dram_tensor("wvT_hbm", [D, D], F32, kind="ExternalInput")
    bq_d = nc.dram_tensor("bq", [1, D], F32, kind="ExternalInput")
    bk_d = nc.dram_tensor("bk", [1, D], F32, kind="ExternalInput")
    bvc_d = nc.dram_tensor("bvc_hbm", [128, DK], F32, kind="ExternalInput")
    gamma_d = nc.dram_tensor("gamma", [1, 1], F32, kind="ExternalInput")
    wg_d = nc.dram_tensor("wg", [I, D], F32, kind="ExternalInput")
    wu_d = nc.dram_tensor("wu", [I, D], F32, kind="ExternalInput")
    wd_d = nc.dram_tensor("wd", [D, I], F32, kind="ExternalInput")
    idb_d = nc.dram_tensor("idb_hbm", [128, 128], BF16, kind="ExternalInput")
    idf_d = nc.dram_tensor("idf_hbm", [1, 1], F32, kind="ExternalInput")
    onec_d = nc.dram_tensor("onec_hbm", [128, 1], BF16, kind="ExternalInput")

    y_d = nc.dram_tensor("y", [C, D], F32, kind="ExternalOutput")

    wg_scr = nc.dram_tensor("wg_scr", [IK, DK, 128, 128], BF16, kind="Internal")
    row_scr = nc.dram_tensor("row_scr", [2, D], F32, kind="Internal")
    wu_scr = nc.dram_tensor("wu_scr", [IK, DK, 128, 128], BF16, kind="Internal")
    wd_scr = nc.dram_tensor("wd_scr", [IK, 128, D], BF16, kind="Internal")

    dbg = {}
    if "A" in debug:
        dbg["qt"] = nc.dram_tensor("dbg_qt", [C, D], F32, kind="ExternalOutput")
        dbg["kt"] = nc.dram_tensor("dbg_kt", [C, D], F32, kind="ExternalOutput")
        dbg["g"] = nc.dram_tensor("dbg_g", [D, D], F32, kind="ExternalOutput")
        dbg["rows"] = nc.dram_tensor("dbg_rows", [8, D], F32, kind="ExternalOutput")
    if "W" in debug:
        dbg["wgq"] = nc.dram_tensor("dbg_wgq", [IK * DK * 128, 128], F32, kind="ExternalOutput")
        dbg["wdq"] = nc.dram_tensor("dbg_wdq", [I, D], F32, kind="ExternalOutput")
    if "C" in debug:
        dbg["bnsqd"] = nc.dram_tensor("dbg_bnsqd", [128, D], F32, kind="ExternalOutput")
        dbg["bred2d"] = nc.dram_tensor("dbg_bred2d", [128, 128], F32, kind="ExternalOutput")
        dbg["scs"] = nc.dram_tensor("dbg_scs", [128, 4], F32, kind="ExternalOutput")
        dbg["s1a"] = nc.dram_tensor("dbg_s1a", [128, D], F32, kind="ExternalOutput")
        dbg["bnagd"] = nc.dram_tensor("dbg_bnagd", [128, 2], F32, kind="ExternalOutput")
        dbg["vfd"] = nc.dram_tensor("dbg_vfd", [128, DK * TILE], F32, kind="ExternalOutput")
        dbg["xrd"] = nc.dram_tensor("dbg_xrd", [128, D], F32, kind="ExternalOutput")
    if "B" in debug:
        dbg["y1"] = nc.dram_tensor("dbg_y1", [TILE, D], F32, kind="ExternalOutput")
        dbg["xqt"] = nc.dram_tensor("dbg_xqt", [128, DK * TILE], F32, kind="ExternalOutput")
        dbg["hqt"] = nc.dram_tensor("dbg_hqt", [128, IK * TILE], F32, kind="ExternalOutput")
        dbg["zf"] = nc.dram_tensor("dbg_zf", [128, D], F32, kind="ExternalOutput")

    es = ExitStack()
    sb = lambda name, shape, dt: es.enter_context(nc.sbuf_tensor(name, shape, dt))
    sem = lambda name: es.enter_context(nc.semaphore(name))

    idb = sb("idb", [128, 128], BF16)
    idf = sb("idf", [1, 1], F32)
    onec = sb("onec", [128, 1], BF16)
    gam = sb("gam", [1, 1], F32)
    bvc = sb("bvc", [128, DK], F32)
    G = sb("G", [128, DK, D], BF16)

    s_init = sem("s_init")
    with nc.Block() as block:
        @block.sync
        def _(sync):
            for dst, src in ((idb, idb_d), (idf, idf_d), (onec, onec_d),
                             (gam, gamma_d), (bvc, bvc_d)):
                sync.dma_start(out=dst[:], in_=src.ap()).then_inc(s_init, 16)

    esA = ExitStack()
    sbA = lambda name, shape, dt: esA.enter_context(nc.sbuf_tensor(name, shape, dt))

    Q_t = sbA("Q_t", [128, NCC, D], BF16)
    K_t = sbA("K_t", [128, NCC, D], BF16)
    esA1 = ExitStack()
    sbA1 = lambda name, shape, dt: esA1.enter_context(nc.sbuf_tensor(name, shape, dt))
    wqT = sbA1("wqT", [128, DK, D], BF16)
    wkT = sbA1("wkT", [128, DK, D], BF16)
    bq_bc = sbA1("bq_bc", [128, D], F32)
    bk_bc = sbA1("bk_bc", [128, D], F32)
    xTc = [sbA1(f"xTc{i}", [128, DK, 128], BF16) for i in range(2)]

    s_aw = sem("s_aw")
    s_awg = sem("s_awg")
    s_ax = [sem("s_ax0"), sem("s_ax1")]
    s_aq = sem("s_aq")
    s_ae = sem("s_ae")

    with nc.Block() as block, ExitStack() as esP:
        qk_ps = [esP.enter_context(nc.psum_tensor(f"qkps{i}", [128, 512], F32)) for i in range(8)]

        @block.gpsimd
        def _(gpsimd):
            gpsimd.wait_ge(s_init, 80)
            gpsimd.dma_start(out=wqT[:], in_=wqT_d.ap()).then_inc(s_awg, 16)
            gpsimd.dma_start(out=wkT[:], in_=wkT_d.ap()).then_inc(s_awg, 16)
            for cc in range(NCC):
                if cc >= 2:
                    gpsimd.wait_ge(s_aq, 4 * (cc - 1))
                src = bass.AP(x_feat, cc * 128, [[C, 128], [128 * C, DK], [1, 128]])
                gpsimd.dma_start(out=xTc[cc % 2][:], in_=src).then_inc(s_ax[cc % 2], 16)

        @block.sync
        def _(sync):
            sync.dma_start(out=bq_bc[:], in_=bass.AP(bq_d, 0, [[0, 128], [1, D]])).then_inc(s_aw, 16)
            sync.dma_start(out=bk_bc[:], in_=bass.AP(bk_d, 0, [[0, 128], [1, D]])).then_inc(s_aw, 16)

        @block.tensor
        def _(tensor):
            tensor.wait_ge(s_aw, 32)
            tensor.wait_ge(s_awg, 32)
            for cc in range(NCC):
                tensor.wait_ge(s_ax[cc % 2], 16 * (cc // 2 + 1))
                if cc >= 2:
                    tensor.wait_ge(s_ae, cc - 1)
                bs = 4 * (cc % 2)
                xc = xTc[cc % 2]
                for dk in range(DK):
                    lhs = xc[:, dk, :]
                    for j, (wT, nh) in enumerate(((wqT, 0), (wqT, 1), (wkT, 0), (wkT, 1))):
                        mm = tensor.matmul(qk_ps[bs + j][:], lhs,
                                           wT[:, dk, nh * 512:(nh + 1) * 512],
                                           start=(dk == 0), stop=(dk == DK - 1))
                        if dk == DK - 1:
                            mm.then_inc(s_aq, 1)

        @block.vector
        def _(vector):
            for cc in range(NCC):
                vector.wait_ge(s_aq, 4 * (cc + 1))
                bs = 4 * (cc % 2)
                for j, (dst, bias, nh) in enumerate(((Q_t, bq_bc, 0), (Q_t, bq_bc, 1),
                                                     (K_t, bk_bc, 0), (K_t, bk_bc, 1))):
                    vector.tensor_tensor(
                        out=dst[:, cc, nh * 512:(nh + 1) * 512],
                        in0=qk_ps[bs + j][:], in1=bias[:, nh * 512:(nh + 1) * 512], op=OP.add)
                    if j == 3:
                        vector.drain().then_inc(s_ae, 1)

    esA1.close()
    sqb = [sbA(f"sqb{i}", [128, D], BF16) for i in range(4)]
    gt_bc = sbA("gt_bc", [128, D], F32)
    gq_bc = sbA("gq_bc", [128, D], F32)
    qsq_r = sbA("qsq_r", [1, D], F32)
    qcs_r = sbA("qcs_r", [1, D], F32)
    ksq_r = sbA("ksq_r", [1, D], F32)
    p_r = sbA("p_r", [1, D], F32)
    qn_r = sbA("qn_r", [1, D], F32)
    kn_r = sbA("kn_r", [1, D], F32)
    rqn_r = sbA("rqn_r", [1, D], F32)
    t_r = sbA("t_r", [1, D], F32)
    tmp_r = sbA("tmp_r", [1, D], F32)
    gt_r = sbA("gt_r", [1, D], F32)
    rkn_c = sbA("rkn_c", [128, DK], F32)
    s_sq = sem("s_sq")
    s_st = sem("s_st")
    with nc.Block() as block, ExitStack() as esP:
        st_ps = [esP.enter_context(nc.psum_tensor(f"stps{i}", [1, 512], F32)) for i in range(6)]

        @block.scalar
        def _(scalar):
            for cc in range(NCC):
                if cc >= 2:
                    scalar.wait_ge(s_st, cc - 1)
                scalar.activation(sqb[2 * (cc % 2)][:], Q_t[:, cc, :], AF.Square)
                scalar.activation(sqb[2 * (cc % 2) + 1][:], K_t[:, cc, :], AF.Square).then_inc(s_sq, 1)

        @block.tensor
        def _(tensor):
            for cc in range(NCC):
                tensor.wait_ge(s_sq, cc + 1)
                qs, ks = sqb[2 * (cc % 2)], sqb[2 * (cc % 2) + 1]
                st, sp = (cc == 0), (cc == NCC - 1)
                last = None
                for nh in range(2):
                    sl = slice(nh * 512, (nh + 1) * 512)
                    tensor.matmul(st_ps[nh][:], onec[:], qs[:, sl], start=st, stop=sp)
                    tensor.matmul(st_ps[2 + nh][:], onec[:], Q_t[:, cc, sl], start=st, stop=sp)
                    last = tensor.matmul(st_ps[4 + nh][:], onec[:], ks[:, sl], start=st, stop=sp)
                last.then_inc(s_st, 1)

        @block.vector
        def _(vector):
            vector.wait_ge(s_st, NCC)
            for nh in range(2):
                sl = slice(nh * 512, (nh + 1) * 512)
                vector.tensor_copy(qsq_r[:, sl], st_ps[nh][:])
                vector.tensor_copy(qcs_r[:, sl], st_ps[2 + nh][:])
                vector.tensor_copy(ksq_r[:, sl], st_ps[4 + nh][:])

    s_r = sem("s_r")
    s_rc = sem("s_rc")
    with nc.Block() as block, ExitStack() as esP:
        kc_ps = esP.enter_context(nc.psum_tensor("kcps", [128, DK], F32))

        @block.scalar
        def _(scalar):
            scalar.activation(tmp_r[:], qsq_r[:], AF.Ln).then_inc(s_rc, 1)
            scalar.wait_ge(s_rc, 1)
            scalar.activation(qn_r[:], tmp_r[:], AF.Exp, scale=0.5).then_inc(s_rc, 1)
            scalar.wait_ge(s_rc, 2)
            scalar.activation(tmp_r[:], ksq_r[:], AF.Ln).then_inc(s_rc, 1)
            scalar.wait_ge(s_rc, 3)
            scalar.activation(kn_r[:], tmp_r[:], AF.Exp, scale=0.5).then_inc(s_r, 1)

        @block.vector
        def _(vector):
            vector.wait_ge(s_r, 1)
            vector.reciprocal(rqn_r[:], qn_r[:])
            vector.drain()
            vector.reciprocal(tmp_r[:], kn_r[:])
            vector.drain().then_inc(s_r, 1)

        @block.tensor
        def _(tensor):
            tensor.wait_ge(s_r, 2)
            last = None
            for k in range(DK):
                last = tensor.transpose(kc_ps[:, k:k + 1], tmp_r[:, k * 128:(k + 1) * 128], idf[:])
            last.then_inc(s_r, 1)

        @block.vector
        def _(vector):
            vector.wait_ge(s_r, 3)
            vector.tensor_copy(rkn_c[:], kc_ps[:])

    s_g = sem("s_g")
    s_ge = sem("s_ge")
    s_p = sem("s_p")
    with nc.Block() as block, ExitStack() as esP:
        g_ps = [esP.enter_context(nc.psum_tensor(f"gps{i}", [128, 512], F32)) for i in range(8)]

        @block.tensor
        def _(tensor):
            for mh in range(2):
                if mh == 1:
                    tensor.wait_ge(s_ge, 8)
                for cc in range(NCC):
                    for mk in range(4):
                        m0 = (mh * 4 + mk) * 128
                        lhs = K_t[:, cc, m0:m0 + 128]
                        for nh in range(2):
                            mm = tensor.matmul(g_ps[2 * mk + nh][:], lhs,
                                               Q_t[:, cc, nh * 512:(nh + 1) * 512],
                                               start=(cc == 0), stop=(cc == NCC - 1))
                            if cc == NCC - 1:
                                mm.then_inc(s_g, 1)
        @block.vector
        def _(vector):
            for mh in range(2):
                vector.wait_ge(s_g, 8 * (mh + 1))
                for mk in range(4):
                    gmk = mh * 4 + mk
                    for nh in range(2):
                        vector.tensor_scalar(
                            out=G[:, gmk, nh * 512:(nh + 1) * 512],
                            in0=g_ps[2 * mk + nh][:],
                            scalar1=rkn_c[:, gmk:gmk + 1], scalar2=None,
                            op0=OP.mult)
                vector.drain().then_inc(s_ge, 8)

    with nc.Block() as block, ExitStack() as esP:
        p_ps = [esP.enter_context(nc.psum_tensor(f"pps{i}", [1, 512], F32)) for i in range(2)]

        @block.tensor
        def _(tensor):
            last = None
            for gmk in range(DK):
                for nh in range(2):
                    last = tensor.matmul(p_ps[nh][:], onec[:], G[:, gmk, nh * 512:(nh + 1) * 512],
                                         start=(gmk == 0), stop=(gmk == DK - 1))
            last.then_inc(s_p, 1)

        @block.vector
        def _(vector):
            vector.wait_ge(s_p, 1)
            for nh in range(2):
                vector.tensor_copy(p_r[:, nh * 512:(nh + 1) * 512], p_ps[nh][:])

    s_t = sem("s_t")
    with nc.Block() as block:
        @block.vector
        def _(vector):
            VF(vector, vector.tensor_scalar(out=tmp_r[:], in0=qcs_r[:], scalar1=EPS_ATTN, scalar2=None, op0=OP.mult))
            VF(vector, vector.tensor_tensor(out=tmp_r[:], in0=tmp_r[:], in1=p_r[:], op=OP.add))
            VF(vector, vector.tensor_tensor(out=tmp_r[:], in0=tmp_r[:], in1=rqn_r[:], op=OP.mult))
            VF(vector, vector.tensor_scalar(out=tmp_r[:], in0=tmp_r[:], scalar1=float(D), scalar2=None, op0=OP.add))
            vector.reciprocal(t_r[:], tmp_r[:])
            vector.drain()
            vector.tensor_tensor(out=tmp_r[:], in0=t_r[:], in1=rqn_r[:], op=OP.mult)
            VF(vector, vector.tensor_scalar(out=tmp_r[:], in0=tmp_r[:], scalar1=gam[0:1, 0:1], scalar2=None, op0=OP.mult))
            vector.tensor_scalar(out=gt_r[:], in0=t_r[:], scalar1=gam[0:1, 0:1],
                                 scalar2=None, op0=OP.mult)
            vector.drain().then_inc(s_t, 1)

        @block.sync
        def _(sync):
            sync.wait_ge(s_t, 1)
            sync.dma_start(out=bass.AP(row_scr, 0, [[D, 1], [1, D]]), in_=tmp_r[:]).then_inc(s_t, 16)
            sync.dma_start(out=bass.AP(row_scr, D, [[D, 1], [1, D]]), in_=gt_r[:]).then_inc(s_t, 16)
            sync.wait_ge(s_t, 33)
            sync.dma_start(out=gq_bc[:], in_=bass.AP(row_scr, 0, [[0, 128], [1, D]])).then_inc(s_t, 16)
            sync.dma_start(out=gt_bc[:], in_=bass.AP(row_scr, D, [[0, 128], [1, D]])).then_inc(s_t, 16)

        @block.vector
        def _(vector):
            vector.wait_ge(s_t, 65)
            for mk in range(DK):
                vector.tensor_tensor(out=G[:, mk, :], in0=G[:, mk, :], in1=gq_bc[:], op=OP.mult)
                vector.tensor_tensor(out=G[:, mk, :], in0=G[:, mk, :], in1=gt_bc[:], op=OP.add)

    if "A" in debug:
        s_dA = sem("s_dA")
        s_dA2 = sem("s_dA2")
        with nc.Block() as block:
            @block.gpsimd
            def _(gpsimd):
                n = 0
                for cc in range(NCC):
                    gpsimd.dma_start(out=bass.AP(dbg["qt"], cc * 128 * D, [[D, 128], [1, D]]),
                                     in_=Q_t[:, cc, :]).then_inc(s_dA, 16)
                    gpsimd.dma_start(out=bass.AP(dbg["kt"], cc * 128 * D, [[D, 128], [1, D]]),
                                     in_=K_t[:, cc, :]).then_inc(s_dA, 16)
                    n += 2
                for mk in range(DK):
                    gpsimd.dma_start(out=bass.AP(dbg["g"], mk * 128 * D, [[D, 128], [1, D]]),
                                     in_=G[:, mk, :]).then_inc(s_dA, 16)
                    n += 1
                gpsimd.wait_ge(s_dA, 16 * n)

            @block.sync
            def _(sync):
                for i, r in enumerate((qsq_r, qcs_r, ksq_r, p_r, qn_r, kn_r, t_r, tmp_r)):
                    sync.dma_start(out=bass.AP(dbg["rows"], i * D, [[D, 1], [1, D]]),
                                   in_=r[:]).then_inc(s_dA2, 16)
                sync.wait_ge(s_dA2, 16 * 8)

    esA.close()
    if stage == "A":
        es.close()
        return nc, dbg

    esW = ExitStack()
    sbW = lambda name, shape, dt: esW.enter_context(nc.sbuf_tensor(name, shape, dt))

    wbuf = [sbW(f"wbuf{i}", [128, I], F32) for i in range(2)]
    wabs = sbW("wabs", [128, I], F32)
    wsgb = sbW("wsgb", [128, I], BF16)
    wsc = sbW("wsc", [128, I // 16], F32)
    wrs4 = sbW("wrs4", [128, I // 16], F32)
    wrs2 = sbW("wrs2", [128, I // 16], F32)
    wsh = sbW("wsh", [128, I // 16], F32)
    wo1 = sbW("wo1", [128, I], BF16)
    wo2 = sbW("wo2", [128, I], BF16)
    wqv = [sbW(f"wqv{i}", [128, I], BF16) for i in range(2)]
    wtr = [sbW(f"wtr{i}", [128, IK * 128], BF16) for i in range(2)]

    s_wl = [sem("s_wl0"), sem("s_wl1")]
    s_wa = sem("s_wa")
    s_wv = sem("s_wv")
    s_wg2 = sem("s_wg2")
    s_wq = sem("s_wq")
    s_wt = sem("s_wt")
    s_we = sem("s_we")
    s_ws = [sem("s_ws0"), sem("s_ws1")]

    jobs = [("wg", i) for i in range(IK)] + [("wu", i) for i in range(IK)] + [("wd", i) for i in range(DK)]
    NJ = len(jobs)
    jw = lambda j: D if j[0] in ("wg", "wu") else I

    def stores_through(jn):
        tot = 0
        for j in jobs[:jn]:
            tot += DK if j[0] in ("wg", "wu") else 0
        return tot

    with nc.Block() as block, ExitStack() as esP:
        wt_ps = [[esP.enter_context(nc.psum_tensor(f"wtps{i}_{j}", [128, 1024], BF16)) for j in range(3)]
                 for i in range(2)]

        def emit_wstore(sync, pj):
            kind, idx = jobs[pj]
            sync.wait_ge(s_we, pj + 1)
            if kind in ("wg", "wu"):
                scr = wg_scr if kind == "wg" else wu_scr
                for hk in range(DK):
                    dst = bass.AP(scr, (idx * DK + hk) * 128 * 128, [[128, 128], [1, 128]])
                    sync.dma_start(out=dst, in_=wtr[pj % 2][:, hk * 128:(hk + 1) * 128]).then_inc(s_ws[pj % 2], 16)
                return DK
            for ik in range(IK):
                dst = bass.AP(wd_scr, ik * 128 * D + idx * 128, [[D, 128], [1, 128]])
                sync.dma_start(out=dst, in_=wtr[pj % 2][:, ik * 128:(ik + 1) * 128]).then_inc(s_ws[pj % 2], 16)
            return IK

        @block.sync
        def _(sync):
            tot = 0
            for jn, job in enumerate(jobs):
                kind, idx = job
                W = jw(job)
                if jn >= 2:
                    sync.wait_ge(s_wa, 2 * (jn - 1))
                src = {"wg": wg_d, "wu": wu_d, "wd": wd_d}[kind]
                sync.dma_start(out=wbuf[jn % 2][:, 0:W],
                               in_=bass.AP(src, idx * 128 * W, [[W, 128], [1, W]])).then_inc(s_wl[jn % 2], 16)
                if jn >= 1:
                    tot += emit_wstore(sync, jn - 1)
            tot += emit_wstore(sync, NJ - 1)
            tot0 = sum((DK if jobs[j][0] in ("wg", "wu") else IK) for j in range(0, NJ, 2))
            tot1 = sum((DK if jobs[j][0] in ("wg", "wu") else IK) for j in range(1, NJ, 2))
            sync.wait_ge(s_ws[0], 16 * tot0)
            sync.wait_ge(s_ws[1], 16 * tot1)
            sync.wait_ge(s_we, NJ)

        @block.scalar
        def _(scalar):
            for jn, job in enumerate(jobs):
                W = jw(job)
                scalar.wait_ge(s_wl[jn % 2], 16 * (jn // 2 + 1))
                if jn >= 1:
                    scalar.wait_ge(s_wq, jn)
                scalar.activation(wabs[:, 0:W], wbuf[jn % 2][:, 0:W], AF.Abs)
                scalar.activation(wsgb[:, 0:W], wbuf[jn % 2][:, 0:W], AF.Sign).then_inc(s_wa, 2)

        def n_stores_par(pj):
            tot = 0
            for j in range(pj % 2, pj - 1, 2):
                tot += DK if jobs[j][0] in ("wg", "wu") else IK
            return tot

        def emit_wevac(vector, pj):
            W = jw(jobs[pj])
            vector.wait_ge(s_wt, pj + 1)
            if pj >= 2:
                vector.wait_ge(s_ws[pj % 2], 16 * n_stores_par(pj))
            nblk = W // 128
            last = None
            for p in range((nblk + 7) // 8):
                w_p = min(1024, (nblk - 8 * p) * 128)
                last = vector.tensor_copy(wtr[pj % 2][:, p * 1024:p * 1024 + w_p],
                                          wt_ps[pj % 2][p][:, 0:w_p])
            vector.drain().then_inc(s_we, 1)

        @block.vector
        def _(vector):
            for jn, job in enumerate(jobs):
                W = jw(job)
                vector.wait_ge(s_wa, 2 * (jn + 1))
                emit_quant(vector, wabs, W, wsc, wrs4, wrs2, wsh, wo1, wo2)
                vector.drain().then_inc(s_wv, 1)
                vector.wait_ge(s_wg2, jn + 1)
                nb = W // 16
                shb = bass.AP(wsh, 0, [[I // 16, 128], [1, nb], [0, 16]])
                if jn >= 2:
                    vector.wait_ge(s_wt, jn - 1)
                vector.tensor_tensor(out=wqv[jn % 2][:, 0:W], in0=wo1[:, 0:W],
                                     in1=shb, op=OP.mult)
                vector.drain().then_inc(s_wq, 1)
                if jn >= 1:
                    emit_wevac(vector, jn - 1)
            emit_wevac(vector, NJ - 1)

        @block.gpsimd
        def _(gpsimd):
            for jn, job in enumerate(jobs):
                W = jw(job)
                gpsimd.wait_ge(s_wv, jn + 1)
                gpsimd.tensor_tensor(out=wo1[:, 0:W], in0=wo1[:, 0:W],
                                     in1=wsgb[:, 0:W], op=OP.mult).then_inc(s_wg2, 1)

        @block.tensor
        def _(tensor):
            for jn, job in enumerate(jobs):
                W = jw(job)
                tensor.wait_ge(s_wq, jn + 1)
                if jn >= 2:
                    tensor.wait_ge(s_we, jn - 1)
                nblk = W // 128
                last = None
                for b in range(nblk):
                    last = tensor.transpose(wt_ps[jn % 2][b // 8][:, (b % 8) * 128:(b % 8 + 1) * 128],
                                            wqv[jn % 2][:, b * 128:(b + 1) * 128], idb[:])
                last.then_inc(s_wt, 1)

    if "W" in debug:
        s_dW = sem("s_dW")
        with nc.Block() as block:
            @block.gpsimd
            def _(gpsimd):
                n = 0
                for ik in range(IK):
                    for hk in range(DK):
                        gpsimd.dma_start(
                            out=wbuf[0][:, 0:128],
                            in_=bass.AP(wg_scr, (ik * DK + hk) * 128 * 128, [[128, 128], [1, 128]])
                        ).then_inc(s_dW, 16)
                        n += 1
                        gpsimd.wait_ge(s_dW, 32 * n - 16)
                        gpsimd.dma_start(
                            out=bass.AP(dbg["wgq"], (ik * DK + hk) * 128 * 128, [[128, 128], [1, 128]]),
                            in_=wbuf[0][:, 0:128]).then_inc(s_dW, 16)
                        gpsimd.wait_ge(s_dW, 32 * n)
                for ik in range(IK):
                    gpsimd.dma_start(out=wbuf[0][:, 0:D],
                                     in_=bass.AP(wd_scr, ik * 128 * D, [[D, 128], [1, D]])).then_inc(s_dW, 16)
                    n += 1
                    gpsimd.wait_ge(s_dW, 32 * n - 16)
                    gpsimd.dma_start(out=bass.AP(dbg["wdq"], ik * 128 * D, [[D, 128], [1, D]]),
                                     in_=wbuf[0][:, 0:D]).then_inc(s_dW, 16)
                    gpsimd.wait_ge(s_dW, 32 * n)

    esW.close()
    if stage == "W":
        es.close()
        return nc, dbg

    esB = ExitStack()
    sbB = lambda name, shape, dt: esB.enter_context(nc.sbuf_tensor(name, shape, dt))

    wvT = sbB("wvT", [128, DK, D], BF16)
    xTt = [sbB(f"xTt{i}", [128, DK, TILE], BF16) for i in range(2)]
    xres = [sbB(f"xres{i}", [128, D], F32) for i in range(2)]
    Vf = sbB("Vf", [128, DK, TILE], BF16)
    y1t = sbB("y1t", [128, CSUB, D], F32)
    s1f = sbB("s1f", [128, D], F32)
    yab = sbB("yab", [128, D], F32)
    ysg = sbB("ysg", [128, D], BF16)
    yo1 = sbB("yo1", [128, D], BF16)
    yo2 = sbB("yo2", [128, D], BF16)
    yxq = sbB("yxq", [128, D], BF16)
    ysc = sbB("ysc", [128, D // 16], F32)
    yrs4 = sbB("yrs4", [128, D // 16], F32)
    yrs2 = sbB("yrs2", [128, D // 16], F32)
    ysh = sbB("ysh", [128, D // 16], F32)
    xqT = sbB("xqT", [128, DK, TILE], BF16)
    wgs = [sbB(f"wgs{i}", [128, 4 * DK * 128], BF16) for i in range(2)]
    wus = [sbB(f"wus{i}", [128, 4 * DK * 128], BF16) for i in range(2)]
    hst = sbB("hst", [128, 512], F32)
    hab = sbB("hab", [128, 512], F32)
    hsgb = sbB("hsgb", [128, 512], BF16)
    ho1 = sbB("ho1", [128, 512], BF16)
    ho2 = sbB("ho2", [128, 512], BF16)
    hqv = sbB("hqv", [128, 512], BF16)
    hsc = sbB("hsc", [128, 32], F32)
    hrs4 = sbB("hrs4", [128, 32], F32)
    hrs2 = sbB("hrs2", [128, 32], F32)
    hsh = sbB("hsh", [128, 32], F32)
    hqT = sbB("hqT", [128, IK, TILE], BF16)
    wds = [sbB(f"wds{i}", [128, D], BF16) for i in range(4)]
    sil = sbB("sil", [128, 512], F32)
    y2b = [sbB(f"y2b{i}", [128, D], F32) for i in range(2)]
    bnsq = sbB("bnsq", [128, D], F32)
    bred = sbB("bred", [128, 128], F32)
    bred2 = sbB("bred2", [128, 128], F32)
    if "C" in debug:
        bnsq_t = sbB("bnsq_t", [128, D], F32)
        bred2_t = sbB("bred2_t", [128, 128], F32)
        scs_t = sbB("scs_t", [128, 4], F32)
        s1a_t = sbB("s1a_t", [128, D], F32)
        bnag_t = sbB("bnag_t", [128, 2], F32)
        xr_t = sbB("xr_t", [128, D], F32)
    bnag = sbB("bnag", [128, 2], F32)
    bns2 = sbB("bns2", [128, 1], F32)
    bnr1 = sbB("bnr1", [128, 1], F32)
    bnr2 = sbB("bnr2", [128, 1], F32)
    epsc = sbB("epsc", [128, 1], F32)

    s_bw0 = sem("s_bw0")
    s_bx = [sem("s_bx0"), sem("s_bx1")]
    s_br = [sem("s_br0"), sem("s_br1")]
    s_bv = sem("s_bv")
    s_bve = sem("s_bve")
    s_ba = sem("s_ba")
    s_be = sem("s_be")
    s_bse = sem("s_bse")
    s_bcm = sem("s_bcm")
    s_bgp = sem("s_bgp")
    s_bq = sem("s_bq")
    s2_wg = [sem("s2_wg0"), sem("s2_wg1")]
    s2_g = sem("s2_g")
    s2_sil = sem("s2_sil")
    s2_h = sem("s2_h")
    s2_hse = sem("s2_hse")
    s2_hcm = sem("s2_hcm")
    s2_hgp = sem("s2_hgp")
    s2_t = sem("s2_t")
    s2_te = sem("s2_te")
    s2_d = sem("s2_d")
    s2_dk = sem("s2_dk")
    s2_wd = [sem(f"s2_wd{i}") for i in range(4)]
    s2_l2 = sem("s2_l2")
    s2_lse = sem("s2_lse")
    s2_o = sem("s2_o")
    s2_dbg = sem("s2_dbg")
    s2_dbg2 = sem("s2_dbg2")

    with nc.Block() as block:
        @block.gpsimd
        def _(gpsimd):
            gpsimd.dma_start(out=wvT[:], in_=wvT_d.ap()).then_inc(s_bw0, 16)
            src = bass.AP(x_feat, 0, [[C, 128], [128 * C, DK], [1, TILE]])
            gpsimd.dma_start(out=xTt[0][:], in_=src).then_inc(s_bx[0], 16)

        @block.vector
        def _(vector):
            vector.memset(epsc[:], EPS_LN)
            vector.drain()

        @block.sync
        def _(sync):
            sync.wait_ge(s_bw0, 16)
            sync.wait_ge(s_bx[0], 16)

    for t in range(NTILES):
        b_bv, b_bve, b_ba = 8 * t, 8 * t, CSUB * t
        b_be, b_bse, b_bcm, b_bgp, b_bq = 12 * t, 12 * t, 4 * t, 4 * t, 8 * t
        with nc.Block() as block, ExitStack() as esP:
            pp = lambda name, shape, dt: esP.enter_context(nc.psum_tensor(name, shape, dt))
            vps = [pp(f"vps{t}_{i}", [128, TILE], F32) for i in range(2)]
            aps = [pp(f"aps{t}_{i}", [128, D], F32) for i in range(2)]
            tps = [pp(f"tps{t}_{i}", [128, 1024], BF16) for i in range(2)]

            @block.gpsimd
            def _(gpsimd):
                if t + 1 < NTILES:
                    src = bass.AP(x_feat, (t + 1) * TILE, [[C, 128], [128 * C, DK], [1, TILE]])
                    gpsimd.dma_start(out=xTt[(t + 1) % 2][:], in_=src).then_inc(s_bx[(t + 1) % 2], 16)
                for cs in range(CSUB):
                    gpsimd.wait_ge(s_bcm, b_bcm + cs + 1)
                    gpsimd.wait_ge(s_bse, b_bse + 3 * cs + 3)
                    gpsimd.tensor_tensor(out=yo1[:], in0=yo1[:], in1=ysg[:], op=OP.mult).then_inc(s_bgp, 1)

            @block.sync
            def _(sync):
                for cs in range(CSUB):
                    if cs >= 2:
                        sync.wait_ge(s_be, b_be + 3 * (cs - 2) + 1)
                    r0 = t * TILE + cs * 128
                    sync.dma_start(out=xres[cs % 2][:],
                                   in_=bass.AP(x_tok, r0 * D, [[D, 128], [1, D]])).then_inc(s_br[cs % 2], 16)
                i0, W = STRIPS[0]
                nik = W // 128
                sync.dma_start(out=wgs[0][:, 0:nik * DK * 128],
                               in_=bass.AP(wg_scr, 0, [[128, 128], [128 * 128, nik * DK], [1, 128]])
                               ).then_inc(s2_wg[0], 16)
                sync.dma_start(out=wus[0][:, 0:nik * DK * 128],
                               in_=bass.AP(wu_scr, 0, [[128, 128], [128 * 128, nik * DK], [1, 128]])
                               ).then_inc(s2_wg[0], 16)

            @block.tensor
            def _(tensor):
                tensor.wait_ge(s_bx[t % 2], 16 * (t // 2 + 1))
                for mk in range(DK):
                    if mk >= 2:
                        tensor.wait_ge(s_bve, b_bve + mk - 1)
                    for dk in range(DK):
                        mm = tensor.matmul(vps[mk % 2][:], wvT[:, dk, mk * 128:(mk + 1) * 128],
                                           xTt[t % 2][:, dk, :], start=(dk == 0), stop=(dk == DK - 1))
                        if dk == DK - 1:
                            mm.then_inc(s_bv, 1)
                tensor.wait_ge(s_bve, b_bve + 8)

                def emit_xq_transpose(j):
                    tensor.wait_ge(s_be, b_be + 3 * j + 3)
                    if j >= 2:
                        tensor.wait_ge(s_bq, b_bq + 2 * (j - 1))
                    last = None
                    for dk in range(DK):
                        last = tensor.transpose(tps[j % 2][:, dk * 128:(dk + 1) * 128],
                                                yxq[:, dk * 128:(dk + 1) * 128], idb[:])
                    last.then_inc(s_bq, 1)

                for cs in range(CSUB):
                    if cs >= 2:
                        tensor.wait_ge(s_be, b_be + 3 * (cs - 1))
                    for mk in range(DK):
                        for nh in range(2):
                            mm = tensor.matmul(aps[cs % 2][:, nh * 512:(nh + 1) * 512],
                                               Vf[:, mk, cs * 128:(cs + 1) * 128],
                                               G[:, mk, nh * 512:(nh + 1) * 512],
                                               start=(mk == 0), stop=(mk == DK - 1))
                            if mk == DK - 1 and nh == 1:
                                mm.then_inc(s_ba, 1)
                    if cs >= 1:
                        emit_xq_transpose(cs - 1)
                emit_xq_transpose(CSUB - 1)

            @block.scalar
            def _(scalar):
                for mk in range(DK):
                    scalar.wait_ge(s_bv, b_bv + mk + 1)
                    scalar.activation(Vf[:, mk, :], vps[mk % 2][:], AF.Identity,
                                      bias=bvc[:, mk:mk + 1]).then_inc(s_bve, 1)
                for cs in range(CSUB):
                    scalar.wait_ge(s_be, b_be + 3 * cs + 1)
                    scalar.activation(bnr1[:], bnag[:, 1:2], AF.Ln, bias=epsc[:]).then_inc(s_bse, 1)
                    scalar.wait_ge(s_bse, b_bse + 3 * cs + 1)
                    scalar.activation(bnr2[:], bnr1[:], AF.Exp, scale=-0.5).then_inc(s_bse, 1)
                    if "C" in dbg_flags and t == NTILES - 1 and cs == 3:
                        scalar.activation(scs_t[:, 0:1], bnag[:, 1:2], AF.Identity)
                        scalar.activation(scs_t[:, 1:2], bnr1[:], AF.Identity)
                        scalar.activation(scs_t[:, 2:3], bnr2[:], AF.Identity)
                        scalar.activation(scs_t[:, 3:4], epsc[:], AF.Identity)
                    scalar.wait_ge(s_be, b_be + 3 * cs + 2)
                    scalar.activation(yab[:], y1t[:, cs, :], AF.Abs)
                    scalar.activation(ysg[:], y1t[:, cs, :], AF.Sign).then_inc(s_bse, 1)

            @block.vector
            def _(vector):
                for cs in range(CSUB):
                    vector.wait_ge(s_ba, b_ba + cs + 1)
                    vector.wait_ge(s_br[cs % 2], 16 * (2 * t + cs // 2 + 1))
                    vector.tensor_tensor(out=s1f[:], in0=xres[cs % 2][:], in1=aps[cs % 2][:], op=OP.add)
                    if "C" in dbg_flags and t == NTILES - 1 and cs == 3:
                        vector.tensor_copy(s1a_t[:], s1f[:])
                        vector.tensor_copy(xr_t[:], xres[cs % 2][:])
                    emit_bn(vector, s1f, bnsq, bnag, bns2, bred, bred2)
                    vector.drain().then_inc(s_be, 1)
                    if "C" in dbg_flags and t == NTILES - 1 and cs == 3:
                        vector.tensor_copy(bnag_t[:], bnag[:])
                        vector.tensor_copy(bnsq_t[:], bnsq[:])
                        vector.tensor_copy(bred2_t[:, 0:64], bred2[:, 0:64])
                        vector.tensor_copy(bred2_t[:, 64:128], bred[:, 0:64])
                    vector.wait_ge(s_bse, b_bse + 3 * cs + 2)
                    vector.tensor_scalar(out=y1t[:, cs, :], in0=s1f[:],
                                         scalar1=bnag[0:128, 0:1], scalar2=bnr2[0:128, 0:1],
                                         op0=OP.subtract, op1=OP.mult)
                    vector.drain().then_inc(s_be, 1)
                    vector.wait_ge(s_bse, b_bse + 3 * cs + 3)
                    emit_quant(vector, yab, D, ysc, yrs4, yrs2, ysh, yo1, yo2)
                    vector.drain().then_inc(s_bcm, 1)
                    vector.wait_ge(s_bgp, b_bgp + cs + 1)
                    shb = bass.AP(ysh, 0, [[D // 16, 128], [1, D // 16], [0, 16]])
                    vector.tensor_tensor(out=yxq[:], in0=yo1[:], in1=shb, op=OP.mult)
                    vector.drain().then_inc(s_be, 1)
                    vector.wait_ge(s_bq, b_bq + 2 * cs + 1)
                    dst = bass.AP(xqT, cs * 128, [[DK * TILE, 128], [TILE, DK], [1, 128]])
                    vector.tensor_copy(dst, tps[cs % 2][:])
                    vector.drain().then_inc(s_bq, 1)

        b2_wg = 32 * (NST * t + 1)
        b_g, b_sil, b_h = 2 * NST * CSUB * t, NST * CSUB * t, 2 * NST * CSUB * t
        b_hse, b_hcm, b_hgp = 2 * NST * CSUB * t, NST * CSUB * t, NST * CSUB * t
        b_t2, b_te = NST * CSUB * t, NST * CSUB * t
        b_d, b_l2, b_lse, b_o = CSUB * t, 2 * CSUB * t, 2 * CSUB * t, 64 * t

        with nc.Block() as block, ExitStack() as esP:
            pp = lambda name, shape, dt: esP.enter_context(nc.psum_tensor(name, shape, dt))
            gps = [pp(f"bgps{t}_{i}", [128, 512], F32) for i in range(4)]
            hps = [pp(f"bhps{t}_{i}", [128, 512], BF16) for i in range(2)]

            @block.sync
            def _(sync):
                for sn in range(1, NST):
                    i0, W = STRIPS[sn]
                    nik = W // 128
                    ik0 = i0 // 128
                    if sn >= 2:
                        sync.wait_ge(s2_g, b_g + 2 * CSUB * (sn - 1))
                    sync.dma_start(out=wgs[sn % 2][:, 0:nik * DK * 128],
                                   in_=bass.AP(wg_scr, ik0 * DK * 128 * 128,
                                               [[128, 128], [128 * 128, nik * DK], [1, 128]])
                                   ).then_inc(s2_wg[sn % 2], 16)
                    sync.dma_start(out=wus[sn % 2][:, 0:nik * DK * 128],
                                   in_=bass.AP(wu_scr, ik0 * DK * 128 * 128,
                                               [[128, 128], [128 * 128, nik * DK], [1, 128]])
                                   ).then_inc(s2_wg[sn % 2], 16)

            @block.tensor
            def _(tensor):
                for sn in range(NST):
                    i0, W = STRIPS[sn]
                    nik = W // 128
                    tensor.wait_ge(s2_wg[sn % 2], 32 * (3 * t + sn // 2 + 1))
                    for cs in range(CSUB):
                        gu = sn * CSUB + cs
                        if gu >= 2:
                            tensor.wait_ge(s2_h, b_h + 2 * (gu - 1))
                        for pi, wbufs in enumerate((wgs, wus)):
                            last = None
                            for hk in range(DK):
                                rhs = bass.AP(wbufs[sn % 2], hk * 128,
                                              [[4 * DK * 128, 128], [DK * 128, nik], [1, 128]])
                                last = tensor.matmul(gps[2 * (gu % 2) + pi][:, 0:W],
                                                     xqT[:, hk, cs * 128:(cs + 1) * 128], rhs,
                                                     start=(hk == 0), stop=(hk == DK - 1))
                            last.then_inc(s2_g, 1)
                        tensor.wait_ge(s2_h, b_h + 2 * gu + 2)
                        if gu >= 2:
                            tensor.wait_ge(s2_te, b_te + gu - 1)
                        lastt = None
                        for b_ in range(nik):
                            lastt = tensor.transpose(hps[gu % 2][:, b_ * 128:(b_ + 1) * 128],
                                                     hqv[:, b_ * 128:(b_ + 1) * 128], idb[:])
                        lastt.then_inc(s2_t, 1)

            @block.scalar
            def _(scalar):
                for sn in range(NST):
                    i0, W = STRIPS[sn]
                    for cs in range(CSUB):
                        gu = sn * CSUB + cs
                        scalar.wait_ge(s2_g, b_g + 2 * gu + 1)
                        if gu >= 1:
                            scalar.wait_ge(s2_h, b_h + 2 * gu - 1)
                        scalar.activation(sil[:, 0:W], gps[2 * (gu % 2)][:, 0:W], AF.Silu).then_inc(s2_sil, 1)
                        scalar.wait_ge(s2_h, b_h + 2 * gu + 1)
                        scalar.activation(hab[:, 0:W], hst[:, 0:W], AF.Abs)
                        scalar.activation(hsgb[:, 0:W], hst[:, 0:W], AF.Sign).then_inc(s2_hse, 2)

            @block.vector
            def _(vector):
                for sn in range(NST):
                    i0, W = STRIPS[sn]
                    for cs in range(CSUB):
                        gu = sn * CSUB + cs
                        vector.wait_ge(s2_sil, b_sil + gu + 1)
                        vector.wait_ge(s2_g, b_g + 2 * gu + 2)
                        if gu >= 1:
                            vector.wait_ge(s2_hse, b_hse + 2 * gu)
                        vector.tensor_tensor(out=hst[:, 0:W], in0=sil[:, 0:W],
                                             in1=gps[2 * (gu % 2) + 1][:, 0:W], op=OP.mult)
                        vector.drain().then_inc(s2_h, 1)
                        vector.wait_ge(s2_hse, b_hse + 2 * gu + 2)
                        emit_quant(vector, hab, W, hsc, hrs4, hrs2, hsh, ho1, ho2)
                        vector.drain().then_inc(s2_hcm, 1)
                        vector.wait_ge(s2_hgp, b_hgp + gu + 1)
                        nb = W // 16
                        shb = bass.AP(hsh, 0, [[32, 128], [1, nb], [0, 16]])
                        vector.tensor_tensor(out=hqv[:, 0:W], in0=ho1[:, 0:W],
                                             in1=shb, op=OP.mult)
                        vector.drain().then_inc(s2_h, 1)
                        vector.wait_ge(s2_t, b_t2 + gu + 1)
                        nik = W // 128
                        ik0 = i0 // 128
                        dst = bass.AP(hqT, ik0 * TILE + cs * 128, [[IK * TILE, 128], [TILE, nik], [1, 128]])
                        vector.tensor_copy(dst, hps[gu % 2][:, 0:W])
                        vector.drain().then_inc(s2_te, 1)

            @block.gpsimd
            def _(gpsimd):
                for sn in range(NST):
                    i0, W = STRIPS[sn]
                    for cs in range(CSUB):
                        gu = sn * CSUB + cs
                        gpsimd.wait_ge(s2_hcm, b_hcm + gu + 1)
                        gpsimd.wait_ge(s2_hse, b_hse + 2 * gu + 2)
                        gpsimd.tensor_tensor(out=ho1[:, 0:W], in0=ho1[:, 0:W],
                                             in1=hsgb[:, 0:W], op=OP.mult).then_inc(s2_hgp, 1)

        wd_slot_cnt = [6, 6, 5, 5]
        with nc.Block() as block, ExitStack() as esP:
            pp = lambda name, shape, dt: esP.enter_context(nc.psum_tensor(name, shape, dt))
            dps = [pp(f"bdps{t}_{i}", [128, D], F32) for i in range(4)]

            @block.sync
            def _(sync):
                for ik in range(IK):
                    g = IK * t + ik
                    if g >= 4:
                        sync.wait_ge(s2_dk, g - 3)
                    sync.dma_start(out=wds[ik % 4][:],
                                   in_=bass.AP(wd_scr, ik * 128 * D, [[D, 128], [1, D]])
                                   ).then_inc(s2_wd[ik % 4], 16)
                for cs in range(CSUB):
                    sync.wait_ge(s2_l2, b_l2 + 2 * cs + 2)
                    r0 = t * TILE + cs * 128
                    sync.dma_start(out=bass.AP(y_d, r0 * D, [[D, 128], [1, D]]),
                                   in_=y2b[cs % 2][:]).then_inc(s2_o, 16)
                if t == NTILES - 1:
                    sync.wait_ge(s2_o, 16 * 4 * NTILES)

            @block.tensor
            def _(tensor):
                for ik in range(IK):
                    slot = ik % 4
                    tensor.wait_ge(s2_wd[slot], 16 * (wd_slot_cnt[slot] * t + ik // 4 + 1))
                    last = None
                    for cs in range(CSUB):
                        for nh in range(2):
                            last = tensor.matmul(dps[cs][:, nh * 512:(nh + 1) * 512],
                                                 hqT[:, ik, cs * 128:(cs + 1) * 128],
                                                 wds[slot][:, nh * 512:(nh + 1) * 512],
                                                 start=(ik == 0), stop=(ik == IK - 1))
                    last.then_inc(s2_dk, 1)

            @block.scalar
            def _(scalar):
                for cs in range(CSUB):
                    scalar.wait_ge(s2_l2, b_l2 + 2 * cs + 1)
                    scalar.activation(bnr1[:], bnag[:, 1:2], AF.Ln, bias=epsc[:]).then_inc(s2_lse, 1)
                    scalar.wait_ge(s2_lse, b_lse + 2 * cs + 1)
                    scalar.activation(bnr2[:], bnr1[:], AF.Exp, scale=-0.5).then_inc(s2_lse, 1)

            @block.vector
            def _(vector):
                for cs in range(CSUB):
                    vector.wait_ge(s2_dk, IK * (t + 1))
                    if cs >= 2:
                        vector.wait_ge(s2_o, b_o + 16 * (cs - 1))
                    for nh in range(2):
                        sl = slice(nh * 512, (nh + 1) * 512)
                        vector.tensor_tensor(out=s1f[:, sl], in0=y1t[:, cs, sl], in1=dps[cs][:, sl], op=OP.add)
                    emit_bn(vector, s1f, bnsq, bnag, bns2, bred, bred2)
                    vector.drain().then_inc(s2_l2, 1)
                    vector.wait_ge(s2_lse, b_lse + 2 * cs + 2)
                    vector.tensor_scalar(out=y2b[cs % 2][:], in0=s1f[:],
                                         scalar1=bnag[0:128, 0:1], scalar2=bnr2[0:128, 0:1],
                                         op0=OP.subtract, op1=OP.mult)
                    vector.drain().then_inc(s2_l2, 1)

        if "B" in debug and t == NTILES - 1:
            with nc.Block() as block:
                @block.sync
                def _(sync):
                    sync.wait_ge(s2_o, 16 * 4 * NTILES)
                    for cs in range(CSUB):
                        r0 = cs * 128
                        sync.dma_start(out=bass.AP(dbg["y1"], r0 * D, [[D, 128], [1, D]]),
                                       in_=y1t[:, cs, :]).then_inc(s2_dbg, 16)
                    sync.dma_start(out=bass.AP(dbg["zf"], 0, [[D, 128], [1, D]]),
                                   in_=s1f[:]).then_inc(s2_dbg, 16)
                    sync.wait_ge(s2_dbg, 16 * 5)

                @block.gpsimd
                def _(gpsimd):
                    gpsimd.wait_ge(s2_o, 16 * 4 * NTILES)
                    gpsimd.dma_start(out=bass.AP(dbg["xqt"], 0,
                                                 [[DK * TILE, 128], [1, DK * TILE]]),
                                     in_=xqT[:]).then_inc(s2_dbg2, 16)
                    gpsimd.dma_start(out=bass.AP(dbg["hqt"], 0,
                                                 [[IK * TILE, 128], [1, IK * TILE]]),
                                     in_=hqT[:]).then_inc(s2_dbg2, 16)
                    gpsimd.wait_ge(s2_dbg2, 16 * 2)

    if "C" in debug:
        s2_dbg3 = sem("s2_dbg3")
        with nc.Block() as block:
            @block.gpsimd
            def _(gpsimd):
                gpsimd.wait_ge(s2_o, 16 * 4 * NTILES)
                gpsimd.dma_start(out=dbg["bnsqd"].ap(), in_=bnsq_t[:]).then_inc(s2_dbg3, 16)
                gpsimd.dma_start(out=dbg["bred2d"].ap(), in_=bred2_t[:]).then_inc(s2_dbg3, 16)
                gpsimd.dma_start(out=dbg["scs"].ap(), in_=scs_t[:]).then_inc(s2_dbg3, 16)
                gpsimd.dma_start(out=dbg["s1a"].ap(), in_=s1a_t[:]).then_inc(s2_dbg3, 16)
                gpsimd.dma_start(out=dbg["bnagd"].ap(), in_=bnag_t[:]).then_inc(s2_dbg3, 16)
                gpsimd.dma_start(out=dbg["xrd"].ap(), in_=xr_t[:]).then_inc(s2_dbg3, 16)
                gpsimd.dma_start(out=bass.AP(dbg["vfd"], 0, [[DK * TILE, 128], [1, DK * TILE]]),
                                 in_=Vf[:]).then_inc(s2_dbg3, 16)
                gpsimd.wait_ge(s2_dbg3, 16 * 7)

    esB.close()
    es.close()
    return nc, dbg


_PROG_CACHE = {}


def _build():
    if "nc" not in _PROG_CACHE:
        nc, _ = build_core_program(stage="ALL", debug=())
        mybir.codegen_inst_isa_subclasses(nc)
        _PROG_CACHE["nc"] = nc
    return _PROG_CACHE["nc"]


def _wT_layout(W):
    Dm = W.shape[0]
    return np.ascontiguousarray(W.T.reshape(Dm // 128, 128, Dm).transpose(1, 0, 2).reshape(Dm, Dm))


def kernel(L0_attn_x, L0_attn_Wq, L0_attn_bq, L0_attn_Wk, L0_attn_bk,
           L0_attn_Wv, L0_attn_bv, L0_attn_gamma,
           L0_ffn_w_gate, L0_ffn_w_up, L0_ffn_w_down):
    x = np.ascontiguousarray(np.asarray(L0_attn_x, dtype=np.float32))
    B = x.shape[0]
    assert x.shape == (B, C, D)
    wqT = _wT_layout(np.asarray(L0_attn_Wq, dtype=np.float32))
    wkT = _wT_layout(np.asarray(L0_attn_Wk, dtype=np.float32))
    wvT = _wT_layout(np.asarray(L0_attn_Wv, dtype=np.float32))
    bq = np.ascontiguousarray(np.asarray(L0_attn_bq, dtype=np.float32).reshape(1, D))
    bk = np.ascontiguousarray(np.asarray(L0_attn_bk, dtype=np.float32).reshape(1, D))
    bvc = np.ascontiguousarray(np.asarray(L0_attn_bv, dtype=np.float32).reshape(DK, 128).T)
    gam = np.ascontiguousarray(np.asarray(L0_attn_gamma, dtype=np.float32).reshape(1, 1))
    wg = np.ascontiguousarray(np.asarray(L0_ffn_w_gate, dtype=np.float32))
    wu = np.ascontiguousarray(np.asarray(L0_ffn_w_up, dtype=np.float32))
    wd = np.ascontiguousarray(np.asarray(L0_ffn_w_down, dtype=np.float32))
    idb = np.eye(128, dtype=ml_dtypes.bfloat16)
    idf = np.ones((1, 1), dtype=np.float32)
    onec = np.ones((128, 1), dtype=ml_dtypes.bfloat16)

    nc = _build()
    in_maps = []
    for b in range(B):
        in_maps.append({
            "x_tok": x[b],
            "x_feat": np.ascontiguousarray(x[b].T),
            "wqT_hbm": wqT, "wkT_hbm": wkT, "wvT_hbm": wvT,
            "bq": bq, "bk": bk, "bvc_hbm": bvc, "gamma": gam,
            "wg": wg, "wu": wu, "wd": wd,
            "idb_hbm": idb, "idf_hbm": idf, "onec_hbm": onec,
        })
    res = run_bass_kernel_spmd(nc, in_maps, core_ids=list(range(B)))
    out = np.stack([np.asarray(res.results[b]["y"], dtype=np.float32) for b in range(B)], axis=0)
    return out


# version 10
# speedup vs baseline: 1.0265x; 1.0044x over previous
import os
import numpy as np
import ml_dtypes
from contextlib import ExitStack

import concourse.bass as bass
import concourse.mybir as mybir
from concourse.bass_utils import run_bass_kernel_spmd

F32 = mybir.dt.float32
BF16 = mybir.dt.bfloat16
AF = mybir.ActivationFunctionType
OP = mybir.AluOpType

import concourse.dve_ops as dve_ops
from concourse.dve_op import DveOp
from concourse.dve_spec import Spec, Src0, Src1, C0, C1, One, lower as dve_lower


def _register_op(name, spec_body, reference):
    ver = "v3"
    spec = Spec(body=spec_body)
    compiled = dve_lower(spec, ver=ver, perf_en=False, name=name)
    op = DveOp(name, spec, subdim=False, uops_sha={ver: compiled.sha(ver)}, reference=reference)
    if name not in dve_ops._SUB_OPCODE_FOR_NAME:
        dve_ops.OPS.append(op)
        row = dve_ops._CUSTOM_DVE_ROW_BASE + len(dve_ops.OPS) - 1
        assert row < 0x20
        dve_ops._SUB_OPCODE_FOR_NAME[name] = row
        dve_ops.CUSTOM_DVE_SPECS[name] = spec
        dve_ops.CUSTOM_DVE_REFS[name] = reference
    return next(o for o in dve_ops.OPS if o.name == name)


_m1 = Src0 * Src1
QLAD1 = _register_op(
    "AQL1",
    (((_m1 > One) + (_m1 > (C1 - C0))) + (_m1 > C0)) + (_m1 > (C1 - One)),
    lambda in0, in1, s0, s1, imm2: (
        ((in0 * in1 > 1.0) * 1.0 + (in0 * in1 > (s1 - s0)) * 1.0)
        + ((in0 * in1 > s0) * 1.0 + (in0 * in1 > (s1 - 1.0)) * 1.0)
    ),
)
_m2 = Src0 * Src1
_q7 = _m2 > (C0 + C0)
_t2 = ((_m2 > C0) + (_m2 > C1)) + (_q7 + _q7)
QLAD2 = _register_op(
    "AQL2",
    _t2 + _t2,
    lambda in0, in1, s0, s1, imm2: 2.0 * (
        ((in0 * in1 > s0) * 1.0 + (in0 * in1 > s1) * 1.0)
        + 2.0 * (in0 * in1 > (s0 + s0)) * 1.0
    ),
)

C = 4096
D = 1024
I = 2816
NCC = C // 128
DK = D // 128
IK = I // 128
TILE = 512
NTILES = C // TILE
CSUB = TILE // 128
STRIPS = [(0, 512), (512, 512), (1024, 512), (1536, 512), (2048, 512), (2560, 256)]
NST = len(STRIPS)
EPS_ATTN = 1e-6
EPS_LN = 1e-5


class VFence:
    def __init__(self):
        self.sem = None
        self.n = 0

    def bind(self, sem):
        self.sem = sem

    def __call__(self, eng, inst):
        return inst


VF = VFence()


def emit_quant(vector, src_abs, w, sc, rs4, rs2, sh, o1, o2):
    nb = w // 16
    nbt = sc.shape[1]
    a3 = src_abs[:, 0:w].rearrange("p (b k) -> p b k", k=16)
    vector.tensor_reduce(out=sc[:, 0:nb], in_=a3, axis=mybir.AxisListType.X,
                         op=OP.max, apply_absolute_value=False)
    vector.drain()
    vector.reciprocal(rs2[:, 0:nb], sc[:, 0:nb])
    vector.drain()
    vector.tensor_scalar(out=rs4[:, 0:nb], in0=rs2[:, 0:nb],
                         scalar1=1.6666666666666666e11, scalar2=24.0, op0=OP.min, op1=OP.mult)
    vector.tensor_scalar(out=sh[:, 0:nb], in0=sc[:, 0:nb],
                         scalar1=6e-12, scalar2=1.0 / 12.0, op0=OP.max, op1=OP.mult)
    vector.tensor_scalar(out=rs2[:, 0:nb], in0=rs2[:, 0:nb],
                         scalar1=1.6666666666666666e11, scalar2=12.0, op0=OP.min, op1=OP.mult)
    rs4b = bass.AP(rs4, 0, [[nbt, 128], [1, nb], [0, 16]])
    rs2b = bass.AP(rs2, 0, [[nbt, 128], [1, nb], [0, 16]])
    vector._custom_dve(QLAD1, out=o1[:, 0:w], in0=src_abs[:, 0:w], in1=rs4b, s0=5.0, s1=8.0)
    vector._custom_dve(QLAD2, out=o2[:, 0:w], in0=src_abs[:, 0:w], in1=rs2b, s0=5.0, s1=7.0)
    return vector.tensor_tensor(out=o1[:, 0:w], in0=o1[:, 0:w], in1=o2[:, 0:w], op=OP.add)


def emit_bn(vector, src, bnsq, bnag, bns2, bred, bred2):
    a3 = src[:, 0:1024].rearrange("p (b k) -> p b k", k=16)
    q3 = bnsq[:, 0:1024].rearrange("p (b k) -> p b k", k=16)
    vector.tensor_tensor(out=bnsq[:], in0=src[:], in1=src[:], op=OP.mult)
    vector.drain()
    vector.tensor_reduce(out=bred2[:, 0:64], in_=q3, axis=mybir.AxisListType.X, op=OP.add)
    vector.drain()
    vector.tensor_reduce(out=bred[:, 0:64], in_=a3, axis=mybir.AxisListType.X, op=OP.add)
    vector.drain()
    w = 32
    while w >= 1:
        vector.tensor_tensor(out=bred[:, 0:w], in0=bred[:, 0:w], in1=bred[:, w:2 * w], op=OP.add)
        vector.tensor_tensor(out=bred2[:, 0:w], in0=bred2[:, 0:w],
                             in1=bred2[:, w:2 * w], op=OP.add)
        w //= 2
    vector.tensor_scalar(out=bnag[:, 0:1], in0=bred[:, 0:1], scalar1=1.0 / 1024.0, scalar2=None, op0=OP.mult)
    vector.tensor_scalar(out=bnag[:, 1:2], in0=bred2[:, 0:1], scalar1=1.0 / 1024.0, scalar2=None, op0=OP.mult)
    vector.tensor_tensor(out=bns2[:, 0:1], in0=bnag[:, 0:1], in1=bnag[:, 0:1], op=OP.mult)
    vector.drain()
    return vector.tensor_tensor(out=bnag[:, 1:2], in0=bnag[:, 1:2], in1=bns2[:, 0:1], op=OP.subtract)


def build_core_program(stage="ALL", debug=()):
    dbg_flags = debug
    nc = bass.Bass("TRN2", target_bir_lowering=False, detect_race_conditions=False)

    x_tok = nc.dram_tensor("x_tok", [C, D], F32, kind="ExternalInput")
    x_feat = nc.dram_tensor("x_feat", [D, C], F32, kind="ExternalInput")
    wqT_d = nc.dram_tensor("wqT_hbm", [D, D], F32, kind="ExternalInput")
    wkT_d = nc.dram_tensor("wkT_hbm", [D, D], F32, kind="ExternalInput")
    wvT_d = nc.dram_tensor("wvT_hbm", [D, D], F32, kind="ExternalInput")
    bq_d = nc.dram_tensor("bq", [1, D], F32, kind="ExternalInput")
    bk_d = nc.dram_tensor("bk", [1, D], F32, kind="ExternalInput")
    bvc_d = nc.dram_tensor("bvc_hbm", [128, DK], F32, kind="ExternalInput")
    gamma_d = nc.dram_tensor("gamma", [1, 1], F32, kind="ExternalInput")
    wg_d = nc.dram_tensor("wg", [I, D], F32, kind="ExternalInput")
    wu_d = nc.dram_tensor("wu", [I, D], F32, kind="ExternalInput")
    wd_d = nc.dram_tensor("wd", [D, I], F32, kind="ExternalInput")
    idb_d = nc.dram_tensor("idb_hbm", [128, 128], BF16, kind="ExternalInput")
    idf_d = nc.dram_tensor("idf_hbm", [1, 1], F32, kind="ExternalInput")
    onec_d = nc.dram_tensor("onec_hbm", [128, 1], BF16, kind="ExternalInput")

    y_d = nc.dram_tensor("y", [C, D], F32, kind="ExternalOutput")

    wg_scr = nc.dram_tensor("wg_scr", [IK, DK, 128, 128], BF16, kind="Internal")
    row_scr = nc.dram_tensor("row_scr", [2, D], F32, kind="Internal")
    wu_scr = nc.dram_tensor("wu_scr", [IK, DK, 128, 128], BF16, kind="Internal")
    wd_scr = nc.dram_tensor("wd_scr", [IK, 128, D], BF16, kind="Internal")

    dbg = {}
    if "A" in debug:
        dbg["qt"] = nc.dram_tensor("dbg_qt", [C, D], F32, kind="ExternalOutput")
        dbg["kt"] = nc.dram_tensor("dbg_kt", [C, D], F32, kind="ExternalOutput")
        dbg["g"] = nc.dram_tensor("dbg_g", [D, D], F32, kind="ExternalOutput")
        dbg["rows"] = nc.dram_tensor("dbg_rows", [8, D], F32, kind="ExternalOutput")
    if "W" in debug:
        dbg["wgq"] = nc.dram_tensor("dbg_wgq", [IK * DK * 128, 128], F32, kind="ExternalOutput")
        dbg["wdq"] = nc.dram_tensor("dbg_wdq", [I, D], F32, kind="ExternalOutput")
    if "C" in debug:
        dbg["bnsqd"] = nc.dram_tensor("dbg_bnsqd", [128, D], F32, kind="ExternalOutput")
        dbg["bred2d"] = nc.dram_tensor("dbg_bred2d", [128, 128], F32, kind="ExternalOutput")
        dbg["scs"] = nc.dram_tensor("dbg_scs", [128, 4], F32, kind="ExternalOutput")
        dbg["s1a"] = nc.dram_tensor("dbg_s1a", [128, D], F32, kind="ExternalOutput")
        dbg["bnagd"] = nc.dram_tensor("dbg_bnagd", [128, 2], F32, kind="ExternalOutput")
        dbg["vfd"] = nc.dram_tensor("dbg_vfd", [128, DK * TILE], F32, kind="ExternalOutput")
        dbg["xrd"] = nc.dram_tensor("dbg_xrd", [128, D], F32, kind="ExternalOutput")
    if "B" in debug:
        dbg["y1"] = nc.dram_tensor("dbg_y1", [TILE, D], F32, kind="ExternalOutput")
        dbg["xqt"] = nc.dram_tensor("dbg_xqt", [128, DK * TILE], F32, kind="ExternalOutput")
        dbg["hqt"] = nc.dram_tensor("dbg_hqt", [128, IK * TILE], F32, kind="ExternalOutput")
        dbg["zf"] = nc.dram_tensor("dbg_zf", [128, D], F32, kind="ExternalOutput")

    es = ExitStack()
    sb = lambda name, shape, dt: es.enter_context(nc.sbuf_tensor(name, shape, dt))
    sem = lambda name: es.enter_context(nc.semaphore(name))

    idb = sb("idb", [128, 128], BF16)
    idf = sb("idf", [1, 1], F32)
    onec = sb("onec", [128, 1], BF16)
    gam = sb("gam", [1, 1], F32)
    bvc = sb("bvc", [128, DK], F32)
    G = sb("G", [128, DK, D], BF16)

    s_init = sem("s_init")
    with nc.Block() as block:
        @block.sync
        def _(sync):
            for dst, src in ((idb, idb_d), (idf, idf_d), (onec, onec_d),
                             (gam, gamma_d), (bvc, bvc_d)):
                sync.dma_start(out=dst[:], in_=src.ap()).then_inc(s_init, 16)

    esA = ExitStack()
    sbA = lambda name, shape, dt: esA.enter_context(nc.sbuf_tensor(name, shape, dt))

    Q_t = sbA("Q_t", [128, NCC, D], BF16)
    K_t = sbA("K_t", [128, NCC, D], BF16)
    esA1 = ExitStack()
    sbA1 = lambda name, shape, dt: esA1.enter_context(nc.sbuf_tensor(name, shape, dt))
    wqT = sbA1("wqT", [128, DK, D], BF16)
    wkT = sbA1("wkT", [128, DK, D], BF16)
    bq_bc = sbA1("bq_bc", [128, D], F32)
    bk_bc = sbA1("bk_bc", [128, D], F32)
    xTc = [sbA1(f"xTc{i}", [128, DK, 128], BF16) for i in range(2)]

    s_aw = sem("s_aw")
    s_awg = sem("s_awg")
    s_ax = [sem("s_ax0"), sem("s_ax1")]
    s_aq = sem("s_aq")
    s_ae = sem("s_ae")

    with nc.Block() as block, ExitStack() as esP:
        qk_ps = [esP.enter_context(nc.psum_tensor(f"qkps{i}", [128, 512], F32)) for i in range(8)]

        @block.gpsimd
        def _(gpsimd):
            gpsimd.wait_ge(s_init, 80)
            gpsimd.dma_start(out=wqT[:], in_=wqT_d.ap()).then_inc(s_awg, 16)
            gpsimd.dma_start(out=wkT[:], in_=wkT_d.ap()).then_inc(s_awg, 16)
            for cc in range(NCC):
                if cc >= 2:
                    gpsimd.wait_ge(s_aq, 4 * (cc - 1))
                src = bass.AP(x_feat, cc * 128, [[C, 128], [128 * C, DK], [1, 128]])
                gpsimd.dma_start(out=xTc[cc % 2][:], in_=src).then_inc(s_ax[cc % 2], 16)

        @block.sync
        def _(sync):
            sync.dma_start(out=bq_bc[:], in_=bass.AP(bq_d, 0, [[0, 128], [1, D]])).then_inc(s_aw, 16)
            sync.dma_start(out=bk_bc[:], in_=bass.AP(bk_d, 0, [[0, 128], [1, D]])).then_inc(s_aw, 16)

        @block.tensor
        def _(tensor):
            tensor.wait_ge(s_aw, 32)
            tensor.wait_ge(s_awg, 32)
            for cc in range(NCC):
                tensor.wait_ge(s_ax[cc % 2], 16 * (cc // 2 + 1))
                if cc >= 2:
                    tensor.wait_ge(s_ae, cc - 1)
                bs = 4 * (cc % 2)
                xc = xTc[cc % 2]
                for dk in range(DK):
                    lhs = xc[:, dk, :]
                    for j, (wT, nh) in enumerate(((wqT, 0), (wqT, 1), (wkT, 0), (wkT, 1))):
                        mm = tensor.matmul(qk_ps[bs + j][:], lhs,
                                           wT[:, dk, nh * 512:(nh + 1) * 512],
                                           start=(dk == 0), stop=(dk == DK - 1))
                        if dk == DK - 1:
                            mm.then_inc(s_aq, 1)

        @block.vector
        def _(vector):
            for cc in range(NCC):
                vector.wait_ge(s_aq, 4 * (cc + 1))
                bs = 4 * (cc % 2)
                for j, (dst, bias, nh) in enumerate(((Q_t, bq_bc, 0), (Q_t, bq_bc, 1),
                                                     (K_t, bk_bc, 0), (K_t, bk_bc, 1))):
                    vector.tensor_tensor(
                        out=dst[:, cc, nh * 512:(nh + 1) * 512],
                        in0=qk_ps[bs + j][:], in1=bias[:, nh * 512:(nh + 1) * 512], op=OP.add)
                    if j == 3:
                        vector.drain().then_inc(s_ae, 1)

    esA1.close()
    sqb = [sbA(f"sqb{i}", [128, D], BF16) for i in range(4)]
    gt_bc = sbA("gt_bc", [128, D], F32)
    gq_bc = sbA("gq_bc", [128, D], F32)
    qsq_r = sbA("qsq_r", [1, D], F32)
    qcs_r = sbA("qcs_r", [1, D], F32)
    ksq_r = sbA("ksq_r", [1, D], F32)
    p_r = sbA("p_r", [1, D], F32)
    qn_r = sbA("qn_r", [1, D], F32)
    kn_r = sbA("kn_r", [1, D], F32)
    rqn_r = sbA("rqn_r", [1, D], F32)
    t_r = sbA("t_r", [1, D], F32)
    tmp_r = sbA("tmp_r", [1, D], F32)
    gt_r = sbA("gt_r", [1, D], F32)
    rkn_c = sbA("rkn_c", [128, DK], F32)
    s_sq = sem("s_sq")
    s_st = sem("s_st")
    with nc.Block() as block, ExitStack() as esP:
        st_ps = [esP.enter_context(nc.psum_tensor(f"stps{i}", [1, 512], F32)) for i in range(6)]

        @block.scalar
        def _(scalar):
            for cc in range(NCC):
                if cc >= 2:
                    scalar.wait_ge(s_st, cc - 1)
                scalar.activation(sqb[2 * (cc % 2)][:], Q_t[:, cc, :], AF.Square)
                scalar.activation(sqb[2 * (cc % 2) + 1][:], K_t[:, cc, :], AF.Square).then_inc(s_sq, 1)

        @block.tensor
        def _(tensor):
            for cc in range(NCC):
                tensor.wait_ge(s_sq, cc + 1)
                qs, ks = sqb[2 * (cc % 2)], sqb[2 * (cc % 2) + 1]
                st, sp = (cc == 0), (cc == NCC - 1)
                last = None
                for nh in range(2):
                    sl = slice(nh * 512, (nh + 1) * 512)
                    tensor.matmul(st_ps[nh][:], onec[:], qs[:, sl], start=st, stop=sp)
                    tensor.matmul(st_ps[2 + nh][:], onec[:], Q_t[:, cc, sl], start=st, stop=sp)
                    last = tensor.matmul(st_ps[4 + nh][:], onec[:], ks[:, sl], start=st, stop=sp)
                last.then_inc(s_st, 1)

        @block.vector
        def _(vector):
            vector.wait_ge(s_st, NCC)
            for nh in range(2):
                sl = slice(nh * 512, (nh + 1) * 512)
                vector.tensor_copy(qsq_r[:, sl], st_ps[nh][:])
                vector.tensor_copy(qcs_r[:, sl], st_ps[2 + nh][:])
                vector.tensor_copy(ksq_r[:, sl], st_ps[4 + nh][:])

    s_r = sem("s_r")
    s_rc = sem("s_rc")
    with nc.Block() as block, ExitStack() as esP:
        kc_ps = esP.enter_context(nc.psum_tensor("kcps", [128, DK], F32))

        @block.scalar
        def _(scalar):
            scalar.activation(tmp_r[:], qsq_r[:], AF.Ln).then_inc(s_rc, 1)
            scalar.wait_ge(s_rc, 1)
            scalar.activation(qn_r[:], tmp_r[:], AF.Exp, scale=0.5).then_inc(s_rc, 1)
            scalar.wait_ge(s_rc, 2)
            scalar.activation(tmp_r[:], ksq_r[:], AF.Ln).then_inc(s_rc, 1)
            scalar.wait_ge(s_rc, 3)
            scalar.activation(kn_r[:], tmp_r[:], AF.Exp, scale=0.5).then_inc(s_r, 1)

        @block.vector
        def _(vector):
            vector.wait_ge(s_r, 1)
            vector.reciprocal(rqn_r[:], qn_r[:])
            vector.drain()
            vector.reciprocal(tmp_r[:], kn_r[:])
            vector.drain().then_inc(s_r, 1)

        @block.tensor
        def _(tensor):
            tensor.wait_ge(s_r, 2)
            last = None
            for k in range(DK):
                last = tensor.transpose(kc_ps[:, k:k + 1], tmp_r[:, k * 128:(k + 1) * 128], idf[:])
            last.then_inc(s_r, 1)

        @block.vector
        def _(vector):
            vector.wait_ge(s_r, 3)
            vector.tensor_copy(rkn_c[:], kc_ps[:])

    s_g = sem("s_g")
    s_ge = sem("s_ge")
    s_p = sem("s_p")
    with nc.Block() as block, ExitStack() as esP:
        g_ps = [esP.enter_context(nc.psum_tensor(f"gps{i}", [128, 512], F32)) for i in range(8)]

        @block.tensor
        def _(tensor):
            for mh in range(2):
                if mh == 1:
                    tensor.wait_ge(s_ge, 8)
                for cc in range(NCC):
                    for mk in range(4):
                        m0 = (mh * 4 + mk) * 128
                        lhs = K_t[:, cc, m0:m0 + 128]
                        for nh in range(2):
                            mm = tensor.matmul(g_ps[2 * mk + nh][:], lhs,
                                               Q_t[:, cc, nh * 512:(nh + 1) * 512],
                                               start=(cc == 0), stop=(cc == NCC - 1))
                            if cc == NCC - 1:
                                mm.then_inc(s_g, 1)
        @block.vector
        def _(vector):
            for mh in range(2):
                vector.wait_ge(s_g, 8 * (mh + 1))
                for mk in range(4):
                    gmk = mh * 4 + mk
                    for nh in range(2):
                        vector.tensor_scalar(
                            out=G[:, gmk, nh * 512:(nh + 1) * 512],
                            in0=g_ps[2 * mk + nh][:],
                            scalar1=rkn_c[:, gmk:gmk + 1], scalar2=None,
                            op0=OP.mult)
                vector.drain().then_inc(s_ge, 8)

    with nc.Block() as block, ExitStack() as esP:
        p_ps = [esP.enter_context(nc.psum_tensor(f"pps{i}", [1, 512], F32)) for i in range(2)]

        @block.tensor
        def _(tensor):
            last = None
            for gmk in range(DK):
                for nh in range(2):
                    last = tensor.matmul(p_ps[nh][:], onec[:], G[:, gmk, nh * 512:(nh + 1) * 512],
                                         start=(gmk == 0), stop=(gmk == DK - 1))
            last.then_inc(s_p, 1)

        @block.vector
        def _(vector):
            vector.wait_ge(s_p, 1)
            for nh in range(2):
                vector.tensor_copy(p_r[:, nh * 512:(nh + 1) * 512], p_ps[nh][:])

    s_t = sem("s_t")
    with nc.Block() as block:
        @block.vector
        def _(vector):
            VF(vector, vector.tensor_scalar(out=tmp_r[:], in0=qcs_r[:], scalar1=EPS_ATTN, scalar2=None, op0=OP.mult))
            VF(vector, vector.tensor_tensor(out=tmp_r[:], in0=tmp_r[:], in1=p_r[:], op=OP.add))
            VF(vector, vector.tensor_tensor(out=tmp_r[:], in0=tmp_r[:], in1=rqn_r[:], op=OP.mult))
            VF(vector, vector.tensor_scalar(out=tmp_r[:], in0=tmp_r[:], scalar1=float(D), scalar2=None, op0=OP.add))
            vector.reciprocal(t_r[:], tmp_r[:])
            vector.drain()
            vector.tensor_tensor(out=tmp_r[:], in0=t_r[:], in1=rqn_r[:], op=OP.mult)
            VF(vector, vector.tensor_scalar(out=tmp_r[:], in0=tmp_r[:], scalar1=gam[0:1, 0:1], scalar2=None, op0=OP.mult))
            vector.tensor_scalar(out=gt_r[:], in0=t_r[:], scalar1=gam[0:1, 0:1],
                                 scalar2=None, op0=OP.mult)
            vector.drain().then_inc(s_t, 1)

        @block.sync
        def _(sync):
            sync.wait_ge(s_t, 1)
            sync.dma_start(out=bass.AP(row_scr, 0, [[D, 1], [1, D]]), in_=tmp_r[:]).then_inc(s_t, 16)
            sync.dma_start(out=bass.AP(row_scr, D, [[D, 1], [1, D]]), in_=gt_r[:]).then_inc(s_t, 16)
            sync.wait_ge(s_t, 33)
            sync.dma_start(out=gq_bc[:], in_=bass.AP(row_scr, 0, [[0, 128], [1, D]])).then_inc(s_t, 16)
            sync.dma_start(out=gt_bc[:], in_=bass.AP(row_scr, D, [[0, 128], [1, D]])).then_inc(s_t, 16)

        @block.vector
        def _(vector):
            vector.wait_ge(s_t, 65)
            for mk in range(DK):
                vector.tensor_tensor(out=G[:, mk, :], in0=G[:, mk, :], in1=gq_bc[:], op=OP.mult)
                vector.tensor_tensor(out=G[:, mk, :], in0=G[:, mk, :], in1=gt_bc[:], op=OP.add)

    if "A" in debug:
        s_dA = sem("s_dA")
        s_dA2 = sem("s_dA2")
        with nc.Block() as block:
            @block.gpsimd
            def _(gpsimd):
                n = 0
                for cc in range(NCC):
                    gpsimd.dma_start(out=bass.AP(dbg["qt"], cc * 128 * D, [[D, 128], [1, D]]),
                                     in_=Q_t[:, cc, :]).then_inc(s_dA, 16)
                    gpsimd.dma_start(out=bass.AP(dbg["kt"], cc * 128 * D, [[D, 128], [1, D]]),
                                     in_=K_t[:, cc, :]).then_inc(s_dA, 16)
                    n += 2
                for mk in range(DK):
                    gpsimd.dma_start(out=bass.AP(dbg["g"], mk * 128 * D, [[D, 128], [1, D]]),
                                     in_=G[:, mk, :]).then_inc(s_dA, 16)
                    n += 1
                gpsimd.wait_ge(s_dA, 16 * n)

            @block.sync
            def _(sync):
                for i, r in enumerate((qsq_r, qcs_r, ksq_r, p_r, qn_r, kn_r, t_r, tmp_r)):
                    sync.dma_start(out=bass.AP(dbg["rows"], i * D, [[D, 1], [1, D]]),
                                   in_=r[:]).then_inc(s_dA2, 16)
                sync.wait_ge(s_dA2, 16 * 8)

    esA.close()
    if stage == "A":
        es.close()
        return nc, dbg

    esW = ExitStack()
    sbW = lambda name, shape, dt: esW.enter_context(nc.sbuf_tensor(name, shape, dt))

    wbuf = [sbW(f"wbuf{i}", [128, I], F32) for i in range(2)]
    wabs = sbW("wabs", [128, I], F32)
    wsgb = sbW("wsgb", [128, I], BF16)
    wsc = sbW("wsc", [128, I // 16], F32)
    wrs4 = sbW("wrs4", [128, I // 16], F32)
    wrs2 = sbW("wrs2", [128, I // 16], F32)
    wsh = sbW("wsh", [128, I // 16], F32)
    wo1 = sbW("wo1", [128, I], BF16)
    wo2 = sbW("wo2", [128, I], BF16)
    wqv = [sbW(f"wqv{i}", [128, I], BF16) for i in range(2)]
    wtr = [sbW(f"wtr{i}", [128, IK * 128], BF16) for i in range(2)]

    s_wl = [sem("s_wl0"), sem("s_wl1")]
    s_wa = sem("s_wa")
    s_wv = sem("s_wv")
    s_wg2 = sem("s_wg2")
    s_wq = sem("s_wq")
    s_wt = sem("s_wt")
    s_we = sem("s_we")
    s_ws = [sem("s_ws0"), sem("s_ws1")]

    jobs = [("wg", i) for i in range(IK)] + [("wu", i) for i in range(IK)] + [("wd", i) for i in range(DK)]
    NJ = len(jobs)
    jw = lambda j: D if j[0] in ("wg", "wu") else I

    def stores_through(jn):
        tot = 0
        for j in jobs[:jn]:
            tot += DK if j[0] in ("wg", "wu") else 0
        return tot

    with nc.Block() as block, ExitStack() as esP:
        wt_ps = [[esP.enter_context(nc.psum_tensor(f"wtps{i}_{j}", [128, 1024], BF16)) for j in range(3)]
                 for i in range(2)]

        def emit_wstore(sync, pj):
            kind, idx = jobs[pj]
            sync.wait_ge(s_we, pj + 1)
            if kind in ("wg", "wu"):
                scr = wg_scr if kind == "wg" else wu_scr
                for hk in range(DK):
                    dst = bass.AP(scr, (idx * DK + hk) * 128 * 128, [[128, 128], [1, 128]])
                    sync.dma_start(out=dst, in_=wtr[pj % 2][:, hk * 128:(hk + 1) * 128]).then_inc(s_ws[pj % 2], 16)
                return DK
            for ik in range(IK):
                dst = bass.AP(wd_scr, ik * 128 * D + idx * 128, [[D, 128], [1, 128]])
                sync.dma_start(out=dst, in_=wtr[pj % 2][:, ik * 128:(ik + 1) * 128]).then_inc(s_ws[pj % 2], 16)
            return IK

        @block.sync
        def _(sync):
            tot = 0
            for jn, job in enumerate(jobs):
                kind, idx = job
                W = jw(job)
                if jn >= 2:
                    sync.wait_ge(s_wa, 2 * (jn - 1))
                src = {"wg": wg_d, "wu": wu_d, "wd": wd_d}[kind]
                sync.dma_start(out=wbuf[jn % 2][:, 0:W],
                               in_=bass.AP(src, idx * 128 * W, [[W, 128], [1, W]])).then_inc(s_wl[jn % 2], 16)
                if jn >= 1:
                    tot += emit_wstore(sync, jn - 1)
            tot += emit_wstore(sync, NJ - 1)
            tot0 = sum((DK if jobs[j][0] in ("wg", "wu") else IK) for j in range(0, NJ, 2))
            tot1 = sum((DK if jobs[j][0] in ("wg", "wu") else IK) for j in range(1, NJ, 2))
            sync.wait_ge(s_ws[0], 16 * tot0)
            sync.wait_ge(s_ws[1], 16 * tot1)
            sync.wait_ge(s_we, NJ)

        @block.scalar
        def _(scalar):
            for jn, job in enumerate(jobs):
                W = jw(job)
                scalar.wait_ge(s_wl[jn % 2], 16 * (jn // 2 + 1))
                if jn >= 1:
                    scalar.wait_ge(s_wq, jn)
                scalar.activation(wabs[:, 0:W], wbuf[jn % 2][:, 0:W], AF.Abs)
                scalar.activation(wsgb[:, 0:W], wbuf[jn % 2][:, 0:W], AF.Sign).then_inc(s_wa, 2)

        def n_stores_par(pj):
            tot = 0
            for j in range(pj % 2, pj - 1, 2):
                tot += DK if jobs[j][0] in ("wg", "wu") else IK
            return tot

        def emit_wevac(vector, pj):
            W = jw(jobs[pj])
            vector.wait_ge(s_wt, pj + 1)
            if pj >= 2:
                vector.wait_ge(s_ws[pj % 2], 16 * n_stores_par(pj))
            nblk = W // 128
            last = None
            for p in range((nblk + 7) // 8):
                w_p = min(1024, (nblk - 8 * p) * 128)
                last = vector.tensor_copy(wtr[pj % 2][:, p * 1024:p * 1024 + w_p],
                                          wt_ps[pj % 2][p][:, 0:w_p])
            vector.drain().then_inc(s_we, 1)

        @block.vector
        def _(vector):
            for jn, job in enumerate(jobs):
                W = jw(job)
                vector.wait_ge(s_wa, 2 * (jn + 1))
                emit_quant(vector, wabs, W, wsc, wrs4, wrs2, wsh, wo1, wo2)
                vector.drain().then_inc(s_wv, 1)
                vector.wait_ge(s_wg2, jn + 1)
                nb = W // 16
                shb = bass.AP(wsh, 0, [[I // 16, 128], [1, nb], [0, 16]])
                if jn >= 2:
                    vector.wait_ge(s_wt, jn - 1)
                vector.tensor_tensor(out=wqv[jn % 2][:, 0:W], in0=wo1[:, 0:W],
                                     in1=shb, op=OP.mult)
                vector.drain().then_inc(s_wq, 1)
                if jn >= 1:
                    emit_wevac(vector, jn - 1)
            emit_wevac(vector, NJ - 1)

        @block.gpsimd
        def _(gpsimd):
            for jn, job in enumerate(jobs):
                W = jw(job)
                gpsimd.wait_ge(s_wv, jn + 1)
                gpsimd.tensor_tensor(out=wo1[:, 0:W], in0=wo1[:, 0:W],
                                     in1=wsgb[:, 0:W], op=OP.mult).then_inc(s_wg2, 1)

        @block.tensor
        def _(tensor):
            for jn, job in enumerate(jobs):
                W = jw(job)
                tensor.wait_ge(s_wq, jn + 1)
                if jn >= 2:
                    tensor.wait_ge(s_we, jn - 1)
                nblk = W // 128
                last = None
                for b in range(nblk):
                    last = tensor.transpose(wt_ps[jn % 2][b // 8][:, (b % 8) * 128:(b % 8 + 1) * 128],
                                            wqv[jn % 2][:, b * 128:(b + 1) * 128], idb[:])
                last.then_inc(s_wt, 1)

    if "W" in debug:
        s_dW = sem("s_dW")
        with nc.Block() as block:
            @block.gpsimd
            def _(gpsimd):
                n = 0
                for ik in range(IK):
                    for hk in range(DK):
                        gpsimd.dma_start(
                            out=wbuf[0][:, 0:128],
                            in_=bass.AP(wg_scr, (ik * DK + hk) * 128 * 128, [[128, 128], [1, 128]])
                        ).then_inc(s_dW, 16)
                        n += 1
                        gpsimd.wait_ge(s_dW, 32 * n - 16)
                        gpsimd.dma_start(
                            out=bass.AP(dbg["wgq"], (ik * DK + hk) * 128 * 128, [[128, 128], [1, 128]]),
                            in_=wbuf[0][:, 0:128]).then_inc(s_dW, 16)
                        gpsimd.wait_ge(s_dW, 32 * n)
                for ik in range(IK):
                    gpsimd.dma_start(out=wbuf[0][:, 0:D],
                                     in_=bass.AP(wd_scr, ik * 128 * D, [[D, 128], [1, D]])).then_inc(s_dW, 16)
                    n += 1
                    gpsimd.wait_ge(s_dW, 32 * n - 16)
                    gpsimd.dma_start(out=bass.AP(dbg["wdq"], ik * 128 * D, [[D, 128], [1, D]]),
                                     in_=wbuf[0][:, 0:D]).then_inc(s_dW, 16)
                    gpsimd.wait_ge(s_dW, 32 * n)

    esW.close()
    if stage == "W":
        es.close()
        return nc, dbg

    esB = ExitStack()
    sbB = lambda name, shape, dt: esB.enter_context(nc.sbuf_tensor(name, shape, dt))

    wvT = sbB("wvT", [128, DK, D], BF16)
    xTt = [sbB(f"xTt{i}", [128, DK, TILE], BF16) for i in range(2)]
    xres = [sbB(f"xres{i}", [128, D], F32) for i in range(2)]
    Vf = sbB("Vf", [128, DK, TILE], BF16)
    y1t = sbB("y1t", [128, CSUB, D], F32)
    s1f = sbB("s1f", [128, D], F32)
    yab = sbB("yab", [128, D], F32)
    ysg = sbB("ysg", [128, D], BF16)
    yo1 = sbB("yo1", [128, D], BF16)
    yo2 = sbB("yo2", [128, D], BF16)
    yxq = sbB("yxq", [128, D], BF16)
    ysc = sbB("ysc", [128, D // 16], F32)
    yrs4 = sbB("yrs4", [128, D // 16], F32)
    yrs2 = sbB("yrs2", [128, D // 16], F32)
    ysh = sbB("ysh", [128, D // 16], F32)
    xqT = sbB("xqT", [128, DK, TILE], BF16)
    wgs = [sbB(f"wgs{i}", [128, 4 * DK * 128], BF16) for i in range(2)]
    wus = [sbB(f"wus{i}", [128, 4 * DK * 128], BF16) for i in range(2)]
    hst = sbB("hst", [128, 512], F32)
    hab = sbB("hab", [128, 512], F32)
    hsgb = sbB("hsgb", [128, 512], BF16)
    ho1 = sbB("ho1", [128, 512], BF16)
    ho2 = sbB("ho2", [128, 512], BF16)
    hqv = sbB("hqv", [128, 512], BF16)
    hsc = sbB("hsc", [128, 32], F32)
    hrs4 = sbB("hrs4", [128, 32], F32)
    hrs2 = sbB("hrs2", [128, 32], F32)
    hsh = sbB("hsh", [128, 32], F32)
    hqT = sbB("hqT", [128, IK, TILE], BF16)
    wds = [sbB(f"wds{i}", [128, D], BF16) for i in range(4)]
    sil = sbB("sil", [128, 512], F32)
    y2b = [sbB(f"y2b{i}", [128, D], F32) for i in range(2)]
    bnsq = sbB("bnsq", [128, D], F32)
    bred = sbB("bred", [128, 128], F32)
    bred2 = sbB("bred2", [128, 128], F32)
    if "C" in debug:
        bnsq_t = sbB("bnsq_t", [128, D], F32)
        bred2_t = sbB("bred2_t", [128, 128], F32)
        scs_t = sbB("scs_t", [128, 4], F32)
        s1a_t = sbB("s1a_t", [128, D], F32)
        bnag_t = sbB("bnag_t", [128, 2], F32)
        xr_t = sbB("xr_t", [128, D], F32)
    bnag = sbB("bnag", [128, 2], F32)
    bns2 = sbB("bns2", [128, 1], F32)
    bnr1 = sbB("bnr1", [128, 1], F32)
    bnr2 = sbB("bnr2", [128, 1], F32)
    epsc = sbB("epsc", [128, 1], F32)

    s_bw0 = sem("s_bw0")
    s_bx = [sem("s_bx0"), sem("s_bx1")]
    s_br = [sem("s_br0"), sem("s_br1")]
    s_bv = sem("s_bv")
    s_bve = sem("s_bve")
    s_ba = sem("s_ba")
    s_be = sem("s_be")
    s_bse = sem("s_bse")
    s_bcm = sem("s_bcm")
    s_bgp = sem("s_bgp")
    s_bq = sem("s_bq")
    s2_wg = [sem("s2_wg0"), sem("s2_wg1")]
    s2_g = sem("s2_g")
    s2_sil = sem("s2_sil")
    s2_h = sem("s2_h")
    s2_hse = sem("s2_hse")
    s2_hcm = sem("s2_hcm")
    s2_hgp = sem("s2_hgp")
    s2_t = sem("s2_t")
    s2_te = sem("s2_te")
    s2_d = sem("s2_d")
    s2_dk = sem("s2_dk")
    s2_wd = [sem(f"s2_wd{i}") for i in range(4)]
    s2_l2 = sem("s2_l2")
    s2_lse = sem("s2_lse")
    s2_o = sem("s2_o")
    s2_dbg = sem("s2_dbg")
    s2_dbg2 = sem("s2_dbg2")

    with nc.Block() as block:
        @block.gpsimd
        def _(gpsimd):
            gpsimd.dma_start(out=wvT[:], in_=wvT_d.ap()).then_inc(s_bw0, 16)
            src = bass.AP(x_feat, 0, [[C, 128], [128 * C, DK], [1, TILE]])
            gpsimd.dma_start(out=xTt[0][:], in_=src).then_inc(s_bx[0], 16)

        @block.vector
        def _(vector):
            vector.memset(epsc[:], EPS_LN)
            vector.drain()

        @block.sync
        def _(sync):
            sync.wait_ge(s_bw0, 16)
            sync.wait_ge(s_bx[0], 16)

    for t in range(NTILES):
        b_bv, b_bve, b_ba = 8 * t, 8 * t, CSUB * t
        b_be, b_bse, b_bcm, b_bgp, b_bq = 12 * t, 12 * t, 4 * t, 4 * t, 8 * t
        with nc.Block() as block, ExitStack() as esP:
            pp = lambda name, shape, dt: esP.enter_context(nc.psum_tensor(name, shape, dt))
            vps = [pp(f"vps{t}_{i}", [128, TILE], F32) for i in range(2)]
            aps = [pp(f"aps{t}_{i}", [128, D], F32) for i in range(2)]
            tps = [pp(f"tps{t}_{i}", [128, 1024], BF16) for i in range(2)]

            @block.gpsimd
            def _(gpsimd):
                if t + 1 < NTILES:
                    src = bass.AP(x_feat, (t + 1) * TILE, [[C, 128], [128 * C, DK], [1, TILE]])
                    gpsimd.dma_start(out=xTt[(t + 1) % 2][:], in_=src).then_inc(s_bx[(t + 1) % 2], 16)
                for cs in range(CSUB):
                    gpsimd.wait_ge(s_bcm, b_bcm + cs + 1)
                    gpsimd.wait_ge(s_bse, b_bse + 3 * cs + 3)
                    gpsimd.tensor_tensor(out=yo1[:], in0=yo1[:], in1=ysg[:], op=OP.mult).then_inc(s_bgp, 1)

            @block.sync
            def _(sync):
                for cs in range(CSUB):
                    if cs >= 2:
                        sync.wait_ge(s_be, b_be + 3 * (cs - 2) + 1)
                    r0 = t * TILE + cs * 128
                    sync.dma_start(out=xres[cs % 2][:],
                                   in_=bass.AP(x_tok, r0 * D, [[D, 128], [1, D]])).then_inc(s_br[cs % 2], 16)
                i0, W = STRIPS[0]
                nik = W // 128
                sync.dma_start(out=wgs[0][:, 0:nik * DK * 128],
                               in_=bass.AP(wg_scr, 0, [[128, 128], [128 * 128, nik * DK], [1, 128]])
                               ).then_inc(s2_wg[0], 16)
                sync.dma_start(out=wus[0][:, 0:nik * DK * 128],
                               in_=bass.AP(wu_scr, 0, [[128, 128], [128 * 128, nik * DK], [1, 128]])
                               ).then_inc(s2_wg[0], 16)

            @block.tensor
            def _(tensor):
                tensor.wait_ge(s_bx[t % 2], 16 * (t // 2 + 1))
                for mk in range(DK):
                    if mk >= 2:
                        tensor.wait_ge(s_bve, b_bve + mk - 1)
                    for dk in range(DK):
                        mm = tensor.matmul(vps[mk % 2][:], wvT[:, dk, mk * 128:(mk + 1) * 128],
                                           xTt[t % 2][:, dk, :], start=(dk == 0), stop=(dk == DK - 1))
                        if dk == DK - 1:
                            mm.then_inc(s_bv, 1)
                tensor.wait_ge(s_bve, b_bve + 8)

                def emit_xq_transpose(j):
                    tensor.wait_ge(s_be, b_be + 3 * j + 3)
                    if j >= 2:
                        tensor.wait_ge(s_bq, b_bq + 2 * (j - 1))
                    last = None
                    for dk in range(DK):
                        last = tensor.transpose(tps[j % 2][:, dk * 128:(dk + 1) * 128],
                                                yxq[:, dk * 128:(dk + 1) * 128], idb[:])
                    last.then_inc(s_bq, 1)

                for cs in range(CSUB):
                    if cs >= 2:
                        tensor.wait_ge(s_be, b_be + 3 * (cs - 1))
                    for mk in range(DK):
                        for nh in range(2):
                            mm = tensor.matmul(aps[cs % 2][:, nh * 512:(nh + 1) * 512],
                                               Vf[:, mk, cs * 128:(cs + 1) * 128],
                                               G[:, mk, nh * 512:(nh + 1) * 512],
                                               start=(mk == 0), stop=(mk == DK - 1))
                            if mk == DK - 1 and nh == 1:
                                mm.then_inc(s_ba, 1)
                    if cs >= 1:
                        emit_xq_transpose(cs - 1)
                emit_xq_transpose(CSUB - 1)

            @block.scalar
            def _(scalar):
                for mk in range(DK):
                    scalar.wait_ge(s_bv, b_bv + mk + 1)
                    scalar.activation(Vf[:, mk, :], vps[mk % 2][:], AF.Identity,
                                      bias=bvc[:, mk:mk + 1]).then_inc(s_bve, 1)
                for cs in range(CSUB):
                    scalar.wait_ge(s_be, b_be + 3 * cs + 1)
                    scalar.activation(bnr1[:], bnag[:, 1:2], AF.Ln, bias=epsc[:]).then_inc(s_bse, 1)
                    scalar.wait_ge(s_bse, b_bse + 3 * cs + 1)
                    scalar.activation(bnr2[:], bnr1[:], AF.Exp, scale=-0.5).then_inc(s_bse, 1)
                    if "C" in dbg_flags and t == NTILES - 1 and cs == 3:
                        scalar.activation(scs_t[:, 0:1], bnag[:, 1:2], AF.Identity)
                        scalar.activation(scs_t[:, 1:2], bnr1[:], AF.Identity)
                        scalar.activation(scs_t[:, 2:3], bnr2[:], AF.Identity)
                        scalar.activation(scs_t[:, 3:4], epsc[:], AF.Identity)
                    scalar.wait_ge(s_be, b_be + 3 * cs + 2)
                    scalar.activation(yab[:], y1t[:, cs, :], AF.Abs)
                    scalar.activation(ysg[:], y1t[:, cs, :], AF.Sign).then_inc(s_bse, 1)

            @block.vector
            def _(vector):
                for cs in range(CSUB):
                    vector.wait_ge(s_ba, b_ba + cs + 1)
                    vector.wait_ge(s_br[cs % 2], 16 * (2 * t + cs // 2 + 1))
                    vector.scalar_tensor_tensor(out=s1f[:], in0=xres[cs % 2][:], scalar=2.0,
                                                in1=aps[cs % 2][:], op0=OP.mult, op1=OP.add)
                    if "C" in dbg_flags and t == NTILES - 1 and cs == 3:
                        vector.tensor_copy(s1a_t[:], s1f[:])
                        vector.tensor_copy(xr_t[:], xres[cs % 2][:])
                    emit_bn(vector, s1f, bnsq, bnag, bns2, bred, bred2)
                    vector.drain().then_inc(s_be, 1)
                    if "C" in dbg_flags and t == NTILES - 1 and cs == 3:
                        vector.tensor_copy(bnag_t[:], bnag[:])
                        vector.tensor_copy(bnsq_t[:], bnsq[:])
                        vector.tensor_copy(bred2_t[:, 0:64], bred2[:, 0:64])
                        vector.tensor_copy(bred2_t[:, 64:128], bred[:, 0:64])
                    vector.wait_ge(s_bse, b_bse + 3 * cs + 2)
                    vector.tensor_scalar(out=y1t[:, cs, :], in0=s1f[:],
                                         scalar1=bnag[0:128, 0:1], scalar2=bnr2[0:128, 0:1],
                                         op0=OP.subtract, op1=OP.mult)
                    vector.drain().then_inc(s_be, 1)
                    vector.wait_ge(s_bse, b_bse + 3 * cs + 3)
                    emit_quant(vector, yab, D, ysc, yrs4, yrs2, ysh, yo1, yo2)
                    vector.drain().then_inc(s_bcm, 1)
                    vector.wait_ge(s_bgp, b_bgp + cs + 1)
                    shb = bass.AP(ysh, 0, [[D // 16, 128], [1, D // 16], [0, 16]])
                    vector.tensor_tensor(out=yxq[:], in0=yo1[:], in1=shb, op=OP.mult)
                    vector.drain().then_inc(s_be, 1)
                    vector.wait_ge(s_bq, b_bq + 2 * cs + 1)
                    dst = bass.AP(xqT, cs * 128, [[DK * TILE, 128], [TILE, DK], [1, 128]])
                    vector.tensor_copy(dst, tps[cs % 2][:])
                    vector.drain().then_inc(s_bq, 1)

        b2_wg = 32 * (NST * t + 1)
        b_g, b_sil, b_h = 2 * NST * CSUB * t, NST * CSUB * t, 2 * NST * CSUB * t
        b_hse, b_hcm, b_hgp = 2 * NST * CSUB * t, NST * CSUB * t, NST * CSUB * t
        b_t2, b_te = NST * CSUB * t, NST * CSUB * t
        b_d, b_l2, b_lse, b_o = CSUB * t, 2 * CSUB * t, 2 * CSUB * t, 64 * t

        with nc.Block() as block, ExitStack() as esP:
            pp = lambda name, shape, dt: esP.enter_context(nc.psum_tensor(name, shape, dt))
            gps = [pp(f"bgps{t}_{i}", [128, 512], F32) for i in range(4)]
            hps = [pp(f"bhps{t}_{i}", [128, 512], BF16) for i in range(2)]

            @block.sync
            def _(sync):
                for sn in range(1, NST):
                    i0, W = STRIPS[sn]
                    nik = W // 128
                    ik0 = i0 // 128
                    if sn >= 2:
                        sync.wait_ge(s2_g, b_g + 2 * CSUB * (sn - 1))
                    sync.dma_start(out=wgs[sn % 2][:, 0:nik * DK * 128],
                                   in_=bass.AP(wg_scr, ik0 * DK * 128 * 128,
                                               [[128, 128], [128 * 128, nik * DK], [1, 128]])
                                   ).then_inc(s2_wg[sn % 2], 16)
                    sync.dma_start(out=wus[sn % 2][:, 0:nik * DK * 128],
                                   in_=bass.AP(wu_scr, ik0 * DK * 128 * 128,
                                               [[128, 128], [128 * 128, nik * DK], [1, 128]])
                                   ).then_inc(s2_wg[sn % 2], 16)

            @block.tensor
            def _(tensor):
                for sn in range(NST):
                    i0, W = STRIPS[sn]
                    nik = W // 128
                    tensor.wait_ge(s2_wg[sn % 2], 32 * (3 * t + sn // 2 + 1))
                    for cs in range(CSUB):
                        gu = sn * CSUB + cs
                        if gu >= 2:
                            tensor.wait_ge(s2_h, b_h + 2 * (gu - 1))
                        for pi, wbufs in enumerate((wgs, wus)):
                            last = None
                            for hk in range(DK):
                                rhs = bass.AP(wbufs[sn % 2], hk * 128,
                                              [[4 * DK * 128, 128], [DK * 128, nik], [1, 128]])
                                last = tensor.matmul(gps[2 * (gu % 2) + pi][:, 0:W],
                                                     xqT[:, hk, cs * 128:(cs + 1) * 128], rhs,
                                                     start=(hk == 0), stop=(hk == DK - 1))
                            last.then_inc(s2_g, 1)
                        tensor.wait_ge(s2_h, b_h + 2 * gu + 2)
                        if gu >= 2:
                            tensor.wait_ge(s2_te, b_te + gu - 1)
                        lastt = None
                        for b_ in range(nik):
                            lastt = tensor.transpose(hps[gu % 2][:, b_ * 128:(b_ + 1) * 128],
                                                     hqv[:, b_ * 128:(b_ + 1) * 128], idb[:])
                        lastt.then_inc(s2_t, 1)

            @block.scalar
            def _(scalar):
                for sn in range(NST):
                    i0, W = STRIPS[sn]
                    for cs in range(CSUB):
                        gu = sn * CSUB + cs
                        scalar.wait_ge(s2_g, b_g + 2 * gu + 1)
                        if gu >= 1:
                            scalar.wait_ge(s2_h, b_h + 2 * gu - 1)
                        scalar.activation(sil[:, 0:W], gps[2 * (gu % 2)][:, 0:W], AF.Silu).then_inc(s2_sil, 1)
                        scalar.wait_ge(s2_h, b_h + 2 * gu + 1)
                        scalar.activation(hab[:, 0:W], hst[:, 0:W], AF.Abs)
                        scalar.activation(hsgb[:, 0:W], hst[:, 0:W], AF.Sign).then_inc(s2_hse, 2)

            @block.vector
            def _(vector):
                for sn in range(NST):
                    i0, W = STRIPS[sn]
                    for cs in range(CSUB):
                        gu = sn * CSUB + cs
                        vector.wait_ge(s2_sil, b_sil + gu + 1)
                        vector.wait_ge(s2_g, b_g + 2 * gu + 2)
                        if gu >= 1:
                            vector.wait_ge(s2_hse, b_hse + 2 * gu)
                        vector.tensor_tensor(out=hst[:, 0:W], in0=sil[:, 0:W],
                                             in1=gps[2 * (gu % 2) + 1][:, 0:W], op=OP.mult)
                        vector.drain().then_inc(s2_h, 1)
                        vector.wait_ge(s2_hse, b_hse + 2 * gu + 2)
                        emit_quant(vector, hab, W, hsc, hrs4, hrs2, hsh, ho1, ho2)
                        vector.drain().then_inc(s2_hcm, 1)
                        vector.wait_ge(s2_hgp, b_hgp + gu + 1)
                        nb = W // 16
                        shb = bass.AP(hsh, 0, [[32, 128], [1, nb], [0, 16]])
                        vector.tensor_tensor(out=hqv[:, 0:W], in0=ho1[:, 0:W],
                                             in1=shb, op=OP.mult)
                        vector.drain().then_inc(s2_h, 1)
                        vector.wait_ge(s2_t, b_t2 + gu + 1)
                        nik = W // 128
                        ik0 = i0 // 128
                        dst = bass.AP(hqT, ik0 * TILE + cs * 128, [[IK * TILE, 128], [TILE, nik], [1, 128]])
                        vector.tensor_copy(dst, hps[gu % 2][:, 0:W])
                        vector.drain().then_inc(s2_te, 1)

            @block.gpsimd
            def _(gpsimd):
                for sn in range(NST):
                    i0, W = STRIPS[sn]
                    for cs in range(CSUB):
                        gu = sn * CSUB + cs
                        gpsimd.wait_ge(s2_hcm, b_hcm + gu + 1)
                        gpsimd.wait_ge(s2_hse, b_hse + 2 * gu + 2)
                        gpsimd.tensor_tensor(out=ho1[:, 0:W], in0=ho1[:, 0:W],
                                             in1=hsgb[:, 0:W], op=OP.mult).then_inc(s2_hgp, 1)

        wd_slot_cnt = [6, 6, 5, 5]
        with nc.Block() as block, ExitStack() as esP:
            pp = lambda name, shape, dt: esP.enter_context(nc.psum_tensor(name, shape, dt))
            dps = [pp(f"bdps{t}_{i}", [128, D], F32) for i in range(4)]

            @block.sync
            def _(sync):
                for ik in range(IK):
                    g = IK * t + ik
                    if g >= 4:
                        sync.wait_ge(s2_dk, g - 3)
                    sync.dma_start(out=wds[ik % 4][:],
                                   in_=bass.AP(wd_scr, ik * 128 * D, [[D, 128], [1, D]])
                                   ).then_inc(s2_wd[ik % 4], 16)
                for cs in range(CSUB):
                    sync.wait_ge(s2_l2, b_l2 + 2 * cs + 2)
                    r0 = t * TILE + cs * 128
                    sync.dma_start(out=bass.AP(y_d, r0 * D, [[D, 128], [1, D]]),
                                   in_=y2b[cs % 2][:]).then_inc(s2_o, 16)
                if t == NTILES - 1:
                    sync.wait_ge(s2_o, 16 * 4 * NTILES)

            @block.tensor
            def _(tensor):
                for ik in range(IK):
                    slot = ik % 4
                    tensor.wait_ge(s2_wd[slot], 16 * (wd_slot_cnt[slot] * t + ik // 4 + 1))
                    last = None
                    for cs in range(CSUB):
                        for nh in range(2):
                            last = tensor.matmul(dps[cs][:, nh * 512:(nh + 1) * 512],
                                                 hqT[:, ik, cs * 128:(cs + 1) * 128],
                                                 wds[slot][:, nh * 512:(nh + 1) * 512],
                                                 start=(ik == 0), stop=(ik == IK - 1))
                    last.then_inc(s2_dk, 1)

            @block.scalar
            def _(scalar):
                for cs in range(CSUB):
                    scalar.wait_ge(s2_l2, b_l2 + 2 * cs + 1)
                    scalar.activation(bnr1[:], bnag[:, 1:2], AF.Ln, bias=epsc[:]).then_inc(s2_lse, 1)
                    scalar.wait_ge(s2_lse, b_lse + 2 * cs + 1)
                    scalar.activation(bnr2[:], bnr1[:], AF.Exp, scale=-0.5).then_inc(s2_lse, 1)

            @block.vector
            def _(vector):
                for cs in range(CSUB):
                    vector.wait_ge(s2_dk, IK * (t + 1))
                    if cs >= 2:
                        vector.wait_ge(s2_o, b_o + 16 * (cs - 1))
                    for nh in range(2):
                        sl = slice(nh * 512, (nh + 1) * 512)
                        vector.tensor_tensor(out=s1f[:, sl], in0=y1t[:, cs, sl], in1=dps[cs][:, sl], op=OP.add)
                    emit_bn(vector, s1f, bnsq, bnag, bns2, bred, bred2)
                    vector.drain().then_inc(s2_l2, 1)
                    vector.wait_ge(s2_lse, b_lse + 2 * cs + 2)
                    vector.tensor_scalar(out=y2b[cs % 2][:], in0=s1f[:],
                                         scalar1=bnag[0:128, 0:1], scalar2=bnr2[0:128, 0:1],
                                         op0=OP.subtract, op1=OP.mult)
                    vector.drain().then_inc(s2_l2, 1)

        if "B" in debug and t == NTILES - 1:
            with nc.Block() as block:
                @block.sync
                def _(sync):
                    sync.wait_ge(s2_o, 16 * 4 * NTILES)
                    for cs in range(CSUB):
                        r0 = cs * 128
                        sync.dma_start(out=bass.AP(dbg["y1"], r0 * D, [[D, 128], [1, D]]),
                                       in_=y1t[:, cs, :]).then_inc(s2_dbg, 16)
                    sync.dma_start(out=bass.AP(dbg["zf"], 0, [[D, 128], [1, D]]),
                                   in_=s1f[:]).then_inc(s2_dbg, 16)
                    sync.wait_ge(s2_dbg, 16 * 5)

                @block.gpsimd
                def _(gpsimd):
                    gpsimd.wait_ge(s2_o, 16 * 4 * NTILES)
                    gpsimd.dma_start(out=bass.AP(dbg["xqt"], 0,
                                                 [[DK * TILE, 128], [1, DK * TILE]]),
                                     in_=xqT[:]).then_inc(s2_dbg2, 16)
                    gpsimd.dma_start(out=bass.AP(dbg["hqt"], 0,
                                                 [[IK * TILE, 128], [1, IK * TILE]]),
                                     in_=hqT[:]).then_inc(s2_dbg2, 16)
                    gpsimd.wait_ge(s2_dbg2, 16 * 2)

    if "C" in debug:
        s2_dbg3 = sem("s2_dbg3")
        with nc.Block() as block:
            @block.gpsimd
            def _(gpsimd):
                gpsimd.wait_ge(s2_o, 16 * 4 * NTILES)
                gpsimd.dma_start(out=dbg["bnsqd"].ap(), in_=bnsq_t[:]).then_inc(s2_dbg3, 16)
                gpsimd.dma_start(out=dbg["bred2d"].ap(), in_=bred2_t[:]).then_inc(s2_dbg3, 16)
                gpsimd.dma_start(out=dbg["scs"].ap(), in_=scs_t[:]).then_inc(s2_dbg3, 16)
                gpsimd.dma_start(out=dbg["s1a"].ap(), in_=s1a_t[:]).then_inc(s2_dbg3, 16)
                gpsimd.dma_start(out=dbg["bnagd"].ap(), in_=bnag_t[:]).then_inc(s2_dbg3, 16)
                gpsimd.dma_start(out=dbg["xrd"].ap(), in_=xr_t[:]).then_inc(s2_dbg3, 16)
                gpsimd.dma_start(out=bass.AP(dbg["vfd"], 0, [[DK * TILE, 128], [1, DK * TILE]]),
                                 in_=Vf[:]).then_inc(s2_dbg3, 16)
                gpsimd.wait_ge(s2_dbg3, 16 * 7)

    esB.close()
    es.close()
    return nc, dbg


_PROG_CACHE = {}


def _build():
    if "nc" not in _PROG_CACHE:
        nc, _ = build_core_program(stage="ALL", debug=())
        mybir.codegen_inst_isa_subclasses(nc)
        _PROG_CACHE["nc"] = nc
    return _PROG_CACHE["nc"]


def _wT_layout(W):
    Dm = W.shape[0]
    return np.ascontiguousarray(W.T.reshape(Dm // 128, 128, Dm).transpose(1, 0, 2).reshape(Dm, Dm))


def kernel(L0_attn_x, L0_attn_Wq, L0_attn_bq, L0_attn_Wk, L0_attn_bk,
           L0_attn_Wv, L0_attn_bv, L0_attn_gamma,
           L0_ffn_w_gate, L0_ffn_w_up, L0_ffn_w_down):
    x = np.ascontiguousarray(np.asarray(L0_attn_x, dtype=np.float32))
    B = x.shape[0]
    assert x.shape == (B, C, D)
    wqT = _wT_layout(np.asarray(L0_attn_Wq, dtype=np.float32))
    wkT = _wT_layout(np.asarray(L0_attn_Wk, dtype=np.float32))
    wvT = _wT_layout(np.asarray(L0_attn_Wv, dtype=np.float32))
    bq = np.ascontiguousarray(np.asarray(L0_attn_bq, dtype=np.float32).reshape(1, D))
    bk = np.ascontiguousarray(np.asarray(L0_attn_bk, dtype=np.float32).reshape(1, D))
    bvc = np.ascontiguousarray(np.asarray(L0_attn_bv, dtype=np.float32).reshape(DK, 128).T)
    gam = np.ascontiguousarray(np.asarray(L0_attn_gamma, dtype=np.float32).reshape(1, 1))
    wg = np.ascontiguousarray(np.asarray(L0_ffn_w_gate, dtype=np.float32))
    wu = np.ascontiguousarray(np.asarray(L0_ffn_w_up, dtype=np.float32))
    wd = np.ascontiguousarray(np.asarray(L0_ffn_w_down, dtype=np.float32))
    idb = np.eye(128, dtype=ml_dtypes.bfloat16)
    idf = np.ones((1, 1), dtype=np.float32)
    onec = np.ones((128, 1), dtype=ml_dtypes.bfloat16)

    nc = _build()
    in_maps = []
    for b in range(B):
        in_maps.append({
            "x_tok": x[b],
            "x_feat": np.ascontiguousarray(x[b].T),
            "wqT_hbm": wqT, "wkT_hbm": wkT, "wvT_hbm": wvT,
            "bq": bq, "bk": bk, "bvc_hbm": bvc, "gamma": gam,
            "wg": wg, "wu": wu, "wd": wd,
            "idb_hbm": idb, "idf_hbm": idf, "onec_hbm": onec,
        })
    res = run_bass_kernel_spmd(nc, in_maps, core_ids=list(range(B)))
    out = np.stack([np.asarray(res.results[b]["y"], dtype=np.float32) for b in range(B)], axis=0)
    return out


# version 11
# speedup vs baseline: 1.0663x; 1.0388x over previous
import os
import numpy as np
import ml_dtypes
from contextlib import ExitStack

import concourse.bass as bass
import concourse.mybir as mybir
from concourse.bass_utils import run_bass_kernel_spmd

F32 = mybir.dt.float32
BF16 = mybir.dt.bfloat16
AF = mybir.ActivationFunctionType
OP = mybir.AluOpType

import concourse.dve_ops as dve_ops
from concourse.dve_op import DveOp
from concourse.dve_spec import Spec, Src0, Src1, C0, C1, One, lower as dve_lower


def _register_op(name, spec_body, reference):
    ver = "v3"
    spec = Spec(body=spec_body)
    compiled = dve_lower(spec, ver=ver, perf_en=False, name=name)
    op = DveOp(name, spec, subdim=False, uops_sha={ver: compiled.sha(ver)}, reference=reference)
    if name not in dve_ops._SUB_OPCODE_FOR_NAME:
        dve_ops.OPS.append(op)
        row = dve_ops._CUSTOM_DVE_ROW_BASE + len(dve_ops.OPS) - 1
        assert row < 0x20
        dve_ops._SUB_OPCODE_FOR_NAME[name] = row
        dve_ops.CUSTOM_DVE_SPECS[name] = spec
        dve_ops.CUSTOM_DVE_REFS[name] = reference
    return next(o for o in dve_ops.OPS if o.name == name)


_m1 = Src0 * Src1
QLAD1 = _register_op(
    "AQL1",
    (((_m1 > One) + (_m1 > (C1 - C0))) + (_m1 > C0)) + (_m1 > (C1 - One)),
    lambda in0, in1, s0, s1, imm2: (
        ((in0 * in1 > 1.0) * 1.0 + (in0 * in1 > (s1 - s0)) * 1.0)
        + ((in0 * in1 > s0) * 1.0 + (in0 * in1 > (s1 - 1.0)) * 1.0)
    ),
)
_m2 = Src0 * Src1
_q7 = _m2 > (C0 + C0)
_t2 = ((_m2 > C0) + (_m2 > C1)) + (_q7 + _q7)
QLAD2 = _register_op(
    "AQL2",
    _t2 + _t2,
    lambda in0, in1, s0, s1, imm2: 2.0 * (
        ((in0 * in1 > s0) * 1.0 + (in0 * in1 > s1) * 1.0)
        + 2.0 * (in0 * in1 > (s0 + s0)) * 1.0
    ),
)

C = 4096
D = 1024
I = 2816
NCC = C // 128
DK = D // 128
IK = I // 128
TILE = 512
NTILES = C // TILE
CSUB = TILE // 128
STRIPS = [(0, 512), (512, 512), (1024, 512), (1536, 512), (2048, 512), (2560, 256)]
NST = len(STRIPS)
EPS_ATTN = 1e-6
EPS_LN = 1e-5


class VFence:
    def __init__(self):
        self.sem = None
        self.n = 0

    def bind(self, sem):
        self.sem = sem

    def __call__(self, eng, inst):
        return inst


VF = VFence()


def emit_quant(vector, src_abs, w, sc, rs4, rs2, sh, o1, o2):
    nb = w // 16
    nbt = sc.shape[1]
    a3 = src_abs[:, 0:w].rearrange("p (b k) -> p b k", k=16)
    vector.tensor_reduce(out=sc[:, 0:nb], in_=a3, axis=mybir.AxisListType.X,
                         op=OP.max, apply_absolute_value=False)
    vector.drain()
    vector.reciprocal(rs2[:, 0:nb], sc[:, 0:nb])
    vector.drain()
    vector.tensor_scalar(out=rs4[:, 0:nb], in0=rs2[:, 0:nb],
                         scalar1=1.6666666666666666e11, scalar2=24.0, op0=OP.min, op1=OP.mult)
    vector.tensor_scalar(out=sh[:, 0:nb], in0=sc[:, 0:nb],
                         scalar1=6e-12, scalar2=1.0 / 12.0, op0=OP.max, op1=OP.mult)
    vector.tensor_scalar(out=rs2[:, 0:nb], in0=rs2[:, 0:nb],
                         scalar1=1.6666666666666666e11, scalar2=12.0, op0=OP.min, op1=OP.mult)
    rs4b = bass.AP(rs4, 0, [[nbt, 128], [1, nb], [0, 16]])
    rs2b = bass.AP(rs2, 0, [[nbt, 128], [1, nb], [0, 16]])
    vector._custom_dve(QLAD1, out=o1[:, 0:w], in0=src_abs[:, 0:w], in1=rs4b, s0=5.0, s1=8.0)
    vector._custom_dve(QLAD2, out=o2[:, 0:w], in0=src_abs[:, 0:w], in1=rs2b, s0=5.0, s1=7.0)
    return vector.tensor_tensor(out=o1[:, 0:w], in0=o1[:, 0:w], in1=o2[:, 0:w], op=OP.add)


def emit_bn(vector, src, bnsq, bnag, bns2, bred, bred2):
    a3 = src[:, 0:1024].rearrange("p (b k) -> p b k", k=16)
    q3 = bnsq[:, 0:1024].rearrange("p (b k) -> p b k", k=16)
    vector.tensor_tensor(out=bnsq[:], in0=src[:], in1=src[:], op=OP.mult)
    vector.drain()
    vector.tensor_reduce(out=bred2[:, 0:64], in_=q3, axis=mybir.AxisListType.X, op=OP.add)
    vector.drain()
    vector.tensor_reduce(out=bred[:, 0:64], in_=a3, axis=mybir.AxisListType.X, op=OP.add)
    vector.drain()
    w = 32
    while w >= 1:
        vector.tensor_tensor(out=bred[:, 0:w], in0=bred[:, 0:w], in1=bred[:, w:2 * w], op=OP.add)
        vector.tensor_tensor(out=bred2[:, 0:w], in0=bred2[:, 0:w],
                             in1=bred2[:, w:2 * w], op=OP.add)
        w //= 2
    vector.tensor_scalar(out=bnag[:, 0:1], in0=bred[:, 0:1], scalar1=1.0 / 1024.0, scalar2=None, op0=OP.mult)
    vector.tensor_scalar(out=bnag[:, 1:2], in0=bred2[:, 0:1], scalar1=1.0 / 1024.0, scalar2=None, op0=OP.mult)
    vector.tensor_tensor(out=bns2[:, 0:1], in0=bnag[:, 0:1], in1=bnag[:, 0:1], op=OP.mult)
    vector.drain()
    return vector.tensor_tensor(out=bnag[:, 1:2], in0=bnag[:, 1:2], in1=bns2[:, 0:1], op=OP.subtract)


def build_core_program(stage="ALL", debug=()):
    dbg_flags = debug
    nc = bass.Bass("TRN2", target_bir_lowering=False, detect_race_conditions=False)

    x_tok = nc.dram_tensor("x_tok", [C, D], F32, kind="ExternalInput")
    x_feat = nc.dram_tensor("x_feat", [D, C], F32, kind="ExternalInput")
    wqT_d = nc.dram_tensor("wqT_hbm", [D, D], F32, kind="ExternalInput")
    wkT_d = nc.dram_tensor("wkT_hbm", [D, D], F32, kind="ExternalInput")
    wvT_d = nc.dram_tensor("wvT_hbm", [D, D], F32, kind="ExternalInput")
    bq_d = nc.dram_tensor("bq", [1, D], F32, kind="ExternalInput")
    bk_d = nc.dram_tensor("bk", [1, D], F32, kind="ExternalInput")
    bvc_d = nc.dram_tensor("bvc_hbm", [128, DK], F32, kind="ExternalInput")
    gamma_d = nc.dram_tensor("gamma", [1, 1], F32, kind="ExternalInput")
    wg_d = nc.dram_tensor("wg", [I, D], F32, kind="ExternalInput")
    wu_d = nc.dram_tensor("wu", [I, D], F32, kind="ExternalInput")
    wd_d = nc.dram_tensor("wd", [D, I], F32, kind="ExternalInput")
    idb_d = nc.dram_tensor("idb_hbm", [128, 128], BF16, kind="ExternalInput")
    idf_d = nc.dram_tensor("idf_hbm", [1, 1], F32, kind="ExternalInput")
    onec_d = nc.dram_tensor("onec_hbm", [128, 1], BF16, kind="ExternalInput")

    y_d = nc.dram_tensor("y", [C, D], F32, kind="ExternalOutput")

    wg_scr = nc.dram_tensor("wg_scr", [IK, DK, 128, 128], BF16, kind="Internal")
    row_scr = nc.dram_tensor("row_scr", [2, D], F32, kind="Internal")
    wu_scr = nc.dram_tensor("wu_scr", [IK, DK, 128, 128], BF16, kind="Internal")
    wd_scr = nc.dram_tensor("wd_scr", [IK, 128, D], BF16, kind="Internal")

    dbg = {}
    if "A" in debug:
        dbg["qt"] = nc.dram_tensor("dbg_qt", [C, D], F32, kind="ExternalOutput")
        dbg["kt"] = nc.dram_tensor("dbg_kt", [C, D], F32, kind="ExternalOutput")
        dbg["g"] = nc.dram_tensor("dbg_g", [D, D], F32, kind="ExternalOutput")
        dbg["rows"] = nc.dram_tensor("dbg_rows", [8, D], F32, kind="ExternalOutput")
    if "W" in debug:
        dbg["wgq"] = nc.dram_tensor("dbg_wgq", [IK * DK * 128, 128], F32, kind="ExternalOutput")
        dbg["wdq"] = nc.dram_tensor("dbg_wdq", [I, D], F32, kind="ExternalOutput")
    if "C" in debug:
        dbg["bnsqd"] = nc.dram_tensor("dbg_bnsqd", [128, D], F32, kind="ExternalOutput")
        dbg["bred2d"] = nc.dram_tensor("dbg_bred2d", [128, 128], F32, kind="ExternalOutput")
        dbg["scs"] = nc.dram_tensor("dbg_scs", [128, 4], F32, kind="ExternalOutput")
        dbg["s1a"] = nc.dram_tensor("dbg_s1a", [128, D], F32, kind="ExternalOutput")
        dbg["bnagd"] = nc.dram_tensor("dbg_bnagd", [128, 2], F32, kind="ExternalOutput")
        dbg["vfd"] = nc.dram_tensor("dbg_vfd", [128, DK * TILE], F32, kind="ExternalOutput")
        dbg["xrd"] = nc.dram_tensor("dbg_xrd", [128, D], F32, kind="ExternalOutput")
    if "B" in debug:
        dbg["y1"] = nc.dram_tensor("dbg_y1", [TILE, D], F32, kind="ExternalOutput")
        dbg["xqt"] = nc.dram_tensor("dbg_xqt", [128, DK * TILE], F32, kind="ExternalOutput")
        dbg["hqt"] = nc.dram_tensor("dbg_hqt", [128, IK * TILE], F32, kind="ExternalOutput")
        dbg["zf"] = nc.dram_tensor("dbg_zf", [128, D], F32, kind="ExternalOutput")

    es = ExitStack()
    sb = lambda name, shape, dt: es.enter_context(nc.sbuf_tensor(name, shape, dt))
    sem = lambda name: es.enter_context(nc.semaphore(name))

    idb = sb("idb", [128, 128], BF16)
    idf = sb("idf", [1, 1], F32)
    onec = sb("onec", [128, 1], BF16)
    gam = sb("gam", [1, 1], F32)
    bvc = sb("bvc", [128, DK], F32)
    G = sb("G", [128, DK, D], BF16)

    s_init = sem("s_init")
    with nc.Block() as block:
        @block.sync
        def _(sync):
            for dst, src in ((idb, idb_d), (idf, idf_d), (onec, onec_d),
                             (gam, gamma_d), (bvc, bvc_d)):
                sync.dma_start(out=dst[:], in_=src.ap()).then_inc(s_init, 16)

    esA = ExitStack()
    sbA = lambda name, shape, dt: esA.enter_context(nc.sbuf_tensor(name, shape, dt))

    Q_t = sbA("Q_t", [128, NCC, D], BF16)
    K_t = sbA("K_t", [128, NCC, D], BF16)
    esA1 = ExitStack()
    sbA1 = lambda name, shape, dt: esA1.enter_context(nc.sbuf_tensor(name, shape, dt))
    wqT = sbA1("wqT", [128, DK, D], BF16)
    wkT = sbA1("wkT", [128, DK, D], BF16)
    bq_bc = sbA1("bq_bc", [128, D], F32)
    bk_bc = sbA1("bk_bc", [128, D], F32)
    xTc = [sbA1(f"xTc{i}", [128, DK, 128], BF16) for i in range(2)]

    s_aw = sem("s_aw")
    s_awg = sem("s_awg")
    s_ax = [sem("s_ax0"), sem("s_ax1")]
    s_aq = sem("s_aq")
    s_ae = sem("s_ae")

    with nc.Block() as block, ExitStack() as esP:
        qk_ps = [esP.enter_context(nc.psum_tensor(f"qkps{i}", [128, 512], F32)) for i in range(8)]

        @block.gpsimd
        def _(gpsimd):
            gpsimd.wait_ge(s_init, 80)
            gpsimd.dma_start(out=wqT[:], in_=wqT_d.ap()).then_inc(s_awg, 16)
            gpsimd.dma_start(out=wkT[:], in_=wkT_d.ap()).then_inc(s_awg, 16)
            for cc in range(NCC):
                if cc >= 2:
                    gpsimd.wait_ge(s_aq, 4 * (cc - 1))
                src = bass.AP(x_feat, cc * 128, [[C, 128], [128 * C, DK], [1, 128]])
                gpsimd.dma_start(out=xTc[cc % 2][:], in_=src).then_inc(s_ax[cc % 2], 16)

        @block.sync
        def _(sync):
            sync.dma_start(out=bq_bc[:], in_=bass.AP(bq_d, 0, [[0, 128], [1, D]])).then_inc(s_aw, 16)
            sync.dma_start(out=bk_bc[:], in_=bass.AP(bk_d, 0, [[0, 128], [1, D]])).then_inc(s_aw, 16)

        @block.tensor
        def _(tensor):
            tensor.wait_ge(s_aw, 32)
            tensor.wait_ge(s_awg, 32)
            for cc in range(NCC):
                tensor.wait_ge(s_ax[cc % 2], 16 * (cc // 2 + 1))
                if cc >= 2:
                    tensor.wait_ge(s_ae, cc - 1)
                bs = 4 * (cc % 2)
                xc = xTc[cc % 2]
                for dk in range(DK):
                    lhs = xc[:, dk, :]
                    for j, (wT, nh) in enumerate(((wqT, 0), (wqT, 1), (wkT, 0), (wkT, 1))):
                        mm = tensor.matmul(qk_ps[bs + j][:], lhs,
                                           wT[:, dk, nh * 512:(nh + 1) * 512],
                                           start=(dk == 0), stop=(dk == DK - 1))
                        if dk == DK - 1:
                            mm.then_inc(s_aq, 1)

        @block.vector
        def _(vector):
            for cc in range(NCC):
                vector.wait_ge(s_aq, 4 * (cc + 1))
                bs = 4 * (cc % 2)
                for j, (dst, bias, nh) in enumerate(((Q_t, bq_bc, 0), (Q_t, bq_bc, 1),
                                                     (K_t, bk_bc, 0), (K_t, bk_bc, 1))):
                    vector.tensor_tensor(
                        out=dst[:, cc, nh * 512:(nh + 1) * 512],
                        in0=qk_ps[bs + j][:], in1=bias[:, nh * 512:(nh + 1) * 512], op=OP.add)
                    if j == 3:
                        vector.drain().then_inc(s_ae, 1)

    esA1.close()
    sqb = [sbA(f"sqb{i}", [128, D], BF16) for i in range(4)]
    gt_bc = sbA("gt_bc", [128, D], F32)
    gq_bc = sbA("gq_bc", [128, D], F32)
    qsq_r = sbA("qsq_r", [1, D], F32)
    qcs_r = sbA("qcs_r", [1, D], F32)
    ksq_r = sbA("ksq_r", [1, D], F32)
    p_r = sbA("p_r", [1, D], F32)
    qn_r = sbA("qn_r", [1, D], F32)
    kn_r = sbA("kn_r", [1, D], F32)
    rqn_r = sbA("rqn_r", [1, D], F32)
    t_r = sbA("t_r", [1, D], F32)
    tmp_r = sbA("tmp_r", [1, D], F32)
    gt_r = sbA("gt_r", [1, D], F32)
    rkn_c = sbA("rkn_c", [128, DK], F32)
    s_sq = sem("s_sq")
    s_st = sem("s_st")
    with nc.Block() as block, ExitStack() as esP:
        st_ps = [esP.enter_context(nc.psum_tensor(f"stps{i}", [1, 512], F32)) for i in range(6)]

        @block.scalar
        def _(scalar):
            for cc in range(NCC):
                if cc >= 2:
                    scalar.wait_ge(s_st, cc - 1)
                scalar.activation(sqb[2 * (cc % 2)][:], Q_t[:, cc, :], AF.Square)
                scalar.activation(sqb[2 * (cc % 2) + 1][:], K_t[:, cc, :], AF.Square).then_inc(s_sq, 1)

        @block.tensor
        def _(tensor):
            for cc in range(NCC):
                tensor.wait_ge(s_sq, cc + 1)
                qs, ks = sqb[2 * (cc % 2)], sqb[2 * (cc % 2) + 1]
                st, sp = (cc == 0), (cc == NCC - 1)
                last = None
                for nh in range(2):
                    sl = slice(nh * 512, (nh + 1) * 512)
                    tensor.matmul(st_ps[nh][:], onec[:], qs[:, sl], start=st, stop=sp)
                    tensor.matmul(st_ps[2 + nh][:], onec[:], Q_t[:, cc, sl], start=st, stop=sp)
                    last = tensor.matmul(st_ps[4 + nh][:], onec[:], ks[:, sl], start=st, stop=sp)
                last.then_inc(s_st, 1)

        @block.vector
        def _(vector):
            vector.wait_ge(s_st, NCC)
            for nh in range(2):
                sl = slice(nh * 512, (nh + 1) * 512)
                vector.tensor_copy(qsq_r[:, sl], st_ps[nh][:])
                vector.tensor_copy(qcs_r[:, sl], st_ps[2 + nh][:])
                vector.tensor_copy(ksq_r[:, sl], st_ps[4 + nh][:])

    s_r = sem("s_r")
    s_rc = sem("s_rc")
    with nc.Block() as block, ExitStack() as esP:
        kc_ps = esP.enter_context(nc.psum_tensor("kcps", [128, DK], F32))

        @block.scalar
        def _(scalar):
            scalar.activation(tmp_r[:], qsq_r[:], AF.Ln).then_inc(s_rc, 1)
            scalar.wait_ge(s_rc, 1)
            scalar.activation(qn_r[:], tmp_r[:], AF.Exp, scale=0.5).then_inc(s_rc, 1)
            scalar.wait_ge(s_rc, 2)
            scalar.activation(tmp_r[:], ksq_r[:], AF.Ln).then_inc(s_rc, 1)
            scalar.wait_ge(s_rc, 3)
            scalar.activation(kn_r[:], tmp_r[:], AF.Exp, scale=0.5).then_inc(s_r, 1)

        @block.vector
        def _(vector):
            vector.wait_ge(s_r, 1)
            vector.reciprocal(rqn_r[:], qn_r[:])
            vector.drain()
            vector.reciprocal(tmp_r[:], kn_r[:])
            vector.drain().then_inc(s_r, 1)

        @block.tensor
        def _(tensor):
            tensor.wait_ge(s_r, 2)
            last = None
            for k in range(DK):
                last = tensor.transpose(kc_ps[:, k:k + 1], tmp_r[:, k * 128:(k + 1) * 128], idf[:])
            last.then_inc(s_r, 1)

        @block.vector
        def _(vector):
            vector.wait_ge(s_r, 3)
            vector.tensor_copy(rkn_c[:], kc_ps[:])

    s_g = sem("s_g")
    s_ge = sem("s_ge")
    s_p = sem("s_p")
    with nc.Block() as block, ExitStack() as esP:
        g_ps = [esP.enter_context(nc.psum_tensor(f"gps{i}", [128, 512], F32)) for i in range(8)]

        @block.tensor
        def _(tensor):
            for mh in range(2):
                if mh == 1:
                    tensor.wait_ge(s_ge, 8)
                for cc in range(NCC):
                    for mk in range(4):
                        m0 = (mh * 4 + mk) * 128
                        lhs = K_t[:, cc, m0:m0 + 128]
                        for nh in range(2):
                            mm = tensor.matmul(g_ps[2 * mk + nh][:], lhs,
                                               Q_t[:, cc, nh * 512:(nh + 1) * 512],
                                               start=(cc == 0), stop=(cc == NCC - 1))
                            if cc == NCC - 1:
                                mm.then_inc(s_g, 1)
        @block.vector
        def _(vector):
            for mh in range(2):
                vector.wait_ge(s_g, 8 * (mh + 1))
                for mk in range(4):
                    gmk = mh * 4 + mk
                    for nh in range(2):
                        vector.tensor_scalar(
                            out=G[:, gmk, nh * 512:(nh + 1) * 512],
                            in0=g_ps[2 * mk + nh][:],
                            scalar1=rkn_c[:, gmk:gmk + 1], scalar2=None,
                            op0=OP.mult)
                vector.drain().then_inc(s_ge, 8)

    with nc.Block() as block, ExitStack() as esP:
        p_ps = [esP.enter_context(nc.psum_tensor(f"pps{i}", [1, 512], F32)) for i in range(2)]

        @block.tensor
        def _(tensor):
            last = None
            for gmk in range(DK):
                for nh in range(2):
                    last = tensor.matmul(p_ps[nh][:], onec[:], G[:, gmk, nh * 512:(nh + 1) * 512],
                                         start=(gmk == 0), stop=(gmk == DK - 1))
            last.then_inc(s_p, 1)

        @block.vector
        def _(vector):
            vector.wait_ge(s_p, 1)
            for nh in range(2):
                vector.tensor_copy(p_r[:, nh * 512:(nh + 1) * 512], p_ps[nh][:])

    s_t = sem("s_t")
    with nc.Block() as block:
        @block.vector
        def _(vector):
            VF(vector, vector.tensor_scalar(out=tmp_r[:], in0=qcs_r[:], scalar1=EPS_ATTN, scalar2=None, op0=OP.mult))
            VF(vector, vector.tensor_tensor(out=tmp_r[:], in0=tmp_r[:], in1=p_r[:], op=OP.add))
            VF(vector, vector.tensor_tensor(out=tmp_r[:], in0=tmp_r[:], in1=rqn_r[:], op=OP.mult))
            VF(vector, vector.tensor_scalar(out=tmp_r[:], in0=tmp_r[:], scalar1=float(D), scalar2=None, op0=OP.add))
            vector.reciprocal(t_r[:], tmp_r[:])
            vector.drain()
            vector.tensor_tensor(out=tmp_r[:], in0=t_r[:], in1=rqn_r[:], op=OP.mult)
            VF(vector, vector.tensor_scalar(out=tmp_r[:], in0=tmp_r[:], scalar1=gam[0:1, 0:1], scalar2=None, op0=OP.mult))
            vector.tensor_scalar(out=gt_r[:], in0=t_r[:], scalar1=gam[0:1, 0:1],
                                 scalar2=None, op0=OP.mult)
            vector.drain().then_inc(s_t, 1)

        @block.sync
        def _(sync):
            sync.wait_ge(s_t, 1)
            sync.dma_start(out=bass.AP(row_scr, 0, [[D, 1], [1, D]]), in_=tmp_r[:]).then_inc(s_t, 16)
            sync.dma_start(out=bass.AP(row_scr, D, [[D, 1], [1, D]]), in_=gt_r[:]).then_inc(s_t, 16)
            sync.wait_ge(s_t, 33)
            sync.dma_start(out=gq_bc[:], in_=bass.AP(row_scr, 0, [[0, 128], [1, D]])).then_inc(s_t, 16)
            sync.dma_start(out=gt_bc[:], in_=bass.AP(row_scr, D, [[0, 128], [1, D]])).then_inc(s_t, 16)

        @block.vector
        def _(vector):
            vector.wait_ge(s_t, 65)
            for mk in range(DK):
                vector.tensor_tensor(out=G[:, mk, :], in0=G[:, mk, :], in1=gq_bc[:], op=OP.mult)
                vector.tensor_tensor(out=G[:, mk, :], in0=G[:, mk, :], in1=gt_bc[:], op=OP.add)

    if "A" in debug:
        s_dA = sem("s_dA")
        s_dA2 = sem("s_dA2")
        with nc.Block() as block:
            @block.gpsimd
            def _(gpsimd):
                n = 0
                for cc in range(NCC):
                    gpsimd.dma_start(out=bass.AP(dbg["qt"], cc * 128 * D, [[D, 128], [1, D]]),
                                     in_=Q_t[:, cc, :]).then_inc(s_dA, 16)
                    gpsimd.dma_start(out=bass.AP(dbg["kt"], cc * 128 * D, [[D, 128], [1, D]]),
                                     in_=K_t[:, cc, :]).then_inc(s_dA, 16)
                    n += 2
                for mk in range(DK):
                    gpsimd.dma_start(out=bass.AP(dbg["g"], mk * 128 * D, [[D, 128], [1, D]]),
                                     in_=G[:, mk, :]).then_inc(s_dA, 16)
                    n += 1
                gpsimd.wait_ge(s_dA, 16 * n)

            @block.sync
            def _(sync):
                for i, r in enumerate((qsq_r, qcs_r, ksq_r, p_r, qn_r, kn_r, t_r, tmp_r)):
                    sync.dma_start(out=bass.AP(dbg["rows"], i * D, [[D, 1], [1, D]]),
                                   in_=r[:]).then_inc(s_dA2, 16)
                sync.wait_ge(s_dA2, 16 * 8)

    esA.close()
    if stage == "A":
        es.close()
        return nc, dbg

    esW = ExitStack()
    sbW = lambda name, shape, dt: esW.enter_context(nc.sbuf_tensor(name, shape, dt))

    wbuf = [sbW(f"wbuf{i}", [128, I], F32) for i in range(2)]
    wabs = sbW("wabs", [128, I], F32)
    wsgb = sbW("wsgb", [128, I], BF16)
    wsc = sbW("wsc", [128, I // 16], F32)
    wrs4 = sbW("wrs4", [128, I // 16], F32)
    wrs2 = sbW("wrs2", [128, I // 16], F32)
    wsh = sbW("wsh", [128, I // 16], F32)
    wo1 = sbW("wo1", [128, I], BF16)
    wo2 = sbW("wo2", [128, I], BF16)
    wqv = [sbW(f"wqv{i}", [128, I], BF16) for i in range(2)]
    wtr = [sbW(f"wtr{i}", [128, IK * 128], BF16) for i in range(2)]

    s_wl = [sem("s_wl0"), sem("s_wl1")]
    s_wa = sem("s_wa")
    s_wv = sem("s_wv")
    s_wg2 = sem("s_wg2")
    s_wq = sem("s_wq")
    s_wt = sem("s_wt")
    s_we = sem("s_we")
    s_ws = [sem("s_ws0"), sem("s_ws1")]

    jobs = [("wg", i) for i in range(IK)] + [("wu", i) for i in range(IK)] + [("wd", i) for i in range(DK)]
    NJ = len(jobs)
    jw = lambda j: D if j[0] in ("wg", "wu") else I

    def stores_through(jn):
        tot = 0
        for j in jobs[:jn]:
            tot += DK if j[0] in ("wg", "wu") else 0
        return tot

    with nc.Block() as block, ExitStack() as esP:
        wt_ps = [[esP.enter_context(nc.psum_tensor(f"wtps{i}_{j}", [128, 1024], BF16)) for j in range(3)]
                 for i in range(2)]

        def emit_wstore(sync, pj):
            kind, idx = jobs[pj]
            sync.wait_ge(s_we, pj + 1)
            if kind in ("wg", "wu"):
                scr = wg_scr if kind == "wg" else wu_scr
                for hk in range(DK):
                    dst = bass.AP(scr, (idx * DK + hk) * 128 * 128, [[128, 128], [1, 128]])
                    sync.dma_start(out=dst, in_=wtr[pj % 2][:, hk * 128:(hk + 1) * 128]).then_inc(s_ws[pj % 2], 16)
                return DK
            for ik in range(IK):
                dst = bass.AP(wd_scr, ik * 128 * D + idx * 128, [[D, 128], [1, 128]])
                sync.dma_start(out=dst, in_=wtr[pj % 2][:, ik * 128:(ik + 1) * 128]).then_inc(s_ws[pj % 2], 16)
            return IK

        @block.sync
        def _(sync):
            tot = 0
            for jn, job in enumerate(jobs):
                kind, idx = job
                W = jw(job)
                if jn >= 2:
                    sync.wait_ge(s_wa, 2 * (jn - 1))
                src = {"wg": wg_d, "wu": wu_d, "wd": wd_d}[kind]
                sync.dma_start(out=wbuf[jn % 2][:, 0:W],
                               in_=bass.AP(src, idx * 128 * W, [[W, 128], [1, W]])).then_inc(s_wl[jn % 2], 16)
                if jn >= 1:
                    tot += emit_wstore(sync, jn - 1)
            tot += emit_wstore(sync, NJ - 1)
            tot0 = sum((DK if jobs[j][0] in ("wg", "wu") else IK) for j in range(0, NJ, 2))
            tot1 = sum((DK if jobs[j][0] in ("wg", "wu") else IK) for j in range(1, NJ, 2))
            sync.wait_ge(s_ws[0], 16 * tot0)
            sync.wait_ge(s_ws[1], 16 * tot1)
            sync.wait_ge(s_we, NJ)

        @block.scalar
        def _(scalar):
            for jn, job in enumerate(jobs):
                W = jw(job)
                scalar.wait_ge(s_wl[jn % 2], 16 * (jn // 2 + 1))
                if jn >= 1:
                    scalar.wait_ge(s_wq, jn)
                scalar.activation(wabs[:, 0:W], wbuf[jn % 2][:, 0:W], AF.Abs)
                scalar.activation(wsgb[:, 0:W], wbuf[jn % 2][:, 0:W], AF.Sign).then_inc(s_wa, 2)

        def n_stores_par(pj):
            tot = 0
            for j in range(pj % 2, pj - 1, 2):
                tot += DK if jobs[j][0] in ("wg", "wu") else IK
            return tot

        def emit_wevac(vector, pj):
            W = jw(jobs[pj])
            vector.wait_ge(s_wt, pj + 1)
            if pj >= 2:
                vector.wait_ge(s_ws[pj % 2], 16 * n_stores_par(pj))
            nblk = W // 128
            last = None
            for p in range((nblk + 7) // 8):
                w_p = min(1024, (nblk - 8 * p) * 128)
                last = vector.tensor_copy(wtr[pj % 2][:, p * 1024:p * 1024 + w_p],
                                          wt_ps[pj % 2][p][:, 0:w_p])
            vector.drain().then_inc(s_we, 1)

        @block.vector
        def _(vector):
            for jn, job in enumerate(jobs):
                W = jw(job)
                vector.wait_ge(s_wa, 2 * (jn + 1))
                emit_quant(vector, wabs, W, wsc, wrs4, wrs2, wsh, wo1, wo2)
                nb = W // 16
                shb = bass.AP(wsh, 0, [[I // 16, 128], [1, nb], [0, 16]])
                if jn >= 2:
                    vector.wait_ge(s_wt, jn - 1)
                vector.tensor_tensor(out=wo2[:, 0:W], in0=wo1[:, 0:W],
                                     in1=wsgb[:, 0:W], op=OP.mult)
                vector.tensor_tensor(out=wqv[jn % 2][:, 0:W], in0=wo2[:, 0:W],
                                     in1=shb, op=OP.mult)
                vector.drain().then_inc(s_wq, 1)
                if jn >= 1:
                    emit_wevac(vector, jn - 1)
            emit_wevac(vector, NJ - 1)

        @block.tensor
        def _(tensor):
            for jn, job in enumerate(jobs):
                W = jw(job)
                tensor.wait_ge(s_wq, jn + 1)
                if jn >= 2:
                    tensor.wait_ge(s_we, jn - 1)
                nblk = W // 128
                last = None
                for b in range(nblk):
                    last = tensor.transpose(wt_ps[jn % 2][b // 8][:, (b % 8) * 128:(b % 8 + 1) * 128],
                                            wqv[jn % 2][:, b * 128:(b + 1) * 128], idb[:])
                last.then_inc(s_wt, 1)

    if "W" in debug:
        s_dW = sem("s_dW")
        with nc.Block() as block:
            @block.gpsimd
            def _(gpsimd):
                n = 0
                for ik in range(IK):
                    for hk in range(DK):
                        gpsimd.dma_start(
                            out=wbuf[0][:, 0:128],
                            in_=bass.AP(wg_scr, (ik * DK + hk) * 128 * 128, [[128, 128], [1, 128]])
                        ).then_inc(s_dW, 16)
                        n += 1
                        gpsimd.wait_ge(s_dW, 32 * n - 16)
                        gpsimd.dma_start(
                            out=bass.AP(dbg["wgq"], (ik * DK + hk) * 128 * 128, [[128, 128], [1, 128]]),
                            in_=wbuf[0][:, 0:128]).then_inc(s_dW, 16)
                        gpsimd.wait_ge(s_dW, 32 * n)
                for ik in range(IK):
                    gpsimd.dma_start(out=wbuf[0][:, 0:D],
                                     in_=bass.AP(wd_scr, ik * 128 * D, [[D, 128], [1, D]])).then_inc(s_dW, 16)
                    n += 1
                    gpsimd.wait_ge(s_dW, 32 * n - 16)
                    gpsimd.dma_start(out=bass.AP(dbg["wdq"], ik * 128 * D, [[D, 128], [1, D]]),
                                     in_=wbuf[0][:, 0:D]).then_inc(s_dW, 16)
                    gpsimd.wait_ge(s_dW, 32 * n)

    esW.close()
    if stage == "W":
        es.close()
        return nc, dbg

    esB = ExitStack()
    sbB = lambda name, shape, dt: esB.enter_context(nc.sbuf_tensor(name, shape, dt))

    wvT = sbB("wvT", [128, DK, D], BF16)
    xTt = [sbB(f"xTt{i}", [128, DK, TILE], BF16) for i in range(2)]
    xres = [sbB(f"xres{i}", [128, D], F32) for i in range(2)]
    Vf = sbB("Vf", [128, DK, TILE], BF16)
    y1t = sbB("y1t", [128, CSUB, D], F32)
    s1f = sbB("s1f", [128, D], F32)
    yab = sbB("yab", [128, D], F32)
    ysg = sbB("ysg", [128, D], BF16)
    yo1 = sbB("yo1", [128, D], BF16)
    yo2 = sbB("yo2", [128, D], BF16)
    yxq = sbB("yxq", [128, D], BF16)
    ysc = sbB("ysc", [128, D // 16], F32)
    yrs4 = sbB("yrs4", [128, D // 16], F32)
    yrs2 = sbB("yrs2", [128, D // 16], F32)
    ysh = sbB("ysh", [128, D // 16], F32)
    xqT = sbB("xqT", [128, DK, TILE], BF16)
    wgs = [sbB(f"wgs{i}", [128, 4 * DK * 128], BF16) for i in range(2)]
    wus = [sbB(f"wus{i}", [128, 4 * DK * 128], BF16) for i in range(2)]
    hst = sbB("hst", [128, 512], F32)
    hab = sbB("hab", [128, 512], F32)
    hsgb = sbB("hsgb", [128, 512], BF16)
    ho1 = sbB("ho1", [128, 512], BF16)
    ho2 = sbB("ho2", [128, 512], BF16)
    hqv = sbB("hqv", [128, 512], BF16)
    hsc = sbB("hsc", [128, 32], F32)
    hrs4 = sbB("hrs4", [128, 32], F32)
    hrs2 = sbB("hrs2", [128, 32], F32)
    hsh = sbB("hsh", [128, 32], F32)
    hqT = sbB("hqT", [128, IK, TILE], BF16)
    wds = [sbB(f"wds{i}", [128, D], BF16) for i in range(4)]
    sil = sbB("sil", [128, 512], F32)
    y2b = [sbB(f"y2b{i}", [128, D], F32) for i in range(2)]
    bnsq = sbB("bnsq", [128, D], F32)
    bred = sbB("bred", [128, 128], F32)
    bred2 = sbB("bred2", [128, 128], F32)
    if "C" in debug:
        bnsq_t = sbB("bnsq_t", [128, D], F32)
        bred2_t = sbB("bred2_t", [128, 128], F32)
        scs_t = sbB("scs_t", [128, 4], F32)
        s1a_t = sbB("s1a_t", [128, D], F32)
        bnag_t = sbB("bnag_t", [128, 2], F32)
        xr_t = sbB("xr_t", [128, D], F32)
    bnag = sbB("bnag", [128, 2], F32)
    bns2 = sbB("bns2", [128, 1], F32)
    bnr1 = sbB("bnr1", [128, 1], F32)
    bnr2 = sbB("bnr2", [128, 1], F32)
    epsc = sbB("epsc", [128, 1], F32)

    s_bw0 = sem("s_bw0")
    s_bx = [sem("s_bx0"), sem("s_bx1")]
    s_br = [sem("s_br0"), sem("s_br1")]
    s_bv = sem("s_bv")
    s_bve = sem("s_bve")
    s_ba = sem("s_ba")
    s_be = sem("s_be")
    s_bse = sem("s_bse")
    s_bcm = sem("s_bcm")
    s_bgp = sem("s_bgp")
    s_bq = sem("s_bq")
    s2_wg = [sem("s2_wg0"), sem("s2_wg1")]
    s2_g = sem("s2_g")
    s2_sil = sem("s2_sil")
    s2_h = sem("s2_h")
    s2_hse = sem("s2_hse")
    s2_hcm = sem("s2_hcm")
    s2_hgp = sem("s2_hgp")
    s2_t = sem("s2_t")
    s2_te = sem("s2_te")
    s2_d = sem("s2_d")
    s2_dk = sem("s2_dk")
    s2_wd = [sem(f"s2_wd{i}") for i in range(4)]
    s2_l2 = sem("s2_l2")
    s2_lse = sem("s2_lse")
    s2_o = sem("s2_o")
    s2_dbg = sem("s2_dbg")
    s2_dbg2 = sem("s2_dbg2")

    with nc.Block() as block:
        @block.gpsimd
        def _(gpsimd):
            gpsimd.dma_start(out=wvT[:], in_=wvT_d.ap()).then_inc(s_bw0, 16)
            src = bass.AP(x_feat, 0, [[C, 128], [128 * C, DK], [1, TILE]])
            gpsimd.dma_start(out=xTt[0][:], in_=src).then_inc(s_bx[0], 16)

        @block.vector
        def _(vector):
            vector.memset(epsc[:], EPS_LN)
            vector.drain()

        @block.sync
        def _(sync):
            sync.wait_ge(s_bw0, 16)
            sync.wait_ge(s_bx[0], 16)

    for t in range(NTILES):
        b_bv, b_bve, b_ba = 8 * t, 8 * t, CSUB * t
        b_be, b_bse, b_bcm, b_bgp, b_bq = 12 * t, 12 * t, 4 * t, 4 * t, 8 * t
        with nc.Block() as block, ExitStack() as esP:
            pp = lambda name, shape, dt: esP.enter_context(nc.psum_tensor(name, shape, dt))
            vps = [pp(f"vps{t}_{i}", [128, TILE], F32) for i in range(2)]
            aps = [pp(f"aps{t}_{i}", [128, D], F32) for i in range(2)]
            tps = [pp(f"tps{t}_{i}", [128, 1024], BF16) for i in range(2)]

            @block.gpsimd
            def _(gpsimd):
                if t + 1 < NTILES:
                    src = bass.AP(x_feat, (t + 1) * TILE, [[C, 128], [128 * C, DK], [1, TILE]])
                    gpsimd.dma_start(out=xTt[(t + 1) % 2][:], in_=src).then_inc(s_bx[(t + 1) % 2], 16)

            @block.sync
            def _(sync):
                for cs in range(CSUB):
                    if cs >= 2:
                        sync.wait_ge(s_be, b_be + 3 * (cs - 2) + 1)
                    r0 = t * TILE + cs * 128
                    sync.dma_start(out=xres[cs % 2][:],
                                   in_=bass.AP(x_tok, r0 * D, [[D, 128], [1, D]])).then_inc(s_br[cs % 2], 16)
                i0, W = STRIPS[0]
                nik = W // 128
                sync.dma_start(out=wgs[0][:, 0:nik * DK * 128],
                               in_=bass.AP(wg_scr, 0, [[128, 128], [128 * 128, nik * DK], [1, 128]])
                               ).then_inc(s2_wg[0], 16)
                sync.dma_start(out=wus[0][:, 0:nik * DK * 128],
                               in_=bass.AP(wu_scr, 0, [[128, 128], [128 * 128, nik * DK], [1, 128]])
                               ).then_inc(s2_wg[0], 16)

            @block.tensor
            def _(tensor):
                tensor.wait_ge(s_bx[t % 2], 16 * (t // 2 + 1))
                for mk in range(DK):
                    if mk >= 2:
                        tensor.wait_ge(s_bve, b_bve + mk - 1)
                    for dk in range(DK):
                        mm = tensor.matmul(vps[mk % 2][:], wvT[:, dk, mk * 128:(mk + 1) * 128],
                                           xTt[t % 2][:, dk, :], start=(dk == 0), stop=(dk == DK - 1))
                        if dk == DK - 1:
                            mm.then_inc(s_bv, 1)
                tensor.wait_ge(s_bve, b_bve + 8)

                def emit_xq_transpose(j):
                    tensor.wait_ge(s_be, b_be + 3 * j + 3)
                    if j >= 2:
                        tensor.wait_ge(s_bq, b_bq + 2 * (j - 1))
                    last = None
                    for dk in range(DK):
                        last = tensor.transpose(tps[j % 2][:, dk * 128:(dk + 1) * 128],
                                                yxq[:, dk * 128:(dk + 1) * 128], idb[:])
                    last.then_inc(s_bq, 1)

                for cs in range(CSUB):
                    if cs >= 2:
                        tensor.wait_ge(s_be, b_be + 3 * (cs - 1))
                    for mk in range(DK):
                        for nh in range(2):
                            mm = tensor.matmul(aps[cs % 2][:, nh * 512:(nh + 1) * 512],
                                               Vf[:, mk, cs * 128:(cs + 1) * 128],
                                               G[:, mk, nh * 512:(nh + 1) * 512],
                                               start=(mk == 0), stop=(mk == DK - 1))
                            if mk == DK - 1 and nh == 1:
                                mm.then_inc(s_ba, 1)
                    if cs >= 1:
                        emit_xq_transpose(cs - 1)
                emit_xq_transpose(CSUB - 1)

            @block.scalar
            def _(scalar):
                for mk in range(DK):
                    scalar.wait_ge(s_bv, b_bv + mk + 1)
                    scalar.activation(Vf[:, mk, :], vps[mk % 2][:], AF.Identity,
                                      bias=bvc[:, mk:mk + 1]).then_inc(s_bve, 1)
                for cs in range(CSUB):
                    scalar.wait_ge(s_be, b_be + 3 * cs + 1)
                    scalar.activation(bnr1[:], bnag[:, 1:2], AF.Ln, bias=epsc[:]).then_inc(s_bse, 1)
                    scalar.wait_ge(s_bse, b_bse + 3 * cs + 1)
                    scalar.activation(bnr2[:], bnr1[:], AF.Exp, scale=-0.5).then_inc(s_bse, 1)
                    if "C" in dbg_flags and t == NTILES - 1 and cs == 3:
                        scalar.activation(scs_t[:, 0:1], bnag[:, 1:2], AF.Identity)
                        scalar.activation(scs_t[:, 1:2], bnr1[:], AF.Identity)
                        scalar.activation(scs_t[:, 2:3], bnr2[:], AF.Identity)
                        scalar.activation(scs_t[:, 3:4], epsc[:], AF.Identity)
                    scalar.wait_ge(s_be, b_be + 3 * cs + 2)
                    scalar.activation(yab[:], y1t[:, cs, :], AF.Abs)
                    scalar.activation(ysg[:], y1t[:, cs, :], AF.Sign).then_inc(s_bse, 1)

            @block.vector
            def _(vector):
                for cs in range(CSUB):
                    vector.wait_ge(s_ba, b_ba + cs + 1)
                    vector.wait_ge(s_br[cs % 2], 16 * (2 * t + cs // 2 + 1))
                    vector.scalar_tensor_tensor(out=s1f[:], in0=xres[cs % 2][:], scalar=2.0,
                                                in1=aps[cs % 2][:], op0=OP.mult, op1=OP.add)
                    if "C" in dbg_flags and t == NTILES - 1 and cs == 3:
                        vector.tensor_copy(s1a_t[:], s1f[:])
                        vector.tensor_copy(xr_t[:], xres[cs % 2][:])
                    emit_bn(vector, s1f, bnsq, bnag, bns2, bred, bred2)
                    vector.drain().then_inc(s_be, 1)
                    if "C" in dbg_flags and t == NTILES - 1 and cs == 3:
                        vector.tensor_copy(bnag_t[:], bnag[:])
                        vector.tensor_copy(bnsq_t[:], bnsq[:])
                        vector.tensor_copy(bred2_t[:, 0:64], bred2[:, 0:64])
                        vector.tensor_copy(bred2_t[:, 64:128], bred[:, 0:64])
                    vector.wait_ge(s_bse, b_bse + 3 * cs + 2)
                    vector.tensor_scalar(out=y1t[:, cs, :], in0=s1f[:],
                                         scalar1=bnag[0:128, 0:1], scalar2=bnr2[0:128, 0:1],
                                         op0=OP.subtract, op1=OP.mult)
                    vector.drain().then_inc(s_be, 1)
                    vector.wait_ge(s_bse, b_bse + 3 * cs + 3)
                    emit_quant(vector, yab, D, ysc, yrs4, yrs2, ysh, yo1, yo2)
                    shb = bass.AP(ysh, 0, [[D // 16, 128], [1, D // 16], [0, 16]])
                    vector.tensor_tensor(out=yo2[:], in0=yo1[:], in1=ysg[:], op=OP.mult)
                    vector.tensor_tensor(out=yxq[:], in0=yo2[:], in1=shb, op=OP.mult)
                    vector.drain().then_inc(s_be, 1)
                    vector.wait_ge(s_bq, b_bq + 2 * cs + 1)
                    dst = bass.AP(xqT, cs * 128, [[DK * TILE, 128], [TILE, DK], [1, 128]])
                    vector.tensor_copy(dst, tps[cs % 2][:])
                    vector.drain().then_inc(s_bq, 1)

        b2_wg = 32 * (NST * t + 1)
        b_g, b_sil, b_h = 2 * NST * CSUB * t, NST * CSUB * t, 2 * NST * CSUB * t
        b_hse, b_hcm, b_hgp = 2 * NST * CSUB * t, NST * CSUB * t, NST * CSUB * t
        b_t2, b_te = NST * CSUB * t, NST * CSUB * t
        b_d, b_l2, b_lse, b_o = CSUB * t, 2 * CSUB * t, 2 * CSUB * t, 64 * t

        with nc.Block() as block, ExitStack() as esP:
            pp = lambda name, shape, dt: esP.enter_context(nc.psum_tensor(name, shape, dt))
            gps = [pp(f"bgps{t}_{i}", [128, 512], F32) for i in range(4)]
            hps = [pp(f"bhps{t}_{i}", [128, 512], BF16) for i in range(2)]

            @block.sync
            def _(sync):
                for sn in range(1, NST):
                    i0, W = STRIPS[sn]
                    nik = W // 128
                    ik0 = i0 // 128
                    if sn >= 2:
                        sync.wait_ge(s2_g, b_g + 2 * CSUB * (sn - 1))
                    sync.dma_start(out=wgs[sn % 2][:, 0:nik * DK * 128],
                                   in_=bass.AP(wg_scr, ik0 * DK * 128 * 128,
                                               [[128, 128], [128 * 128, nik * DK], [1, 128]])
                                   ).then_inc(s2_wg[sn % 2], 16)
                    sync.dma_start(out=wus[sn % 2][:, 0:nik * DK * 128],
                                   in_=bass.AP(wu_scr, ik0 * DK * 128 * 128,
                                               [[128, 128], [128 * 128, nik * DK], [1, 128]])
                                   ).then_inc(s2_wg[sn % 2], 16)

            @block.tensor
            def _(tensor):
                for sn in range(NST):
                    i0, W = STRIPS[sn]
                    nik = W // 128
                    tensor.wait_ge(s2_wg[sn % 2], 32 * (3 * t + sn // 2 + 1))
                    for cs in range(CSUB):
                        gu = sn * CSUB + cs
                        if gu >= 2:
                            tensor.wait_ge(s2_h, b_h + 2 * (gu - 1))
                        for pi, wbufs in enumerate((wgs, wus)):
                            last = None
                            for hk in range(DK):
                                rhs = bass.AP(wbufs[sn % 2], hk * 128,
                                              [[4 * DK * 128, 128], [DK * 128, nik], [1, 128]])
                                last = tensor.matmul(gps[2 * (gu % 2) + pi][:, 0:W],
                                                     xqT[:, hk, cs * 128:(cs + 1) * 128], rhs,
                                                     start=(hk == 0), stop=(hk == DK - 1))
                            last.then_inc(s2_g, 1)
                        tensor.wait_ge(s2_h, b_h + 2 * gu + 2)
                        if gu >= 2:
                            tensor.wait_ge(s2_te, b_te + gu - 1)
                        lastt = None
                        for b_ in range(nik):
                            lastt = tensor.transpose(hps[gu % 2][:, b_ * 128:(b_ + 1) * 128],
                                                     hqv[:, b_ * 128:(b_ + 1) * 128], idb[:])
                        lastt.then_inc(s2_t, 1)

            @block.scalar
            def _(scalar):
                for sn in range(NST):
                    i0, W = STRIPS[sn]
                    for cs in range(CSUB):
                        gu = sn * CSUB + cs
                        scalar.wait_ge(s2_g, b_g + 2 * gu + 1)
                        if gu >= 1:
                            scalar.wait_ge(s2_h, b_h + 2 * gu - 1)
                        scalar.activation(sil[:, 0:W], gps[2 * (gu % 2)][:, 0:W], AF.Silu).then_inc(s2_sil, 1)
                        scalar.wait_ge(s2_h, b_h + 2 * gu + 1)
                        scalar.activation(hab[:, 0:W], hst[:, 0:W], AF.Abs)
                        scalar.activation(hsgb[:, 0:W], hst[:, 0:W], AF.Sign).then_inc(s2_hse, 2)

            @block.vector
            def _(vector):
                for sn in range(NST):
                    i0, W = STRIPS[sn]
                    for cs in range(CSUB):
                        gu = sn * CSUB + cs
                        vector.wait_ge(s2_sil, b_sil + gu + 1)
                        vector.wait_ge(s2_g, b_g + 2 * gu + 2)
                        if gu >= 1:
                            vector.wait_ge(s2_hse, b_hse + 2 * gu)
                        vector.tensor_tensor(out=hst[:, 0:W], in0=sil[:, 0:W],
                                             in1=gps[2 * (gu % 2) + 1][:, 0:W], op=OP.mult)
                        vector.drain().then_inc(s2_h, 1)
                        vector.wait_ge(s2_hse, b_hse + 2 * gu + 2)
                        emit_quant(vector, hab, W, hsc, hrs4, hrs2, hsh, ho1, ho2)
                        vector.tensor_tensor(out=ho2[:, 0:W], in0=ho1[:, 0:W],
                                             in1=hsgb[:, 0:W], op=OP.mult)
                        nb = W // 16
                        shb = bass.AP(hsh, 0, [[32, 128], [1, nb], [0, 16]])
                        vector.tensor_tensor(out=hqv[:, 0:W], in0=ho2[:, 0:W],
                                             in1=shb, op=OP.mult)
                        vector.drain().then_inc(s2_h, 1)
                        vector.wait_ge(s2_t, b_t2 + gu + 1)
                        nik = W // 128
                        ik0 = i0 // 128
                        dst = bass.AP(hqT, ik0 * TILE + cs * 128, [[IK * TILE, 128], [TILE, nik], [1, 128]])
                        vector.tensor_copy(dst, hps[gu % 2][:, 0:W])
                        vector.drain().then_inc(s2_te, 1)


        wd_slot_cnt = [6, 6, 5, 5]
        with nc.Block() as block, ExitStack() as esP:
            pp = lambda name, shape, dt: esP.enter_context(nc.psum_tensor(name, shape, dt))
            dps = [pp(f"bdps{t}_{i}", [128, D], F32) for i in range(4)]

            @block.sync
            def _(sync):
                for ik in range(IK):
                    g = IK * t + ik
                    if g >= 4:
                        sync.wait_ge(s2_dk, g - 3)
                    sync.dma_start(out=wds[ik % 4][:],
                                   in_=bass.AP(wd_scr, ik * 128 * D, [[D, 128], [1, D]])
                                   ).then_inc(s2_wd[ik % 4], 16)
                for cs in range(CSUB):
                    sync.wait_ge(s2_l2, b_l2 + 2 * cs + 2)
                    r0 = t * TILE + cs * 128
                    sync.dma_start(out=bass.AP(y_d, r0 * D, [[D, 128], [1, D]]),
                                   in_=y2b[cs % 2][:]).then_inc(s2_o, 16)
                if t == NTILES - 1:
                    sync.wait_ge(s2_o, 16 * 4 * NTILES)

            @block.tensor
            def _(tensor):
                for ik in range(IK):
                    slot = ik % 4
                    tensor.wait_ge(s2_wd[slot], 16 * (wd_slot_cnt[slot] * t + ik // 4 + 1))
                    last = None
                    for cs in range(CSUB):
                        for nh in range(2):
                            last = tensor.matmul(dps[cs][:, nh * 512:(nh + 1) * 512],
                                                 hqT[:, ik, cs * 128:(cs + 1) * 128],
                                                 wds[slot][:, nh * 512:(nh + 1) * 512],
                                                 start=(ik == 0), stop=(ik == IK - 1))
                    last.then_inc(s2_dk, 1)

            @block.scalar
            def _(scalar):
                for cs in range(CSUB):
                    scalar.wait_ge(s2_l2, b_l2 + 2 * cs + 1)
                    scalar.activation(bnr1[:], bnag[:, 1:2], AF.Ln, bias=epsc[:]).then_inc(s2_lse, 1)
                    scalar.wait_ge(s2_lse, b_lse + 2 * cs + 1)
                    scalar.activation(bnr2[:], bnr1[:], AF.Exp, scale=-0.5).then_inc(s2_lse, 1)

            @block.vector
            def _(vector):
                for cs in range(CSUB):
                    vector.wait_ge(s2_dk, IK * (t + 1))
                    if cs >= 2:
                        vector.wait_ge(s2_o, b_o + 16 * (cs - 1))
                    for nh in range(2):
                        sl = slice(nh * 512, (nh + 1) * 512)
                        vector.tensor_tensor(out=s1f[:, sl], in0=y1t[:, cs, sl], in1=dps[cs][:, sl], op=OP.add)
                    emit_bn(vector, s1f, bnsq, bnag, bns2, bred, bred2)
                    vector.drain().then_inc(s2_l2, 1)
                    vector.wait_ge(s2_lse, b_lse + 2 * cs + 2)
                    vector.tensor_scalar(out=y2b[cs % 2][:], in0=s1f[:],
                                         scalar1=bnag[0:128, 0:1], scalar2=bnr2[0:128, 0:1],
                                         op0=OP.subtract, op1=OP.mult)
                    vector.drain().then_inc(s2_l2, 1)

        if "B" in debug and t == NTILES - 1:
            with nc.Block() as block:
                @block.sync
                def _(sync):
                    sync.wait_ge(s2_o, 16 * 4 * NTILES)
                    for cs in range(CSUB):
                        r0 = cs * 128
                        sync.dma_start(out=bass.AP(dbg["y1"], r0 * D, [[D, 128], [1, D]]),
                                       in_=y1t[:, cs, :]).then_inc(s2_dbg, 16)
                    sync.dma_start(out=bass.AP(dbg["zf"], 0, [[D, 128], [1, D]]),
                                   in_=s1f[:]).then_inc(s2_dbg, 16)
                    sync.wait_ge(s2_dbg, 16 * 5)

                @block.gpsimd
                def _(gpsimd):
                    gpsimd.wait_ge(s2_o, 16 * 4 * NTILES)
                    gpsimd.dma_start(out=bass.AP(dbg["xqt"], 0,
                                                 [[DK * TILE, 128], [1, DK * TILE]]),
                                     in_=xqT[:]).then_inc(s2_dbg2, 16)
                    gpsimd.dma_start(out=bass.AP(dbg["hqt"], 0,
                                                 [[IK * TILE, 128], [1, IK * TILE]]),
                                     in_=hqT[:]).then_inc(s2_dbg2, 16)
                    gpsimd.wait_ge(s2_dbg2, 16 * 2)

    if "C" in debug:
        s2_dbg3 = sem("s2_dbg3")
        with nc.Block() as block:
            @block.gpsimd
            def _(gpsimd):
                gpsimd.wait_ge(s2_o, 16 * 4 * NTILES)
                gpsimd.dma_start(out=dbg["bnsqd"].ap(), in_=bnsq_t[:]).then_inc(s2_dbg3, 16)
                gpsimd.dma_start(out=dbg["bred2d"].ap(), in_=bred2_t[:]).then_inc(s2_dbg3, 16)
                gpsimd.dma_start(out=dbg["scs"].ap(), in_=scs_t[:]).then_inc(s2_dbg3, 16)
                gpsimd.dma_start(out=dbg["s1a"].ap(), in_=s1a_t[:]).then_inc(s2_dbg3, 16)
                gpsimd.dma_start(out=dbg["bnagd"].ap(), in_=bnag_t[:]).then_inc(s2_dbg3, 16)
                gpsimd.dma_start(out=dbg["xrd"].ap(), in_=xr_t[:]).then_inc(s2_dbg3, 16)
                gpsimd.dma_start(out=bass.AP(dbg["vfd"], 0, [[DK * TILE, 128], [1, DK * TILE]]),
                                 in_=Vf[:]).then_inc(s2_dbg3, 16)
                gpsimd.wait_ge(s2_dbg3, 16 * 7)

    esB.close()
    es.close()
    return nc, dbg


_PROG_CACHE = {}


def _build():
    if "nc" not in _PROG_CACHE:
        nc, _ = build_core_program(stage="ALL", debug=())
        mybir.codegen_inst_isa_subclasses(nc)
        _PROG_CACHE["nc"] = nc
    return _PROG_CACHE["nc"]


def _wT_layout(W):
    Dm = W.shape[0]
    return np.ascontiguousarray(W.T.reshape(Dm // 128, 128, Dm).transpose(1, 0, 2).reshape(Dm, Dm))


def kernel(L0_attn_x, L0_attn_Wq, L0_attn_bq, L0_attn_Wk, L0_attn_bk,
           L0_attn_Wv, L0_attn_bv, L0_attn_gamma,
           L0_ffn_w_gate, L0_ffn_w_up, L0_ffn_w_down):
    x = np.ascontiguousarray(np.asarray(L0_attn_x, dtype=np.float32))
    B = x.shape[0]
    assert x.shape == (B, C, D)
    wqT = _wT_layout(np.asarray(L0_attn_Wq, dtype=np.float32))
    wkT = _wT_layout(np.asarray(L0_attn_Wk, dtype=np.float32))
    wvT = _wT_layout(np.asarray(L0_attn_Wv, dtype=np.float32))
    bq = np.ascontiguousarray(np.asarray(L0_attn_bq, dtype=np.float32).reshape(1, D))
    bk = np.ascontiguousarray(np.asarray(L0_attn_bk, dtype=np.float32).reshape(1, D))
    bvc = np.ascontiguousarray(np.asarray(L0_attn_bv, dtype=np.float32).reshape(DK, 128).T)
    gam = np.ascontiguousarray(np.asarray(L0_attn_gamma, dtype=np.float32).reshape(1, 1))
    wg = np.ascontiguousarray(np.asarray(L0_ffn_w_gate, dtype=np.float32))
    wu = np.ascontiguousarray(np.asarray(L0_ffn_w_up, dtype=np.float32))
    wd = np.ascontiguousarray(np.asarray(L0_ffn_w_down, dtype=np.float32))
    idb = np.eye(128, dtype=ml_dtypes.bfloat16)
    idf = np.ones((1, 1), dtype=np.float32)
    onec = np.ones((128, 1), dtype=ml_dtypes.bfloat16)

    nc = _build()
    in_maps = []
    for b in range(B):
        in_maps.append({
            "x_tok": x[b],
            "x_feat": np.ascontiguousarray(x[b].T),
            "wqT_hbm": wqT, "wkT_hbm": wkT, "wvT_hbm": wvT,
            "bq": bq, "bk": bk, "bvc_hbm": bvc, "gamma": gam,
            "wg": wg, "wu": wu, "wd": wd,
            "idb_hbm": idb, "idf_hbm": idf, "onec_hbm": onec,
        })
    res = run_bass_kernel_spmd(nc, in_maps, core_ids=list(range(B)))
    out = np.stack([np.asarray(res.results[b]["y"], dtype=np.float32) for b in range(B)], axis=0)
    return out


# version 12 (final)
# speedup vs baseline: 1.1180x; 1.0485x over previous
import os
import numpy as np
import ml_dtypes
from contextlib import ExitStack

import concourse.bass as bass
import concourse.mybir as mybir
from concourse.bass_utils import run_bass_kernel_spmd

F32 = mybir.dt.float32
BF16 = mybir.dt.bfloat16
AF = mybir.ActivationFunctionType
OP = mybir.AluOpType

import concourse.dve_ops as dve_ops
from concourse.dve_op import DveOp
from concourse.dve_spec import Spec, Src0, Src1, C0, C1, One, lower as dve_lower


def _register_op(name, spec_body, reference):
    ver = "v3"
    spec = Spec(body=spec_body)
    compiled = dve_lower(spec, ver=ver, perf_en=False, name=name)
    op = DveOp(name, spec, subdim=False, uops_sha={ver: compiled.sha(ver)}, reference=reference)
    if name not in dve_ops._SUB_OPCODE_FOR_NAME:
        dve_ops.OPS.append(op)
        row = dve_ops._CUSTOM_DVE_ROW_BASE + len(dve_ops.OPS) - 1
        assert row < 0x20
        dve_ops._SUB_OPCODE_FOR_NAME[name] = row
        dve_ops.CUSTOM_DVE_SPECS[name] = spec
        dve_ops.CUSTOM_DVE_REFS[name] = reference
    return next(o for o in dve_ops.OPS if o.name == name)


_m1 = Src0 * Src1
QLAD1 = _register_op(
    "AQL1",
    (((_m1 > One) + (_m1 > (C1 - C0))) + (_m1 > C0)) + (_m1 > (C1 - One)),
    lambda in0, in1, s0, s1, imm2: (
        ((in0 * in1 > 1.0) * 1.0 + (in0 * in1 > (s1 - s0)) * 1.0)
        + ((in0 * in1 > s0) * 1.0 + (in0 * in1 > (s1 - 1.0)) * 1.0)
    ),
)
_m2 = Src0 * Src1
_q7 = _m2 > (C0 + C0)
_t2 = ((_m2 > C0) + (_m2 > C1)) + (_q7 + _q7)
QLAD2 = _register_op(
    "AQL2",
    _t2 + _t2,
    lambda in0, in1, s0, s1, imm2: 2.0 * (
        ((in0 * in1 > s0) * 1.0 + (in0 * in1 > s1) * 1.0)
        + 2.0 * (in0 * in1 > (s0 + s0)) * 1.0
    ),
)

C = 4096
D = 1024
I = 2816
NCC = C // 128
DK = D // 128
IK = I // 128
TILE = 512
NTILES = C // TILE
CSUB = TILE // 128
STRIPS = [(0, 512), (512, 512), (1024, 512), (1536, 512), (2048, 512), (2560, 256)]
NST = len(STRIPS)
EPS_ATTN = 1e-6
EPS_LN = 1e-5


class VFence:
    def __init__(self):
        self.sem = None
        self.n = 0

    def bind(self, sem):
        self.sem = sem

    def __call__(self, eng, inst):
        return inst


VF = VFence()


def emit_quant(vector, src_abs, w, sc, rs4, rs2, sh, o1, o2):
    nb = w // 16
    nbt = sc.shape[1]
    a3 = src_abs[:, 0:w].rearrange("p (b k) -> p b k", k=16)
    vector.tensor_reduce(out=sc[:, 0:nb], in_=a3, axis=mybir.AxisListType.X,
                         op=OP.max, apply_absolute_value=False)
    vector.drain()
    vector.reciprocal(rs2[:, 0:nb], sc[:, 0:nb])
    vector.drain()
    vector.tensor_scalar(out=rs4[:, 0:nb], in0=rs2[:, 0:nb],
                         scalar1=1.6666666666666666e11, scalar2=24.0, op0=OP.min, op1=OP.mult)
    vector.tensor_scalar(out=sh[:, 0:nb], in0=sc[:, 0:nb],
                         scalar1=6e-12, scalar2=1.0 / 12.0, op0=OP.max, op1=OP.mult)
    vector.tensor_scalar(out=rs2[:, 0:nb], in0=rs2[:, 0:nb],
                         scalar1=1.6666666666666666e11, scalar2=12.0, op0=OP.min, op1=OP.mult)
    rs4b = bass.AP(rs4, 0, [[nbt, 128], [1, nb], [0, 16]])
    rs2b = bass.AP(rs2, 0, [[nbt, 128], [1, nb], [0, 16]])
    vector._custom_dve(QLAD1, out=o1[:, 0:w], in0=src_abs[:, 0:w], in1=rs4b, s0=5.0, s1=8.0)
    vector._custom_dve(QLAD2, out=o2[:, 0:w], in0=src_abs[:, 0:w], in1=rs2b, s0=5.0, s1=7.0)
    return vector.tensor_tensor(out=o1[:, 0:w], in0=o1[:, 0:w], in1=o2[:, 0:w], op=OP.add)


def emit_bn(vector, src, bnsq, bnag, bns2, bred, bred2):
    a3 = src[:, 0:1024].rearrange("p (b k) -> p b k", k=16)
    q3 = bnsq[:, 0:1024].rearrange("p (b k) -> p b k", k=16)
    vector.tensor_tensor(out=bnsq[:], in0=src[:], in1=src[:], op=OP.mult)
    vector.drain()
    vector.tensor_reduce(out=bred2[:, 0:64], in_=q3, axis=mybir.AxisListType.X, op=OP.add)
    vector.drain()
    vector.tensor_reduce(out=bred[:, 0:64], in_=a3, axis=mybir.AxisListType.X, op=OP.add)
    vector.drain()
    w = 32
    while w >= 1:
        vector.tensor_tensor(out=bred[:, 0:w], in0=bred[:, 0:w], in1=bred[:, w:2 * w], op=OP.add)
        vector.tensor_tensor(out=bred2[:, 0:w], in0=bred2[:, 0:w],
                             in1=bred2[:, w:2 * w], op=OP.add)
        w //= 2
    vector.tensor_scalar(out=bnag[:, 0:1], in0=bred[:, 0:1], scalar1=1.0 / 1024.0, scalar2=None, op0=OP.mult)
    vector.tensor_scalar(out=bnag[:, 1:2], in0=bred2[:, 0:1], scalar1=1.0 / 1024.0, scalar2=None, op0=OP.mult)
    vector.tensor_tensor(out=bns2[:, 0:1], in0=bnag[:, 0:1], in1=bnag[:, 0:1], op=OP.mult)
    vector.drain()
    return vector.tensor_tensor(out=bnag[:, 1:2], in0=bnag[:, 1:2], in1=bns2[:, 0:1], op=OP.subtract)


def build_core_program(stage="ALL", debug=()):
    dbg_flags = debug
    nc = bass.Bass("TRN2", target_bir_lowering=False, detect_race_conditions=False)

    x_tok = nc.dram_tensor("x_tok", [C, D], F32, kind="ExternalInput")
    x_feat = nc.dram_tensor("x_feat", [D, C], F32, kind="ExternalInput")
    wqT_d = nc.dram_tensor("wqT_hbm", [D, D], F32, kind="ExternalInput")
    wkT_d = nc.dram_tensor("wkT_hbm", [D, D], F32, kind="ExternalInput")
    wvT_d = nc.dram_tensor("wvT_hbm", [D, D], F32, kind="ExternalInput")
    bq_d = nc.dram_tensor("bq", [1, D], F32, kind="ExternalInput")
    bk_d = nc.dram_tensor("bk", [1, D], F32, kind="ExternalInput")
    bvc_d = nc.dram_tensor("bvc_hbm", [128, DK], F32, kind="ExternalInput")
    gamma_d = nc.dram_tensor("gamma", [1, 1], F32, kind="ExternalInput")
    wg_d = nc.dram_tensor("wg", [I, D], F32, kind="ExternalInput")
    wu_d = nc.dram_tensor("wu", [I, D], F32, kind="ExternalInput")
    wd_d = nc.dram_tensor("wd", [D, I], F32, kind="ExternalInput")
    idb_d = nc.dram_tensor("idb_hbm", [128, 128], BF16, kind="ExternalInput")
    idf_d = nc.dram_tensor("idf_hbm", [1, 1], F32, kind="ExternalInput")
    onec_d = nc.dram_tensor("onec_hbm", [128, 1], BF16, kind="ExternalInput")

    y_d = nc.dram_tensor("y", [C, D], F32, kind="ExternalOutput")

    wg_scr = nc.dram_tensor("wg_scr", [IK, 128, DK * 128], BF16, kind="Internal")
    row_scr = nc.dram_tensor("row_scr", [2, D], F32, kind="Internal")
    wu_scr = nc.dram_tensor("wu_scr", [IK, 128, DK * 128], BF16, kind="Internal")
    wd_scr = nc.dram_tensor("wd_scr", [IK, 128, D], BF16, kind="Internal")

    dbg = {}
    if "A" in debug:
        dbg["qt"] = nc.dram_tensor("dbg_qt", [C, D], F32, kind="ExternalOutput")
        dbg["kt"] = nc.dram_tensor("dbg_kt", [C, D], F32, kind="ExternalOutput")
        dbg["g"] = nc.dram_tensor("dbg_g", [D, D], F32, kind="ExternalOutput")
        dbg["rows"] = nc.dram_tensor("dbg_rows", [8, D], F32, kind="ExternalOutput")
    if "W" in debug:
        dbg["wgq"] = nc.dram_tensor("dbg_wgq", [IK * DK * 128, 128], F32, kind="ExternalOutput")
        dbg["wdq"] = nc.dram_tensor("dbg_wdq", [I, D], F32, kind="ExternalOutput")
    if "C" in debug:
        dbg["bnsqd"] = nc.dram_tensor("dbg_bnsqd", [128, D], F32, kind="ExternalOutput")
        dbg["bred2d"] = nc.dram_tensor("dbg_bred2d", [128, 128], F32, kind="ExternalOutput")
        dbg["scs"] = nc.dram_tensor("dbg_scs", [128, 4], F32, kind="ExternalOutput")
        dbg["s1a"] = nc.dram_tensor("dbg_s1a", [128, D], F32, kind="ExternalOutput")
        dbg["bnagd"] = nc.dram_tensor("dbg_bnagd", [128, 2], F32, kind="ExternalOutput")
        dbg["vfd"] = nc.dram_tensor("dbg_vfd", [128, DK * TILE], F32, kind="ExternalOutput")
        dbg["xrd"] = nc.dram_tensor("dbg_xrd", [128, D], F32, kind="ExternalOutput")
    if "B" in debug:
        dbg["y1"] = nc.dram_tensor("dbg_y1", [TILE, D], F32, kind="ExternalOutput")
        dbg["xqt"] = nc.dram_tensor("dbg_xqt", [128, DK * TILE], F32, kind="ExternalOutput")
        dbg["hqt"] = nc.dram_tensor("dbg_hqt", [128, IK * TILE], F32, kind="ExternalOutput")
        dbg["zf"] = nc.dram_tensor("dbg_zf", [128, D], F32, kind="ExternalOutput")

    es = ExitStack()
    sb = lambda name, shape, dt: es.enter_context(nc.sbuf_tensor(name, shape, dt))
    sem = lambda name: es.enter_context(nc.semaphore(name))

    idb = sb("idb", [128, 128], BF16)
    idf = sb("idf", [1, 1], F32)
    onec = sb("onec", [128, 1], BF16)
    gam = sb("gam", [1, 1], F32)
    bvc = sb("bvc", [128, DK], F32)
    G = sb("G", [128, DK, D], BF16)

    s_init = sem("s_init")
    with nc.Block() as block:
        @block.sync
        def _(sync):
            for dst, src in ((idb, idb_d), (idf, idf_d), (onec, onec_d),
                             (gam, gamma_d), (bvc, bvc_d)):
                sync.dma_start(out=dst[:], in_=src.ap()).then_inc(s_init, 16)

    esA = ExitStack()
    sbA = lambda name, shape, dt: esA.enter_context(nc.sbuf_tensor(name, shape, dt))

    Q_t = sbA("Q_t", [128, NCC, D], BF16)
    K_t = sbA("K_t", [128, NCC, D], BF16)
    esA1 = ExitStack()
    sbA1 = lambda name, shape, dt: esA1.enter_context(nc.sbuf_tensor(name, shape, dt))
    wqT = sbA1("wqT", [128, DK, D], BF16)
    wkT = sbA1("wkT", [128, DK, D], BF16)
    bq_bc = sbA1("bq_bc", [128, D], F32)
    bk_bc = sbA1("bk_bc", [128, D], F32)
    xTc = [sbA1(f"xTc{i}", [128, DK, 128], BF16) for i in range(2)]

    s_aw = sem("s_aw")
    s_awg = sem("s_awg")
    s_ax = [sem("s_ax0"), sem("s_ax1")]
    s_aq = sem("s_aq")
    s_ae = sem("s_ae")

    with nc.Block() as block, ExitStack() as esP:
        qk_ps = [esP.enter_context(nc.psum_tensor(f"qkps{i}", [128, 512], F32)) for i in range(8)]

        @block.gpsimd
        def _(gpsimd):
            gpsimd.wait_ge(s_init, 80)
            gpsimd.dma_start(out=wqT[:], in_=wqT_d.ap()).then_inc(s_awg, 16)
            gpsimd.dma_start(out=wkT[:], in_=wkT_d.ap()).then_inc(s_awg, 16)
            for cc in range(NCC):
                if cc >= 2:
                    gpsimd.wait_ge(s_aq, 4 * (cc - 1))
                src = bass.AP(x_feat, cc * 128, [[C, 128], [128 * C, DK], [1, 128]])
                gpsimd.dma_start(out=xTc[cc % 2][:], in_=src).then_inc(s_ax[cc % 2], 16)

        @block.sync
        def _(sync):
            sync.dma_start(out=bq_bc[:], in_=bass.AP(bq_d, 0, [[0, 128], [1, D]])).then_inc(s_aw, 16)
            sync.dma_start(out=bk_bc[:], in_=bass.AP(bk_d, 0, [[0, 128], [1, D]])).then_inc(s_aw, 16)

        @block.tensor
        def _(tensor):
            tensor.wait_ge(s_aw, 32)
            tensor.wait_ge(s_awg, 32)
            for cc in range(NCC):
                tensor.wait_ge(s_ax[cc % 2], 16 * (cc // 2 + 1))
                if cc >= 2:
                    tensor.wait_ge(s_ae, cc - 1)
                bs = 4 * (cc % 2)
                xc = xTc[cc % 2]
                for dk in range(DK):
                    lhs = xc[:, dk, :]
                    for j, (wT, nh) in enumerate(((wqT, 0), (wqT, 1), (wkT, 0), (wkT, 1))):
                        mm = tensor.matmul(qk_ps[bs + j][:], lhs,
                                           wT[:, dk, nh * 512:(nh + 1) * 512],
                                           start=(dk == 0), stop=(dk == DK - 1))
                        if dk == DK - 1:
                            mm.then_inc(s_aq, 1)

        @block.vector
        def _(vector):
            for cc in range(NCC):
                vector.wait_ge(s_aq, 4 * (cc + 1))
                bs = 4 * (cc % 2)
                for j, (dst, bias, nh) in enumerate(((Q_t, bq_bc, 0), (Q_t, bq_bc, 1),
                                                     (K_t, bk_bc, 0), (K_t, bk_bc, 1))):
                    vector.tensor_tensor(
                        out=dst[:, cc, nh * 512:(nh + 1) * 512],
                        in0=qk_ps[bs + j][:], in1=bias[:, nh * 512:(nh + 1) * 512], op=OP.add)
                    if j == 3:
                        vector.drain().then_inc(s_ae, 1)

    esA1.close()
    sqb = [sbA(f"sqb{i}", [128, D], BF16) for i in range(4)]
    gt_bc = sbA("gt_bc", [128, D], F32)
    gq_bc = sbA("gq_bc", [128, D], F32)
    qsq_r = sbA("qsq_r", [1, D], F32)
    qcs_r = sbA("qcs_r", [1, D], F32)
    ksq_r = sbA("ksq_r", [1, D], F32)
    p_r = sbA("p_r", [1, D], F32)
    qn_r = sbA("qn_r", [1, D], F32)
    kn_r = sbA("kn_r", [1, D], F32)
    rqn_r = sbA("rqn_r", [1, D], F32)
    t_r = sbA("t_r", [1, D], F32)
    tmp_r = sbA("tmp_r", [1, D], F32)
    gt_r = sbA("gt_r", [1, D], F32)
    rkn_c = sbA("rkn_c", [128, DK], F32)
    s_sq = sem("s_sq")
    s_st = sem("s_st")
    with nc.Block() as block, ExitStack() as esP:
        st_ps = [esP.enter_context(nc.psum_tensor(f"stps{i}", [1, 512], F32)) for i in range(6)]

        @block.scalar
        def _(scalar):
            for cc in range(NCC):
                if cc >= 2:
                    scalar.wait_ge(s_st, cc - 1)
                scalar.activation(sqb[2 * (cc % 2)][:], Q_t[:, cc, :], AF.Square)
                scalar.activation(sqb[2 * (cc % 2) + 1][:], K_t[:, cc, :], AF.Square).then_inc(s_sq, 1)

        @block.tensor
        def _(tensor):
            for cc in range(NCC):
                tensor.wait_ge(s_sq, cc + 1)
                qs, ks = sqb[2 * (cc % 2)], sqb[2 * (cc % 2) + 1]
                st, sp = (cc == 0), (cc == NCC - 1)
                last = None
                for nh in range(2):
                    sl = slice(nh * 512, (nh + 1) * 512)
                    tensor.matmul(st_ps[nh][:], onec[:], qs[:, sl], start=st, stop=sp)
                    tensor.matmul(st_ps[2 + nh][:], onec[:], Q_t[:, cc, sl], start=st, stop=sp)
                    last = tensor.matmul(st_ps[4 + nh][:], onec[:], ks[:, sl], start=st, stop=sp)
                last.then_inc(s_st, 1)

        @block.vector
        def _(vector):
            vector.wait_ge(s_st, NCC)
            for nh in range(2):
                sl = slice(nh * 512, (nh + 1) * 512)
                vector.tensor_copy(qsq_r[:, sl], st_ps[nh][:])
                vector.tensor_copy(qcs_r[:, sl], st_ps[2 + nh][:])
                vector.tensor_copy(ksq_r[:, sl], st_ps[4 + nh][:])

    s_r = sem("s_r")
    s_rc = sem("s_rc")
    with nc.Block() as block, ExitStack() as esP:
        kc_ps = esP.enter_context(nc.psum_tensor("kcps", [128, DK], F32))

        @block.scalar
        def _(scalar):
            scalar.activation(tmp_r[:], qsq_r[:], AF.Ln).then_inc(s_rc, 1)
            scalar.wait_ge(s_rc, 1)
            scalar.activation(qn_r[:], tmp_r[:], AF.Exp, scale=0.5).then_inc(s_rc, 1)
            scalar.wait_ge(s_rc, 2)
            scalar.activation(tmp_r[:], ksq_r[:], AF.Ln).then_inc(s_rc, 1)
            scalar.wait_ge(s_rc, 3)
            scalar.activation(kn_r[:], tmp_r[:], AF.Exp, scale=0.5).then_inc(s_r, 1)

        @block.vector
        def _(vector):
            vector.wait_ge(s_r, 1)
            vector.reciprocal(rqn_r[:], qn_r[:])
            vector.drain()
            vector.reciprocal(tmp_r[:], kn_r[:])
            vector.drain().then_inc(s_r, 1)

        @block.tensor
        def _(tensor):
            tensor.wait_ge(s_r, 2)
            last = None
            for k in range(DK):
                last = tensor.transpose(kc_ps[:, k:k + 1], tmp_r[:, k * 128:(k + 1) * 128], idf[:])
            last.then_inc(s_r, 1)

        @block.vector
        def _(vector):
            vector.wait_ge(s_r, 3)
            vector.tensor_copy(rkn_c[:], kc_ps[:])

    s_g = sem("s_g")
    s_ge = sem("s_ge")
    s_p = sem("s_p")
    with nc.Block() as block, ExitStack() as esP:
        g_ps = [esP.enter_context(nc.psum_tensor(f"gps{i}", [128, 512], F32)) for i in range(8)]

        @block.tensor
        def _(tensor):
            for mh in range(2):
                if mh == 1:
                    tensor.wait_ge(s_ge, 8)
                for cc in range(NCC):
                    for mk in range(4):
                        m0 = (mh * 4 + mk) * 128
                        lhs = K_t[:, cc, m0:m0 + 128]
                        for nh in range(2):
                            mm = tensor.matmul(g_ps[2 * mk + nh][:], lhs,
                                               Q_t[:, cc, nh * 512:(nh + 1) * 512],
                                               start=(cc == 0), stop=(cc == NCC - 1))
                            if cc == NCC - 1:
                                mm.then_inc(s_g, 1)
        @block.vector
        def _(vector):
            for mh in range(2):
                vector.wait_ge(s_g, 8 * (mh + 1))
                for mk in range(4):
                    gmk = mh * 4 + mk
                    for nh in range(2):
                        vector.tensor_scalar(
                            out=G[:, gmk, nh * 512:(nh + 1) * 512],
                            in0=g_ps[2 * mk + nh][:],
                            scalar1=rkn_c[:, gmk:gmk + 1], scalar2=None,
                            op0=OP.mult)
                vector.drain().then_inc(s_ge, 8)

    with nc.Block() as block, ExitStack() as esP:
        p_ps = [esP.enter_context(nc.psum_tensor(f"pps{i}", [1, 512], F32)) for i in range(2)]

        @block.tensor
        def _(tensor):
            last = None
            for gmk in range(DK):
                for nh in range(2):
                    last = tensor.matmul(p_ps[nh][:], onec[:], G[:, gmk, nh * 512:(nh + 1) * 512],
                                         start=(gmk == 0), stop=(gmk == DK - 1))
            last.then_inc(s_p, 1)

        @block.vector
        def _(vector):
            vector.wait_ge(s_p, 1)
            for nh in range(2):
                vector.tensor_copy(p_r[:, nh * 512:(nh + 1) * 512], p_ps[nh][:])

    s_t = sem("s_t")
    with nc.Block() as block:
        @block.vector
        def _(vector):
            VF(vector, vector.tensor_scalar(out=tmp_r[:], in0=qcs_r[:], scalar1=EPS_ATTN, scalar2=None, op0=OP.mult))
            VF(vector, vector.tensor_tensor(out=tmp_r[:], in0=tmp_r[:], in1=p_r[:], op=OP.add))
            VF(vector, vector.tensor_tensor(out=tmp_r[:], in0=tmp_r[:], in1=rqn_r[:], op=OP.mult))
            VF(vector, vector.tensor_scalar(out=tmp_r[:], in0=tmp_r[:], scalar1=float(D), scalar2=None, op0=OP.add))
            vector.reciprocal(t_r[:], tmp_r[:])
            vector.drain()
            vector.tensor_tensor(out=tmp_r[:], in0=t_r[:], in1=rqn_r[:], op=OP.mult)
            VF(vector, vector.tensor_scalar(out=tmp_r[:], in0=tmp_r[:], scalar1=gam[0:1, 0:1], scalar2=None, op0=OP.mult))
            vector.tensor_scalar(out=gt_r[:], in0=t_r[:], scalar1=gam[0:1, 0:1],
                                 scalar2=None, op0=OP.mult)
            vector.drain().then_inc(s_t, 1)

        @block.sync
        def _(sync):
            sync.wait_ge(s_t, 1)
            sync.dma_start(out=bass.AP(row_scr, 0, [[D, 1], [1, D]]), in_=tmp_r[:]).then_inc(s_t, 16)
            sync.dma_start(out=bass.AP(row_scr, D, [[D, 1], [1, D]]), in_=gt_r[:]).then_inc(s_t, 16)
            sync.wait_ge(s_t, 33)
            sync.dma_start(out=gq_bc[:], in_=bass.AP(row_scr, 0, [[0, 128], [1, D]])).then_inc(s_t, 16)
            sync.dma_start(out=gt_bc[:], in_=bass.AP(row_scr, D, [[0, 128], [1, D]])).then_inc(s_t, 16)

        @block.vector
        def _(vector):
            vector.wait_ge(s_t, 65)
            for mk in range(DK):
                vector.tensor_tensor(out=G[:, mk, :], in0=G[:, mk, :], in1=gq_bc[:], op=OP.mult)
                vector.tensor_tensor(out=G[:, mk, :], in0=G[:, mk, :], in1=gt_bc[:], op=OP.add)

    if "A" in debug:
        s_dA = sem("s_dA")
        s_dA2 = sem("s_dA2")
        with nc.Block() as block:
            @block.gpsimd
            def _(gpsimd):
                n = 0
                for cc in range(NCC):
                    gpsimd.dma_start(out=bass.AP(dbg["qt"], cc * 128 * D, [[D, 128], [1, D]]),
                                     in_=Q_t[:, cc, :]).then_inc(s_dA, 16)
                    gpsimd.dma_start(out=bass.AP(dbg["kt"], cc * 128 * D, [[D, 128], [1, D]]),
                                     in_=K_t[:, cc, :]).then_inc(s_dA, 16)
                    n += 2
                for mk in range(DK):
                    gpsimd.dma_start(out=bass.AP(dbg["g"], mk * 128 * D, [[D, 128], [1, D]]),
                                     in_=G[:, mk, :]).then_inc(s_dA, 16)
                    n += 1
                gpsimd.wait_ge(s_dA, 16 * n)

            @block.sync
            def _(sync):
                for i, r in enumerate((qsq_r, qcs_r, ksq_r, p_r, qn_r, kn_r, t_r, tmp_r)):
                    sync.dma_start(out=bass.AP(dbg["rows"], i * D, [[D, 1], [1, D]]),
                                   in_=r[:]).then_inc(s_dA2, 16)
                sync.wait_ge(s_dA2, 16 * 8)

    esA.close()
    if stage == "A":
        es.close()
        return nc, dbg

    esW = ExitStack()
    sbW = lambda name, shape, dt: esW.enter_context(nc.sbuf_tensor(name, shape, dt))

    wbuf = [sbW(f"wbuf{i}", [128, I], F32) for i in range(2)]
    wabs = sbW("wabs", [128, I], F32)
    wsgb = sbW("wsgb", [128, I], BF16)
    wsc = sbW("wsc", [128, I // 16], F32)
    wrs4 = sbW("wrs4", [128, I // 16], F32)
    wrs2 = sbW("wrs2", [128, I // 16], F32)
    wsh = sbW("wsh", [128, I // 16], F32)
    wo1 = sbW("wo1", [128, I], BF16)
    wo2 = sbW("wo2", [128, I], BF16)
    wqv = [sbW(f"wqv{i}", [128, I], BF16) for i in range(2)]
    wtr = [sbW(f"wtr{i}", [128, IK * 128], BF16) for i in range(2)]

    s_wl = [sem("s_wl0"), sem("s_wl1")]
    s_wa = sem("s_wa")
    s_wv = sem("s_wv")
    s_wg2 = sem("s_wg2")
    s_wq = sem("s_wq")
    s_wt = sem("s_wt")
    s_we = sem("s_we")
    s_ws = [sem("s_ws0"), sem("s_ws1")]

    jobs = [("wg", i) for i in range(IK)] + [("wu", i) for i in range(IK)] + [("wd", i) for i in range(DK)]
    NJ = len(jobs)
    jw = lambda j: D if j[0] in ("wg", "wu") else I

    def stores_through(jn):
        tot = 0
        for j in jobs[:jn]:
            tot += 1 if j[0] in ("wg", "wu") else 0
        return tot

    with nc.Block() as block, ExitStack() as esP:
        wt_ps = [[esP.enter_context(nc.psum_tensor(f"wtps{i}_{j}", [128, 1024], BF16)) for j in range(3)]
                 for i in range(2)]

        def emit_wstore(sync, pj):
            kind, idx = jobs[pj]
            sync.wait_ge(s_we, pj + 1)
            if kind in ("wg", "wu"):
                scr = wg_scr if kind == "wg" else wu_scr
                dst = bass.AP(scr, idx * 128 * DK * 128, [[DK * 128, 128], [1, DK * 128]])
                sync.dma_start(out=dst, in_=wtr[pj % 2][:, 0:DK * 128]).then_inc(s_ws[pj % 2], 16)
                return 1
            for ik in range(IK):
                dst = bass.AP(wd_scr, ik * 128 * D + idx * 128, [[D, 128], [1, 128]])
                sync.dma_start(out=dst, in_=wtr[pj % 2][:, ik * 128:(ik + 1) * 128]).then_inc(s_ws[pj % 2], 16)
            return IK

        @block.sync
        def _(sync):
            tot = 0
            for jn, job in enumerate(jobs):
                kind, idx = job
                W = jw(job)
                if jn >= 2:
                    sync.wait_ge(s_wa, 2 * (jn - 1))
                src = {"wg": wg_d, "wu": wu_d, "wd": wd_d}[kind]
                sync.dma_start(out=wbuf[jn % 2][:, 0:W],
                               in_=bass.AP(src, idx * 128 * W, [[W, 128], [1, W]])).then_inc(s_wl[jn % 2], 16)
                if jn >= 1:
                    tot += emit_wstore(sync, jn - 1)
            tot += emit_wstore(sync, NJ - 1)
            tot0 = sum((1 if jobs[j][0] in ("wg", "wu") else IK) for j in range(0, NJ, 2))
            tot1 = sum((1 if jobs[j][0] in ("wg", "wu") else IK) for j in range(1, NJ, 2))
            sync.wait_ge(s_ws[0], 16 * tot0)
            sync.wait_ge(s_ws[1], 16 * tot1)
            sync.wait_ge(s_we, NJ)

        @block.scalar
        def _(scalar):
            for jn, job in enumerate(jobs):
                W = jw(job)
                scalar.wait_ge(s_wl[jn % 2], 16 * (jn // 2 + 1))
                if jn >= 1:
                    scalar.wait_ge(s_wq, jn)
                scalar.activation(wabs[:, 0:W], wbuf[jn % 2][:, 0:W], AF.Abs)
                scalar.activation(wsgb[:, 0:W], wbuf[jn % 2][:, 0:W], AF.Sign).then_inc(s_wa, 2)

        def n_stores_par(pj):
            tot = 0
            for j in range(pj % 2, pj - 1, 2):
                tot += 1 if jobs[j][0] in ("wg", "wu") else IK
            return tot

        def emit_wevac(vector, pj):
            W = jw(jobs[pj])
            vector.wait_ge(s_wt, pj + 1)
            if pj >= 2:
                vector.wait_ge(s_ws[pj % 2], 16 * n_stores_par(pj))
            nblk = W // 128
            last = None
            for p in range((nblk + 7) // 8):
                w_p = min(1024, (nblk - 8 * p) * 128)
                last = vector.tensor_copy(wtr[pj % 2][:, p * 1024:p * 1024 + w_p],
                                          wt_ps[pj % 2][p][:, 0:w_p])
            vector.drain().then_inc(s_we, 1)

        @block.vector
        def _(vector):
            for jn, job in enumerate(jobs):
                W = jw(job)
                vector.wait_ge(s_wa, 2 * (jn + 1))
                emit_quant(vector, wabs, W, wsc, wrs4, wrs2, wsh, wo1, wo2)
                nb = W // 16
                shb = bass.AP(wsh, 0, [[I // 16, 128], [1, nb], [0, 16]])
                if jn >= 2:
                    vector.wait_ge(s_wt, jn - 1)
                vector.tensor_tensor(out=wo2[:, 0:W], in0=wo1[:, 0:W],
                                     in1=wsgb[:, 0:W], op=OP.mult)
                vector.tensor_tensor(out=wqv[jn % 2][:, 0:W], in0=wo2[:, 0:W],
                                     in1=shb, op=OP.mult)
                vector.drain().then_inc(s_wq, 1)
                if jn >= 1:
                    emit_wevac(vector, jn - 1)
            emit_wevac(vector, NJ - 1)

        @block.tensor
        def _(tensor):
            for jn, job in enumerate(jobs):
                W = jw(job)
                tensor.wait_ge(s_wq, jn + 1)
                if jn >= 2:
                    tensor.wait_ge(s_we, jn - 1)
                nblk = W // 128
                last = None
                for b in range(nblk):
                    last = tensor.transpose(wt_ps[jn % 2][b // 8][:, (b % 8) * 128:(b % 8 + 1) * 128],
                                            wqv[jn % 2][:, b * 128:(b + 1) * 128], idb[:])
                last.then_inc(s_wt, 1)

    if "W" in debug:
        s_dW = sem("s_dW")
        with nc.Block() as block:
            @block.gpsimd
            def _(gpsimd):
                n = 0
                for ik in range(IK):
                    for hk in range(DK):
                        gpsimd.dma_start(
                            out=wbuf[0][:, 0:128],
                            in_=bass.AP(wg_scr, ik * 128 * DK * 128 + hk * 128,
                                        [[DK * 128, 128], [1, 128]])
                        ).then_inc(s_dW, 16)
                        n += 1
                        gpsimd.wait_ge(s_dW, 32 * n - 16)
                        gpsimd.dma_start(
                            out=bass.AP(dbg["wgq"], (ik * DK + hk) * 128 * 128, [[128, 128], [1, 128]]),
                            in_=wbuf[0][:, 0:128]).then_inc(s_dW, 16)
                        gpsimd.wait_ge(s_dW, 32 * n)
                for ik in range(IK):
                    gpsimd.dma_start(out=wbuf[0][:, 0:D],
                                     in_=bass.AP(wd_scr, ik * 128 * D, [[D, 128], [1, D]])).then_inc(s_dW, 16)
                    n += 1
                    gpsimd.wait_ge(s_dW, 32 * n - 16)
                    gpsimd.dma_start(out=bass.AP(dbg["wdq"], ik * 128 * D, [[D, 128], [1, D]]),
                                     in_=wbuf[0][:, 0:D]).then_inc(s_dW, 16)
                    gpsimd.wait_ge(s_dW, 32 * n)

    esW.close()
    if stage == "W":
        es.close()
        return nc, dbg

    esB = ExitStack()
    sbB = lambda name, shape, dt: esB.enter_context(nc.sbuf_tensor(name, shape, dt))

    wvT = sbB("wvT", [128, DK, D], BF16)
    xTt = [sbB(f"xTt{i}", [128, DK, TILE], BF16) for i in range(2)]
    xres = [sbB(f"xres{i}", [128, D], F32) for i in range(2)]
    Vf = sbB("Vf", [128, DK, TILE], BF16)
    y1t = sbB("y1t", [128, CSUB, D], F32)
    s1f = sbB("s1f", [128, D], F32)
    yab = sbB("yab", [128, D], F32)
    ysg = sbB("ysg", [128, D], BF16)
    yo1 = sbB("yo1", [128, D], BF16)
    yo2 = sbB("yo2", [128, D], BF16)
    yxq = sbB("yxq", [128, D], BF16)
    ysc = sbB("ysc", [128, D // 16], F32)
    yrs4 = sbB("yrs4", [128, D // 16], F32)
    yrs2 = sbB("yrs2", [128, D // 16], F32)
    ysh = sbB("ysh", [128, D // 16], F32)
    xqT = sbB("xqT", [128, DK, TILE], BF16)
    wgs = [sbB(f"wgs{i}", [128, 4 * DK * 128], BF16) for i in range(2)]
    wus = [sbB(f"wus{i}", [128, 4 * DK * 128], BF16) for i in range(2)]
    hst = sbB("hst", [128, 512], F32)
    hab = sbB("hab", [128, 512], F32)
    hsgb = sbB("hsgb", [128, 512], BF16)
    ho1 = sbB("ho1", [128, 512], BF16)
    ho2 = sbB("ho2", [128, 512], BF16)
    hqv = sbB("hqv", [128, 512], BF16)
    hsc = sbB("hsc", [128, 32], F32)
    hrs4 = sbB("hrs4", [128, 32], F32)
    hrs2 = sbB("hrs2", [128, 32], F32)
    hsh = sbB("hsh", [128, 32], F32)
    hqT = sbB("hqT", [128, IK, TILE], BF16)
    wds = [sbB(f"wds{i}", [128, D], BF16) for i in range(4)]
    sil = sbB("sil", [128, 512], F32)
    y2b = [sbB(f"y2b{i}", [128, D], F32) for i in range(2)]
    bnsq = sbB("bnsq", [128, D], F32)
    bred = sbB("bred", [128, 128], F32)
    bred2 = sbB("bred2", [128, 128], F32)
    if "C" in debug:
        bnsq_t = sbB("bnsq_t", [128, D], F32)
        bred2_t = sbB("bred2_t", [128, 128], F32)
        scs_t = sbB("scs_t", [128, 4], F32)
        s1a_t = sbB("s1a_t", [128, D], F32)
        bnag_t = sbB("bnag_t", [128, 2], F32)
        xr_t = sbB("xr_t", [128, D], F32)
    bnag = sbB("bnag", [128, 2], F32)
    bns2 = sbB("bns2", [128, 1], F32)
    bnr1 = sbB("bnr1", [128, 1], F32)
    bnr2 = sbB("bnr2", [128, 1], F32)
    epsc = sbB("epsc", [128, 1], F32)

    s_bw0 = sem("s_bw0")
    s_bx = [sem("s_bx0"), sem("s_bx1")]
    s_br = [sem("s_br0"), sem("s_br1")]
    s_bv = sem("s_bv")
    s_bve = sem("s_bve")
    s_ba = sem("s_ba")
    s_be = sem("s_be")
    s_bse = sem("s_bse")
    s_bcm = sem("s_bcm")
    s_bgp = sem("s_bgp")
    s_bq = sem("s_bq")
    s2_wg = [sem("s2_wg0"), sem("s2_wg1")]
    s2_g = sem("s2_g")
    s2_sil = sem("s2_sil")
    s2_h = sem("s2_h")
    s2_hse = sem("s2_hse")
    s2_hcm = sem("s2_hcm")
    s2_hgp = sem("s2_hgp")
    s2_t = sem("s2_t")
    s2_te = sem("s2_te")
    s2_d = sem("s2_d")
    s2_dk = sem("s2_dk")
    s2_wd = [sem(f"s2_wd{i}") for i in range(4)]
    s2_l2 = sem("s2_l2")
    s2_lse = sem("s2_lse")
    s2_o = sem("s2_o")
    s2_dbg = sem("s2_dbg")
    s2_dbg2 = sem("s2_dbg2")

    with nc.Block() as block:
        @block.gpsimd
        def _(gpsimd):
            gpsimd.dma_start(out=wvT[:], in_=wvT_d.ap()).then_inc(s_bw0, 16)
            src = bass.AP(x_feat, 0, [[C, 128], [128 * C, DK], [1, TILE]])
            gpsimd.dma_start(out=xTt[0][:], in_=src).then_inc(s_bx[0], 16)

        @block.vector
        def _(vector):
            vector.memset(epsc[:], EPS_LN)
            vector.drain()

        @block.sync
        def _(sync):
            sync.wait_ge(s_bw0, 16)
            sync.wait_ge(s_bx[0], 16)

    for t in range(NTILES):
        b_bv, b_bve, b_ba = 8 * t, 8 * t, CSUB * t
        b_be, b_bse, b_bcm, b_bgp, b_bq = 12 * t, 12 * t, 4 * t, 4 * t, 8 * t
        with nc.Block() as block, ExitStack() as esP:
            pp = lambda name, shape, dt: esP.enter_context(nc.psum_tensor(name, shape, dt))
            vps = [pp(f"vps{t}_{i}", [128, TILE], F32) for i in range(2)]
            aps = [pp(f"aps{t}_{i}", [128, D], F32) for i in range(2)]
            tps = [pp(f"tps{t}_{i}", [128, 1024], BF16) for i in range(2)]

            @block.gpsimd
            def _(gpsimd):
                if t + 1 < NTILES:
                    src = bass.AP(x_feat, (t + 1) * TILE, [[C, 128], [128 * C, DK], [1, TILE]])
                    gpsimd.dma_start(out=xTt[(t + 1) % 2][:], in_=src).then_inc(s_bx[(t + 1) % 2], 16)

            @block.sync
            def _(sync):
                for cs in range(CSUB):
                    if cs >= 2:
                        sync.wait_ge(s_be, b_be + 3 * (cs - 2) + 1)
                    r0 = t * TILE + cs * 128
                    sync.dma_start(out=xres[cs % 2][:],
                                   in_=bass.AP(x_tok, r0 * D, [[D, 128], [1, D]])).then_inc(s_br[cs % 2], 16)
                i0, W = STRIPS[0]
                nik = W // 128
                sync.dma_start(out=wgs[0][:, 0:nik * DK * 128],
                               in_=bass.AP(wg_scr, 0, [[DK * 128, 128], [128 * DK * 128, nik], [1, DK * 128]])
                               ).then_inc(s2_wg[0], 16)
                sync.dma_start(out=wus[0][:, 0:nik * DK * 128],
                               in_=bass.AP(wu_scr, 0, [[DK * 128, 128], [128 * DK * 128, nik], [1, DK * 128]])
                               ).then_inc(s2_wg[0], 16)

            @block.tensor
            def _(tensor):
                tensor.wait_ge(s_bx[t % 2], 16 * (t // 2 + 1))
                for mk in range(DK):
                    if mk >= 2:
                        tensor.wait_ge(s_bve, b_bve + mk - 1)
                    for dk in range(DK):
                        mm = tensor.matmul(vps[mk % 2][:], wvT[:, dk, mk * 128:(mk + 1) * 128],
                                           xTt[t % 2][:, dk, :], start=(dk == 0), stop=(dk == DK - 1))
                        if dk == DK - 1:
                            mm.then_inc(s_bv, 1)
                tensor.wait_ge(s_bve, b_bve + 8)

                def emit_xq_transpose(j):
                    tensor.wait_ge(s_be, b_be + 3 * j + 3)
                    if j >= 2:
                        tensor.wait_ge(s_bq, b_bq + 2 * (j - 1))
                    last = None
                    for dk in range(DK):
                        last = tensor.transpose(tps[j % 2][:, dk * 128:(dk + 1) * 128],
                                                yxq[:, dk * 128:(dk + 1) * 128], idb[:])
                    last.then_inc(s_bq, 1)

                for cs in range(CSUB):
                    if cs >= 2:
                        tensor.wait_ge(s_be, b_be + 3 * (cs - 1))
                    for mk in range(DK):
                        for nh in range(2):
                            mm = tensor.matmul(aps[cs % 2][:, nh * 512:(nh + 1) * 512],
                                               Vf[:, mk, cs * 128:(cs + 1) * 128],
                                               G[:, mk, nh * 512:(nh + 1) * 512],
                                               start=(mk == 0), stop=(mk == DK - 1))
                            if mk == DK - 1 and nh == 1:
                                mm.then_inc(s_ba, 1)
                    if cs >= 1:
                        emit_xq_transpose(cs - 1)
                emit_xq_transpose(CSUB - 1)

            @block.scalar
            def _(scalar):
                for mk in range(DK):
                    scalar.wait_ge(s_bv, b_bv + mk + 1)
                    scalar.activation(Vf[:, mk, :], vps[mk % 2][:], AF.Identity,
                                      bias=bvc[:, mk:mk + 1]).then_inc(s_bve, 1)
                for cs in range(CSUB):
                    scalar.wait_ge(s_be, b_be + 3 * cs + 1)
                    scalar.activation(bnr1[:], bnag[:, 1:2], AF.Ln, bias=epsc[:]).then_inc(s_bse, 1)
                    scalar.wait_ge(s_bse, b_bse + 3 * cs + 1)
                    scalar.activation(bnr2[:], bnr1[:], AF.Exp, scale=-0.5).then_inc(s_bse, 1)
                    if "C" in dbg_flags and t == NTILES - 1 and cs == 3:
                        scalar.activation(scs_t[:, 0:1], bnag[:, 1:2], AF.Identity)
                        scalar.activation(scs_t[:, 1:2], bnr1[:], AF.Identity)
                        scalar.activation(scs_t[:, 2:3], bnr2[:], AF.Identity)
                        scalar.activation(scs_t[:, 3:4], epsc[:], AF.Identity)
                    scalar.wait_ge(s_be, b_be + 3 * cs + 2)
                    scalar.activation(yab[:], y1t[:, cs, :], AF.Abs)
                    scalar.activation(ysg[:], y1t[:, cs, :], AF.Sign).then_inc(s_bse, 1)

            @block.vector
            def _(vector):
                for cs in range(CSUB):
                    vector.wait_ge(s_ba, b_ba + cs + 1)
                    vector.wait_ge(s_br[cs % 2], 16 * (2 * t + cs // 2 + 1))
                    vector.scalar_tensor_tensor(out=s1f[:], in0=xres[cs % 2][:], scalar=2.0,
                                                in1=aps[cs % 2][:], op0=OP.mult, op1=OP.add)
                    if "C" in dbg_flags and t == NTILES - 1 and cs == 3:
                        vector.tensor_copy(s1a_t[:], s1f[:])
                        vector.tensor_copy(xr_t[:], xres[cs % 2][:])
                    emit_bn(vector, s1f, bnsq, bnag, bns2, bred, bred2)
                    vector.drain().then_inc(s_be, 1)
                    if "C" in dbg_flags and t == NTILES - 1 and cs == 3:
                        vector.tensor_copy(bnag_t[:], bnag[:])
                        vector.tensor_copy(bnsq_t[:], bnsq[:])
                        vector.tensor_copy(bred2_t[:, 0:64], bred2[:, 0:64])
                        vector.tensor_copy(bred2_t[:, 64:128], bred[:, 0:64])
                    vector.wait_ge(s_bse, b_bse + 3 * cs + 2)
                    vector.tensor_scalar(out=y1t[:, cs, :], in0=s1f[:],
                                         scalar1=bnag[0:128, 0:1], scalar2=bnr2[0:128, 0:1],
                                         op0=OP.subtract, op1=OP.mult)
                    vector.drain().then_inc(s_be, 1)
                    vector.wait_ge(s_bse, b_bse + 3 * cs + 3)
                    emit_quant(vector, yab, D, ysc, yrs4, yrs2, ysh, yo1, yo2)
                    shb = bass.AP(ysh, 0, [[D // 16, 128], [1, D // 16], [0, 16]])
                    vector.tensor_tensor(out=yo2[:], in0=yo1[:], in1=ysg[:], op=OP.mult)
                    vector.tensor_tensor(out=yxq[:], in0=yo2[:], in1=shb, op=OP.mult)
                    vector.drain().then_inc(s_be, 1)
                    vector.wait_ge(s_bq, b_bq + 2 * cs + 1)
                    dst = bass.AP(xqT, cs * 128, [[DK * TILE, 128], [TILE, DK], [1, 128]])
                    vector.tensor_copy(dst, tps[cs % 2][:])
                    vector.drain().then_inc(s_bq, 1)

        b2_wg = 32 * (NST * t + 1)
        b_g, b_sil, b_h = 2 * NST * CSUB * t, NST * CSUB * t, 2 * NST * CSUB * t
        b_hse, b_hcm, b_hgp = 2 * NST * CSUB * t, NST * CSUB * t, NST * CSUB * t
        b_t2, b_te = NST * CSUB * t, NST * CSUB * t
        b_d, b_l2, b_lse, b_o = CSUB * t, 2 * CSUB * t, 2 * CSUB * t, 64 * t

        with nc.Block() as block, ExitStack() as esP:
            pp = lambda name, shape, dt: esP.enter_context(nc.psum_tensor(name, shape, dt))
            gps = [pp(f"bgps{t}_{i}", [128, 512], F32) for i in range(4)]
            hps = [pp(f"bhps{t}_{i}", [128, 512], BF16) for i in range(2)]

            @block.sync
            def _(sync):
                for sn in range(1, NST):
                    i0, W = STRIPS[sn]
                    nik = W // 128
                    ik0 = i0 // 128
                    if sn >= 2:
                        sync.wait_ge(s2_g, b_g + 2 * CSUB * (sn - 1))
                    sync.dma_start(out=wgs[sn % 2][:, 0:nik * DK * 128],
                                   in_=bass.AP(wg_scr, ik0 * 128 * DK * 128,
                                               [[DK * 128, 128], [128 * DK * 128, nik], [1, DK * 128]])
                                   ).then_inc(s2_wg[sn % 2], 16)
                    sync.dma_start(out=wus[sn % 2][:, 0:nik * DK * 128],
                                   in_=bass.AP(wu_scr, ik0 * 128 * DK * 128,
                                               [[DK * 128, 128], [128 * DK * 128, nik], [1, DK * 128]])
                                   ).then_inc(s2_wg[sn % 2], 16)

            @block.tensor
            def _(tensor):
                for sn in range(NST):
                    i0, W = STRIPS[sn]
                    nik = W // 128
                    tensor.wait_ge(s2_wg[sn % 2], 32 * (3 * t + sn // 2 + 1))
                    for cs in range(CSUB):
                        gu = sn * CSUB + cs
                        if gu >= 2:
                            tensor.wait_ge(s2_h, b_h + 2 * (gu - 1))
                        for pi, wbufs in enumerate((wgs, wus)):
                            last = None
                            for hk in range(DK):
                                rhs = bass.AP(wbufs[sn % 2], hk * 128,
                                              [[4 * DK * 128, 128], [DK * 128, nik], [1, 128]])
                                last = tensor.matmul(gps[2 * (gu % 2) + pi][:, 0:W],
                                                     xqT[:, hk, cs * 128:(cs + 1) * 128], rhs,
                                                     start=(hk == 0), stop=(hk == DK - 1))
                            last.then_inc(s2_g, 1)
                        tensor.wait_ge(s2_h, b_h + 2 * gu + 2)
                        if gu >= 2:
                            tensor.wait_ge(s2_te, b_te + gu - 1)
                        lastt = None
                        for b_ in range(nik):
                            lastt = tensor.transpose(hps[gu % 2][:, b_ * 128:(b_ + 1) * 128],
                                                     hqv[:, b_ * 128:(b_ + 1) * 128], idb[:])
                        lastt.then_inc(s2_t, 1)

            @block.scalar
            def _(scalar):
                for sn in range(NST):
                    i0, W = STRIPS[sn]
                    for cs in range(CSUB):
                        gu = sn * CSUB + cs
                        scalar.wait_ge(s2_g, b_g + 2 * gu + 1)
                        if gu >= 1:
                            scalar.wait_ge(s2_h, b_h + 2 * gu - 1)
                        scalar.activation(sil[:, 0:W], gps[2 * (gu % 2)][:, 0:W], AF.Silu).then_inc(s2_sil, 1)
                        scalar.wait_ge(s2_h, b_h + 2 * gu + 1)
                        scalar.activation(hab[:, 0:W], hst[:, 0:W], AF.Abs)
                        scalar.activation(hsgb[:, 0:W], hst[:, 0:W], AF.Sign).then_inc(s2_hse, 2)

            @block.vector
            def _(vector):
                for sn in range(NST):
                    i0, W = STRIPS[sn]
                    for cs in range(CSUB):
                        gu = sn * CSUB + cs
                        vector.wait_ge(s2_sil, b_sil + gu + 1)
                        vector.wait_ge(s2_g, b_g + 2 * gu + 2)
                        if gu >= 1:
                            vector.wait_ge(s2_hse, b_hse + 2 * gu)
                        vector.tensor_tensor(out=hst[:, 0:W], in0=sil[:, 0:W],
                                             in1=gps[2 * (gu % 2) + 1][:, 0:W], op=OP.mult)
                        vector.drain().then_inc(s2_h, 1)
                        vector.wait_ge(s2_hse, b_hse + 2 * gu + 2)
                        emit_quant(vector, hab, W, hsc, hrs4, hrs2, hsh, ho1, ho2)
                        vector.tensor_tensor(out=ho2[:, 0:W], in0=ho1[:, 0:W],
                                             in1=hsgb[:, 0:W], op=OP.mult)
                        nb = W // 16
                        shb = bass.AP(hsh, 0, [[32, 128], [1, nb], [0, 16]])
                        vector.tensor_tensor(out=hqv[:, 0:W], in0=ho2[:, 0:W],
                                             in1=shb, op=OP.mult)
                        vector.drain().then_inc(s2_h, 1)
                        vector.wait_ge(s2_t, b_t2 + gu + 1)
                        nik = W // 128
                        ik0 = i0 // 128
                        dst = bass.AP(hqT, ik0 * TILE + cs * 128, [[IK * TILE, 128], [TILE, nik], [1, 128]])
                        vector.tensor_copy(dst, hps[gu % 2][:, 0:W])
                        vector.drain().then_inc(s2_te, 1)


        wd_slot_cnt = [6, 6, 5, 5]
        with nc.Block() as block, ExitStack() as esP:
            pp = lambda name, shape, dt: esP.enter_context(nc.psum_tensor(name, shape, dt))
            dps = [pp(f"bdps{t}_{i}", [128, D], F32) for i in range(4)]

            @block.sync
            def _(sync):
                for ik in range(IK):
                    g = IK * t + ik
                    if g >= 4:
                        sync.wait_ge(s2_dk, g - 3)
                    sync.dma_start(out=wds[ik % 4][:],
                                   in_=bass.AP(wd_scr, ik * 128 * D, [[D, 128], [1, D]])
                                   ).then_inc(s2_wd[ik % 4], 16)
                for cs in range(CSUB):
                    sync.wait_ge(s2_l2, b_l2 + 2 * cs + 2)
                    r0 = t * TILE + cs * 128
                    sync.dma_start(out=bass.AP(y_d, r0 * D, [[D, 128], [1, D]]),
                                   in_=y2b[cs % 2][:]).then_inc(s2_o, 16)
                if t == NTILES - 1:
                    sync.wait_ge(s2_o, 16 * 4 * NTILES)

            @block.tensor
            def _(tensor):
                for ik in range(IK):
                    slot = ik % 4
                    tensor.wait_ge(s2_wd[slot], 16 * (wd_slot_cnt[slot] * t + ik // 4 + 1))
                    last = None
                    for cs in range(CSUB):
                        for nh in range(2):
                            last = tensor.matmul(dps[cs][:, nh * 512:(nh + 1) * 512],
                                                 hqT[:, ik, cs * 128:(cs + 1) * 128],
                                                 wds[slot][:, nh * 512:(nh + 1) * 512],
                                                 start=(ik == 0), stop=(ik == IK - 1))
                    last.then_inc(s2_dk, 1)

            @block.scalar
            def _(scalar):
                for cs in range(CSUB):
                    scalar.wait_ge(s2_l2, b_l2 + 2 * cs + 1)
                    scalar.activation(bnr1[:], bnag[:, 1:2], AF.Ln, bias=epsc[:]).then_inc(s2_lse, 1)
                    scalar.wait_ge(s2_lse, b_lse + 2 * cs + 1)
                    scalar.activation(bnr2[:], bnr1[:], AF.Exp, scale=-0.5).then_inc(s2_lse, 1)

            @block.vector
            def _(vector):
                for cs in range(CSUB):
                    vector.wait_ge(s2_dk, IK * (t + 1))
                    if cs >= 2:
                        vector.wait_ge(s2_o, b_o + 16 * (cs - 1))
                    for nh in range(2):
                        sl = slice(nh * 512, (nh + 1) * 512)
                        vector.tensor_tensor(out=s1f[:, sl], in0=y1t[:, cs, sl], in1=dps[cs][:, sl], op=OP.add)
                    emit_bn(vector, s1f, bnsq, bnag, bns2, bred, bred2)
                    vector.drain().then_inc(s2_l2, 1)
                    vector.wait_ge(s2_lse, b_lse + 2 * cs + 2)
                    vector.tensor_scalar(out=y2b[cs % 2][:], in0=s1f[:],
                                         scalar1=bnag[0:128, 0:1], scalar2=bnr2[0:128, 0:1],
                                         op0=OP.subtract, op1=OP.mult)
                    vector.drain().then_inc(s2_l2, 1)

        if "B" in debug and t == NTILES - 1:
            with nc.Block() as block:
                @block.sync
                def _(sync):
                    sync.wait_ge(s2_o, 16 * 4 * NTILES)
                    for cs in range(CSUB):
                        r0 = cs * 128
                        sync.dma_start(out=bass.AP(dbg["y1"], r0 * D, [[D, 128], [1, D]]),
                                       in_=y1t[:, cs, :]).then_inc(s2_dbg, 16)
                    sync.dma_start(out=bass.AP(dbg["zf"], 0, [[D, 128], [1, D]]),
                                   in_=s1f[:]).then_inc(s2_dbg, 16)
                    sync.wait_ge(s2_dbg, 16 * 5)

                @block.gpsimd
                def _(gpsimd):
                    gpsimd.wait_ge(s2_o, 16 * 4 * NTILES)
                    gpsimd.dma_start(out=bass.AP(dbg["xqt"], 0,
                                                 [[DK * TILE, 128], [1, DK * TILE]]),
                                     in_=xqT[:]).then_inc(s2_dbg2, 16)
                    gpsimd.dma_start(out=bass.AP(dbg["hqt"], 0,
                                                 [[IK * TILE, 128], [1, IK * TILE]]),
                                     in_=hqT[:]).then_inc(s2_dbg2, 16)
                    gpsimd.wait_ge(s2_dbg2, 16 * 2)

    if "C" in debug:
        s2_dbg3 = sem("s2_dbg3")
        with nc.Block() as block:
            @block.gpsimd
            def _(gpsimd):
                gpsimd.wait_ge(s2_o, 16 * 4 * NTILES)
                gpsimd.dma_start(out=dbg["bnsqd"].ap(), in_=bnsq_t[:]).then_inc(s2_dbg3, 16)
                gpsimd.dma_start(out=dbg["bred2d"].ap(), in_=bred2_t[:]).then_inc(s2_dbg3, 16)
                gpsimd.dma_start(out=dbg["scs"].ap(), in_=scs_t[:]).then_inc(s2_dbg3, 16)
                gpsimd.dma_start(out=dbg["s1a"].ap(), in_=s1a_t[:]).then_inc(s2_dbg3, 16)
                gpsimd.dma_start(out=dbg["bnagd"].ap(), in_=bnag_t[:]).then_inc(s2_dbg3, 16)
                gpsimd.dma_start(out=dbg["xrd"].ap(), in_=xr_t[:]).then_inc(s2_dbg3, 16)
                gpsimd.dma_start(out=bass.AP(dbg["vfd"], 0, [[DK * TILE, 128], [1, DK * TILE]]),
                                 in_=Vf[:]).then_inc(s2_dbg3, 16)
                gpsimd.wait_ge(s2_dbg3, 16 * 7)

    esB.close()
    es.close()
    return nc, dbg


_PROG_CACHE = {}


def _build():
    if "nc" not in _PROG_CACHE:
        nc, _ = build_core_program(stage="ALL", debug=())
        mybir.codegen_inst_isa_subclasses(nc)
        _PROG_CACHE["nc"] = nc
    return _PROG_CACHE["nc"]


def _wT_layout(W):
    Dm = W.shape[0]
    return np.ascontiguousarray(W.T.reshape(Dm // 128, 128, Dm).transpose(1, 0, 2).reshape(Dm, Dm))


def kernel(L0_attn_x, L0_attn_Wq, L0_attn_bq, L0_attn_Wk, L0_attn_bk,
           L0_attn_Wv, L0_attn_bv, L0_attn_gamma,
           L0_ffn_w_gate, L0_ffn_w_up, L0_ffn_w_down):
    x = np.ascontiguousarray(np.asarray(L0_attn_x, dtype=np.float32))
    B = x.shape[0]
    assert x.shape == (B, C, D)
    wqT = _wT_layout(np.asarray(L0_attn_Wq, dtype=np.float32))
    wkT = _wT_layout(np.asarray(L0_attn_Wk, dtype=np.float32))
    wvT = _wT_layout(np.asarray(L0_attn_Wv, dtype=np.float32))
    bq = np.ascontiguousarray(np.asarray(L0_attn_bq, dtype=np.float32).reshape(1, D))
    bk = np.ascontiguousarray(np.asarray(L0_attn_bk, dtype=np.float32).reshape(1, D))
    bvc = np.ascontiguousarray(np.asarray(L0_attn_bv, dtype=np.float32).reshape(DK, 128).T)
    gam = np.ascontiguousarray(np.asarray(L0_attn_gamma, dtype=np.float32).reshape(1, 1))
    wg = np.ascontiguousarray(np.asarray(L0_ffn_w_gate, dtype=np.float32))
    wu = np.ascontiguousarray(np.asarray(L0_ffn_w_up, dtype=np.float32))
    wd = np.ascontiguousarray(np.asarray(L0_ffn_w_down, dtype=np.float32))
    idb = np.eye(128, dtype=ml_dtypes.bfloat16)
    idf = np.ones((1, 1), dtype=np.float32)
    onec = np.ones((128, 1), dtype=ml_dtypes.bfloat16)

    nc = _build()
    in_maps = []
    for b in range(B):
        in_maps.append({
            "x_tok": x[b],
            "x_feat": np.ascontiguousarray(x[b].T),
            "wqT_hbm": wqT, "wkT_hbm": wkT, "wvT_hbm": wvT,
            "bq": bq, "bk": bk, "bvc_hbm": bvc, "gamma": gam,
            "wg": wg, "wu": wu, "wd": wd,
            "idb_hbm": idb, "idf_hbm": idf, "onec_hbm": onec,
        })
    res = run_bass_kernel_spmd(nc, in_maps, core_ids=list(range(B)))
    out = np.stack([np.asarray(res.results[b]["y"], dtype=np.float32) for b in range(B)], axis=0)
    return out
